# version 22
# baseline (speedup 1.0000x reference)
import contextlib
import math

import numpy as np
import ml_dtypes

import concourse.bass as bass
import concourse.mybir as mybir
from concourse.bass_utils import run_bass_kernel_spmd

F32 = mybir.dt.float32
BF16 = mybir.dt.bfloat16
AF = mybir.ActivationFunctionType
ALU = mybir.AluOpType
AX = mybir.AxisListType

NCORES = 8
T = 2048
D = 1024
NKT = 8
NCH = 4
CW = 512

NEG = -1.0e6
LN_EPS = 1e-5
SIM_SAFE = False


COMPUTE_ENGS = ("tensor", "vector", "scalar", "gpsimd", "sync")
DMA_POOL = 10


class Res:
    __slots__ = ("name", "w", "rs", "init_deps")

    def __init__(self, name="r"):
        self.name = name
        self.w = None
        self.rs = []
        self.init_deps = ()


class _Op:
    __slots__ = ("eng", "fn", "waits", "inc", "is_dma", "ev", "tag",
                 "dma_slot", "dma_epoch", "always_sem")

    def __init__(self, eng, fn, is_dma, tag):
        self.eng = eng
        self.fn = fn
        self.waits = []
        self.inc = False
        self.is_dma = is_dma
        self.ev = None
        self.tag = tag
        self.dma_slot = 0
        self.dma_epoch = 0
        self.always_sem = False


class Sched:
    def __init__(self):
        self.ops = {e: [] for e in COMPUTE_ENGS}
        self.dseq = {e: 0 for e in COMPUTE_ENGS}
        self.all_ops = []

    def op(self, eng, fn, reads=(), writes=(), sreads=(), tag=""):
        o = _Op(eng, fn, False, tag)
        idx = len(self.ops[eng])
        ev = ("c", eng, idx, o)
        o.ev = ev
        self._edges(o, ev, reads, writes, sreads)
        self.ops[eng].append(o)
        self.all_ops.append(o)
        return o

    def dma(self, eng, fn, reads=(), writes=(), tag=""):
        o = _Op(eng, fn, True, tag)
        idx0 = self.dseq[eng]
        self.dseq[eng] += 1
        o.dma_slot = idx0 % DMA_POOL
        o.dma_epoch = idx0 // DMA_POOL + 1
        ev = ("d", eng, idx0, o)
        o.ev = ev
        self._edges(o, ev, reads, writes, ())
        self.ops[eng].append(o)
        self.all_ops.append(o)
        return o

    @staticmethod
    def _need_wait(my_eng, ev, forced):
        kind, eng, idx, op = ev
        if (kind == "c" and eng == my_eng and not forced and not SIM_SAFE
                and not op.always_sem):
            return False
        return True

    def _edges(self, o, ev, reads, writes, sreads):
        for r in reads:
            if r.w is not None and self._need_wait(o.eng, r.w, False):
                o.waits.append(r.w)
            r.rs.append(ev)
        for r in sreads:
            if r.w is not None and self._need_wait(o.eng, r.w, True):
                o.waits.append(r.w)
            r.rs.append(ev)
        for r in writes:
            if r.w is not None and self._need_wait(o.eng, r.w, False):
                o.waits.append(r.w)
            for rev in r.rs:
                if rev is not ev and self._need_wait(o.eng, rev, False):
                    o.waits.append(rev)
            for rev in r.init_deps:
                if rev is not ev and self._need_wait(o.eng, rev, False):
                    o.waits.append(rev)
            r.init_deps = ()
            r.w = ev
            r.rs = []

    def emit(self, nc, block, sems, final_waits=True):
        for o in self.all_ops:
            for ev in o.waits:
                kind, eng, idx, op = ev
                if kind == "c":
                    op.inc = True
        inc_at = {}
        for e in COMPUTE_ENGS:
            c = 0
            ci = 0
            for o in self.ops[e]:
                if not o.is_dma:
                    if o.inc:
                        c += 1
                    inc_at[(e, ci)] = c
                    ci += 1
        cidx_of = {}
        for e in COMPUTE_ENGS:
            ci = 0
            for i, o in enumerate(self.ops[e]):
                if not o.is_dma:
                    cidx_of[id(o)] = ci
                    ci += 1

        def run_stream(eng_obj, e):
            watermark = {}

            def do_wait(key, sem, val):
                if watermark.get(key, -1) < val:
                    eng_obj.wait_ge(sem, val)
                    watermark[key] = val

            for o in self.ops[e]:
                for ev in o.waits:
                    kind, peng, pidx, pop = ev
                    if kind == "c":
                        sem = sems[("c", peng)]
                        val = inc_at[(peng, cidx_of[id(pop)])]
                        do_wait(("c", peng), sem, val)
                    else:
                        sem = sems[("q", peng, pop.dma_slot)]
                        do_wait(("q", peng, pop.dma_slot), sem,
                                16 * pop.dma_epoch)
                if o.is_dma:
                    if o.dma_epoch > 1:
                        do_wait(("q", e, o.dma_slot),
                                sems[("q", e, o.dma_slot)],
                                16 * (o.dma_epoch - 1))
                    ins = o.fn(eng_obj)
                    ins.then_inc(sems[("q", e, o.dma_slot)], 16)
                else:
                    ins = o.fn(eng_obj)
                    if o.inc:
                        ins.then_inc(sems[("c", e)], 1)

        def run_sync(s):
            run_stream(s, "sync")
            if final_waits:
                for e in COMPUTE_ENGS:
                    n = self.dseq[e]
                    for slot in range(min(n, DMA_POOL)):
                        last_epoch = (n - 1 - slot) // DMA_POOL + 1
                        s.wait_ge(sems[("q", e, slot)], 16 * last_epoch)

        block.tensor(lambda t: run_stream(t, "tensor"))
        block.vector(lambda v: run_stream(v, "vector"))
        block.scalar(lambda s: run_stream(s, "scalar"))
        block.gpsimd(lambda g: run_stream(g, "gpsimd"))
        block.sync(run_sync)



class Buf:
    __slots__ = ("arena", "c0", "w", "res")

    def __init__(self, arena, c0, w, res):
        self.arena = arena
        self.c0 = c0
        self.w = w
        self.res = res

    def ap(self, p0=0, p1=128, c0=0, c1=None):
        if c1 is None:
            c1 = self.w
        assert 0 <= c0 < c1 <= self.w
        return self.arena.t[p0:p1, self.c0 + c0:self.c0 + c1]

    def bap(self, p0=0, p1=128, cb0=0, cb1=None):
        if cb1 is None:
            cb1 = 2 * self.w
        assert cb0 % 2 == 0 and cb1 % 2 == 0
        return self.arena.t[p0:p1,
                            self.c0 + cb0 // 2:self.c0 + cb1 // 2].bitcast(BF16)

    def free(self):
        self.arena.free(self)


class Arena:
    GRAN = 16

    def __init__(self, tensor, cols, name):
        self.t = tensor
        self.cols = cols
        self.name = name
        self.free_list = [(0, cols)]
        self.live = {}
        self.hwm = 0
        self.ghosts = {}

    def alloc(self, w, tag=""):
        w = (w + 15) & ~15
        best = None
        for i, (c0, sz) in enumerate(self.free_list):
            if sz >= w and (best is None or sz < self.free_list[best][1]):
                best = i
        if best is None:
            raise RuntimeError(f"arena {self.name}: no space for {w} ({tag}); "
                               f"free={self.free_list}")
        c0, sz = self.free_list[best]
        self.free_list[best] = (c0 + w, sz - w)
        if self.free_list[best][1] == 0:
            del self.free_list[best]
        r = Res(f"{self.name}:{tag}@{c0}")
        deps = []
        for g in range(c0 // self.GRAN, (c0 + w) // self.GRAN):
            evs = self.ghosts.pop(g, None)
            if evs:
                deps.extend(evs)
        if deps:
            seen = set()
            uniq = []
            for e in deps:
                if id(e) not in seen:
                    seen.add(id(e))
                    uniq.append(e)
            r.init_deps = tuple(uniq)
        self.live[c0] = (w, r)
        self.hwm = max(self.hwm, max((a + b for a, b in
                                      [(k, v[0]) for k, v in self.live.items()]),
                                     default=0))
        return Buf(self, c0, w, r)

    def free(self, buf):
        w, r = self.live.pop(buf.c0)
        evs = tuple(([r.w] if r.w is not None else []) + r.rs +
                    list(r.init_deps))
        for g in range(buf.c0 // self.GRAN, (buf.c0 + w) // self.GRAN):
            self.ghosts[g] = evs
        self.free_list.append((buf.c0, w))
        self.free_list.sort()
        out = []
        for c0, sz in self.free_list:
            if out and out[-1][0] + out[-1][1] == c0:
                out[-1] = (out[-1][0], out[-1][1] + sz)
            else:
                out.append((c0, sz))
        self.free_list = out



def _perm_qk_cols():
    idx = np.zeros(512, np.int64)
    for half in range(2):
        for h in range(8):
            for i in range(32):
                idx[half * 256 + h * 32 + i] = h * 64 + half * 32 + i
    return idx


def _perm_w1_cols():
    idx = []
    for blk in range(21):
        idx.extend(range(128 * blk, 128 * blk + 128))
        idx.extend(range(2730 + 128 * blk, 2730 + 128 * blk + 128))
    idx.extend(range(2688, 2730))
    idx.extend(range(2730 + 2688, 5460))
    return np.array(idx, np.int64)


def _perm_rope_b(n_pairs, n_heads):
    idx = np.zeros(2 * n_pairs * n_heads, np.int64)
    for h in range(n_heads):
        for p in range(n_pairs):
            idx[h * n_pairs + p] = h * 2 * n_pairs + 2 * p
            idx[n_pairs * n_heads + h * n_pairs + p] = h * 2 * n_pairs + 2 * p + 1
    return idx


def prepare_weights(inp):
    C = lambda a: np.ascontiguousarray(np.asarray(a, dtype=np.float32))
    w = {}
    w["a_inproj"] = C(np.asarray(inp["L0_attn_a_in_proj_w"]).T)
    w["a_gate"] = C(np.asarray(inp["L0_attn_a_gate_w"]).T)
    w["a_outproj"] = C(np.asarray(inp["L0_attn_a_out_proj_w"]).T)
    w["a_w1"] = C(np.asarray(inp["L0_attn_a_mlp_w1"]).T)
    w["a_w2"] = C(np.asarray(inp["L0_attn_a_mlp_w2"]).T)
    cp = np.asarray(inp["L0_attn_a_coupling"])
    bd = np.zeros((64, 64), np.float32)
    for h in range(4):
        bd[16 * h:16 * h + 16, 16 * h:16 * h + 16] = cp[h].T
    w["a_coupling_bd"] = bd
    for k in ("log_decay", "frequency"):
        w["a_" + k] = C(np.asarray(inp["L0_attn_a_" + k]).reshape(1, -1))
    for k in ("gate_b", "n1_g", "n1_b", "n2_g", "n2_b", "mlp_b2"):
        w["a_" + k] = C(np.asarray(inp["L0_attn_a_" + k]).reshape(-1, 128).T)
    w["a_mlp_b1"] = C(np.asarray(inp["L0_attn_a_mlp_b1"]).reshape(-1, 128).T)
    w["b_dq"] = C(inp["L0_attn_b_W_DQ"])
    w["b_uq"] = C(inp["L0_attn_b_W_UQ"])
    w["b_qr"] = C(np.asarray(inp["L0_attn_b_W_QR"])[:, _perm_rope_b(16, 16)])
    w["b_dkv"] = C(inp["L0_attn_b_W_DKV"])
    w["b_uk"] = C(inp["L0_attn_b_W_UK"])
    w["b_uv"] = C(inp["L0_attn_b_W_UV"])
    kr = np.asarray(inp["L0_attn_b_W_KR"])
    krp = np.zeros((1024, 64), np.float32)
    krp[:, 0:16] = kr[:, 0::2]
    krp[:, 32:48] = kr[:, 1::2]
    w["b_kr"] = C(krp)
    w["b_wo"] = C(inp["L0_attn_b_W_O"])
    pqk = _perm_qk_cols()
    wqkv = np.asarray(inp["L0_ffn_wqkv"], dtype=np.float32)
    wq = wqkv[:, :, 0:512][:, :, pqk]
    wk = wqkv[:, :, 512:1024][:, :, pqk]
    wv = wqkv[:, :, 1024:1536]
    w["f_wqkv"] = C(np.concatenate([wq, wk, wv], axis=2))
    w["f_wo"] = C(inp["L0_ffn_wo"])
    p1 = _perm_w1_cols()
    w["f_w1"] = C(np.asarray(inp["L0_ffn_w1"])[:, :, p1])
    b1p = np.asarray(inp["L0_ffn_b1"])[:, p1]
    b1_pad = np.zeros((2, 44 * 128), np.float32)
    b1_pad[:, :42 * 128] = b1p[:, :42 * 128]
    b1_pad[:, 42 * 128:42 * 128 + 42] = b1p[:, 5376:5418]
    b1_pad[:, 43 * 128:43 * 128 + 42] = b1p[:, 5418:5460]
    w["f_b1"] = C(np.stack([b1_pad[l].reshape(-1, 128).T for l in range(2)]))
    w["f_w2"] = C(inp["L0_ffn_w2"])
    w["f_b2"] = C(np.stack([np.asarray(inp["L0_ffn_b2"])[l].reshape(-1, 128).T
                            for l in range(2)]))
    w["f_lng"] = C(np.stack([np.asarray(inp["L0_ffn_ln_g"])[l].reshape(-1, 128).T
                             for l in range(2)]))
    w["f_lnb"] = C(np.stack([np.asarray(inp["L0_ffn_ln_b"])[l].reshape(-1, 128).T
                             for l in range(2)]))
    qs = np.asarray(inp["L0_ffn_q_scale"], dtype=np.float32)
    ks = np.asarray(inp["L0_ffn_k_scale"], dtype=np.float32)

    def expand(s):
        out = np.zeros((2, 512), np.float32)
        for l in range(2):
            for half in range(2):
                for h in range(8):
                    for i in range(32):
                        out[l, half * 256 + h * 32 + i] = s[l, half * 32 + i]
        return out
    w["f_qs"] = C(np.stack([expand(qs)[l].reshape(-1, 128).T for l in range(2)]))
    w["f_ks"] = C(np.stack([expand(ks)[l].reshape(-1, 128).T for l in range(2)]))
    return w


def host_constants():
    c = {}
    c["ident"] = np.eye(128, dtype=np.float32)
    c["identb"] = np.eye(128).astype(ml_dtypes.bfloat16)
    onescol = np.zeros((128, 2), np.float32)
    onescol[:, 0] = 1.0 / D
    onescol[:, 1] = -1.0 / D
    c["stat_ones"] = onescol.astype(ml_dtypes.bfloat16)
    c["stat_onesf"] = onescol.astype(np.float32)
    c["ones2"] = np.ones((2, 128), dtype=ml_dtypes.bfloat16)
    c["ones1f"] = np.ones((1, 128), dtype=np.float32)
    c["ones1b"] = np.ones((1, 128), dtype=ml_dtypes.bfloat16)
    k = np.arange(128)[:, None]
    q = np.arange(128)[None, :]
    c["m_lt"] = np.where(q < k, np.float32(NEG), np.float32(0.0)).astype(np.float32)
    c["m_ge"] = np.where(q >= k, np.float32(NEG), np.float32(0.0)).astype(np.float32)
    c["m_le"] = np.where(q <= k, np.float32(NEG), np.float32(0.0)).astype(np.float32)
    c["m_gt"] = np.where(q > k, np.float32(NEG), np.float32(0.0)).astype(np.float32)
    c["m01_ge"] = (q >= k).astype(np.float32).astype(ml_dtypes.bfloat16)
    for nm in ("m_lt", "m_ge", "m_le", "m_gt"):
        c[nm + "b"] = c[nm].astype(ml_dtypes.bfloat16)
    c["ones_col"] = np.ones((128, 1), dtype=ml_dtypes.bfloat16)
    oh = np.zeros((4, 128, 8), np.float32)
    for kt in range(4):
        for r in range(128):
            rr = kt * 128 + r
            half = rr // 256
            h = (rr - half * 256) // 32
            oh[kt, r, h] = 1.0
    c["l2_onehot"] = oh.astype(ml_dtypes.bfloat16)
    ohT = np.transpose(oh, (0, 2, 1))
    c["l2_onehot_T"] = np.ascontiguousarray(ohT).astype(ml_dtypes.bfloat16)
    c["eps_row"] = np.full((1, 1), LN_EPS, np.float32)
    c["mod32_col"] = (np.arange(128, dtype=np.float32) % 32).reshape(128, 1)
    c["mod16_col"] = (np.arange(128, dtype=np.float32) % 16).reshape(128, 1)
    c["small"] = np.full((1, 1), 1e-12, np.float32)
    return c



class KB:
    def __init__(self, weights, consts):
        nc = bass.Bass("TRN2", target_bir_lowering=False, debug=False,
                       detect_race_conditions=SIM_SAFE)
        self.nc = nc
        self.s = Sched()
        self.dram = {}
        self.cst_dram = {}
        x = nc.dram_tensor("x_in", [D, T], F32, kind="ExternalInput")
        self.dram["x_in"] = (x.ap(), Res("x_in"))
        y = nc.dram_tensor("x_out", [D, T], F32, kind="ExternalOutput")
        self.dram["x_out"] = (y.ap(), Res("x_out"))
        for name, arr in weights.items():
            t = nc.dram_tensor("w_" + name, list(arr.shape), F32,
                               kind="ExternalInput")
            self.dram[name] = (t.ap(), Res(name))
        sp = nc.dram_tensor("x_spill", [D, T], F32, kind="Internal")
        self.dram["x_spill"] = (sp.ap(), Res("x_spill"))
        sp2 = nc.dram_tensor("x_spill2", [D, T], F32, kind="Internal")
        self.dram["x_spill2"] = (sp2.ap(), Res("x_spill2"))
        fl = nc.dram_tensor("flip_scratch", [16, 128], F32, kind="Internal")
        self.dram["flip"] = (fl.ap(), Res("flip"))
        for name, arr in consts.items():
            t = nc.inline_tensor(np.ascontiguousarray(arr), "c_" + name)
            self.cst_dram[name] = (t.ap(), Res("c_" + name))

        self.st = contextlib.ExitStack()
        st = self.st
        af_t = st.enter_context(nc.sbuf_tensor("arena_f32", [128, 30208], F32))
        ab_t = st.enter_context(nc.sbuf_tensor("arena_bf16", [128, 43008], BF16))
        self.af = Arena(af_t, 30208, "f32")
        self.ab = Arena(ab_t, 43008, "bf16")
        self.psA = st.enter_context(nc.psum_tensor("psA", [128, 2048], F32))
        self.psB = st.enter_context(nc.psum_tensor("psB", [128, 2048], F32))
        self.pbank = [Res(f"pb{i}") for i in range(8)]
        self._bank_rr = 0
        self.cst = {}
        self._load_consts(consts)
        self.tap_bufs = {}
        self.wbf = {}
        self._wq_rr = 0

    def wqueue(self):
        self._wq_rr += 1
        return "sync" if self._wq_rr % 2 else "scalar"

    def make_bf16_twins(self, names, weights):
        nc = self.nc
        for n in names:
            arr = weights[n]
            t = nc.dram_tensor("wbf_" + n, list(arr.shape), BF16,
                               kind="Internal")
            r = Res("wbf_" + n)
            dap, dres = self.dram[n]
            self.s.dma("gpsimd",
                       lambda e, o=t.ap(), i=dap: e.dma_start(out=o, in_=i),
                       reads=[dres], writes=[r], tag="wcast_" + n)
            self.wbf[n] = (t.ap(), r)

    def bank_ap(self, i, c0=0, c1=512, p0=0, p1=128):
        t = self.psA if i < 4 else self.psB
        base = (i % 4) * 512
        return t[p0:p1, base + c0:base + c1]

    def bank_bap(self, i, cb0=0, cb1=1024, p0=0, p1=128):
        t = self.psA if i < 4 else self.psB
        base = (i % 4) * 512
        return t[p0:p1, base + cb0 // 2:base + cb1 // 2].bitcast(BF16)

    def take_banks(self, n):
        for _ in range(8):
            s0 = self._bank_rr % 8
            if (s0 % 4) + n <= 4:
                self._bank_rr = (s0 + n) % 8
                return list(range(s0, s0 + n))
            self._bank_rr = ((s0 // 4) * 4 + 4) % 8
        raise RuntimeError("bank alloc")

    def wide_ap(self, banks, c0, c1, p0=0, p1=128):
        t = self.psA if banks[0] < 4 else self.psB
        base = (banks[0] % 4) * 512
        return t[p0:p1, base + c0:base + c1]

    def _load_consts(self, consts):
        nc, st = self.nc, self.st
        specs = {
            "ident": ([128, 128], F32),
            "identb": ([128, 128], BF16),
            "stat_ones": ([128, 2], BF16),
            "stat_onesf": ([128, 2], F32),
            "ones2": ([2, 128], BF16),
            "ones1f": ([1, 128], F32),
            "ones1b": ([1, 128], BF16),
            "m_ltb": ([128, 128], BF16),
            "m_geb": ([128, 128], BF16),
            "m_leb": ([128, 128], BF16),
            "m_gtb": ([128, 128], BF16),
            "m01_ge": ([128, 128], BF16),
            "ones_col": ([128, 1], BF16),
            "l2_onehot": ([4, 128, 8], BF16),
            "l2_onehot_T": ([4, 8, 128], BF16),
            "eps_row": ([1, 1], F32),
            "mod32_col": ([128, 1], F32),
            "mod16_col": ([128, 1], F32),
            "small": ([1, 1], F32),
        }
        for name, (shape, dt) in specs.items():
            if len(shape) == 3:
                tens = st.enter_context(
                    nc.sbuf_tensor("k_" + name, [shape[1], shape[0] * shape[2]], dt))
                r = Res("k_" + name)
                dap, dres = self.cst_dram[name]
                for i in range(shape[0]):
                    self.dma("sync", tens[:, i * shape[2]:(i + 1) * shape[2]],
                             dap[i], [dres], [r], tag=f"cst_{name}{i}")
                self.cst[name] = (tens, r, shape[2])
            else:
                tens = st.enter_context(nc.sbuf_tensor("k_" + name, shape, dt))
                r = Res("k_" + name)
                dap, dres = self.cst_dram[name]
                self.dma("sync", tens[:, :], dap, [dres], [r], tag="cst_" + name)
                self.cst[name] = (tens, r, None)

    def cap(self, name, i=None):
        tens, r, w = self.cst[name]
        if i is None:
            return tens[:, :], r
        return tens[:, i * w:(i + 1) * w], r

    def dma(self, eng, out_ap, in_ap, reads, writes, tag=""):
        return self.s.dma(eng, lambda e, o=out_ap, i=in_ap: e.dma_start(out=o, in_=i),
                          reads=reads, writes=writes, tag=tag)

    def mm(self, out_ap, lhsT_ap, rhs_ap, start, stop, reads, writes, tag=""):
        return self.s.op(
            "tensor",
            lambda t, o=out_ap, l=lhsT_ap, r=rhs_ap, a=start, b=stop:
                t.matmul(o, l, r, start=a, stop=b),
            reads=reads, writes=writes, tag=tag)

    def transp(self, out_ap, in_ap, reads, writes, tag="", bf16=False):
        ident, ires = self.cap("identb" if bf16 else "ident")
        nc = self.nc

        def fn(t, o=out_ap, i=in_ap, d=ident):
            if bf16:
                with nc.allow_low_precision("bf16 transpose, no accumulation"):
                    return t.transpose(o, i, d)
            return t.transpose(o, i, d)
        return self.s.op("tensor", fn, reads=list(reads) + [ires],
                         writes=writes, tag=tag)

    def act(self, out_ap, in_ap, func, reads, writes, bias=0.0, scale=1.0,
            sreads=(), accum_out=None, tag=""):
        def fn(s, o=out_ap, i=in_ap, f=func, b=bias, sc=scale, ao=accum_out):
            if ao is not None:
                return s.activation(o, i, f, bias=b, scale=sc, accum_out=ao)
            return s.activation(o, i, f, bias=b, scale=sc)
        return self.s.op("scalar", fn, reads=reads, writes=writes,
                         sreads=sreads, tag=tag)

    def tt(self, eng, out_ap, in0, in1, op, reads, writes, tag=""):
        return self.s.op(eng,
                         lambda e, o=out_ap, a=in0, b=in1, p=op:
                             e.tensor_tensor(o, a, b, p),
                         reads=reads, writes=writes, tag=tag)

    def ts(self, eng, out_ap, in0, s1, s2, op0, op1, reads, writes,
           sreads=(), tag=""):
        return self.s.op(eng,
                         lambda e, o=out_ap, a=in0, x=s1, y=s2, p0=op0, p1=op1:
                             e.tensor_scalar(o, a, x, y, p0, p1),
                         reads=reads, writes=writes, sreads=sreads, tag=tag)

    def stt(self, eng, out_ap, in0, scalar, in1, op0, op1, reads, writes,
            sreads=(), tag=""):
        return self.s.op(eng,
                         lambda e, o=out_ap, a=in0, sc=scalar, b=in1, p0=op0, p1=op1:
                             e.scalar_tensor_tensor(o, a, sc, b, p0, p1),
                         reads=reads, writes=writes, sreads=sreads, tag=tag)

    def scan(self, out_ap, d0, d1, init, reads, writes, tag=""):
        return self.s.op("vector",
                         lambda e, o=out_ap, a=d0, b=d1, i=init:
                             e.tensor_tensor_scan(o, a, b, i, ALU.mult, ALU.add),
                         reads=reads, writes=writes, tag=tag)

    def copy(self, eng, out_ap, in_ap, reads, writes, tag=""):
        if eng == "scalar":
            return self.s.op(eng, lambda e, o=out_ap, i=in_ap: e.copy(o, i),
                             reads=reads, writes=writes, tag=tag)
        return self.s.op(eng, lambda e, o=out_ap, i=in_ap: e.tensor_copy(o, i),
                         reads=reads, writes=writes, tag=tag)

    def memset(self, eng, ap, val, writes, tag=""):
        o = self.s.op(eng, lambda e, a=ap, v=val: e.memset(a, v),
                      writes=writes, tag=tag)
        o.always_sem = True
        return o

    def reduce(self, eng, out_ap, in_ap, op, reads, writes, negate=False, tag=""):
        return self.s.op(eng,
                         lambda e, o=out_ap, i=in_ap, p=op, n=negate:
                             e.tensor_reduce(o, i, AX.X, p, negate=n),
                         reads=reads, writes=writes, tag=tag)

    def recip(self, out_ap, in_ap, reads, writes, tag=""):
        return self.s.op("vector",
                         lambda e, o=out_ap, i=in_ap: e.reciprocal(o, i),
                         reads=reads, writes=writes, tag=tag)

    def iota(self, out_ap, pattern, base, cm, writes, tag=""):
        return self.s.op("gpsimd",
                         lambda e, o=out_ap, p=pattern, b=base, c=cm:
                             e.iota(o, p, b, c, allow_small_or_imprecise_dtypes=True),
                         writes=writes, tag=tag)

    def tap(self, name, src_bufs, width=T, p1=128):
        nc = self.nc
        n = len(src_bufs)
        t = nc.dram_tensor("tap_" + name, [n * 128, width], F32,
                           kind="ExternalOutput")
        r = Res("tap_" + name)
        for i, b in enumerate(src_bufs):
            tmp = self.af.alloc(width, "tapt")
            self.copy("vector", tmp.ap(0, p1, 0, width), b.ap(0, p1, 0, width),
                      [b.res], [tmp.res], tag="tapc")
            self.dma("sync", t.ap()[i * 128:i * 128 + p1, :],
                     tmp.ap(0, p1, 0, width), [tmp.res], [r], tag="tapd")
            tmp.free()



class Model:
    def __init__(self, kb: KB):
        self.kb = kb
        self.stage = 99
        self.early_out = None
        self.weights_np = None


    def load_param_cols(self, wname, n, tag, rows_total=None):
        kb = self.kb
        dap, dres = kb.dram[wname]
        total = rows_total if rows_total is not None else n * 128
        buf = kb.ab
        b = kb.af.alloc(n, tag)
        src = dap[0:1, 0:total].rearrange("o (j p) -> (o p) j", p=128)
        kb.dma("sync", b.ap(0, 128, 0, n), src, [dres], [b.res], tag=tag)
        return b

    def stats_rows(self, tile_fn, tag, f32=False):
        kb = self.kb
        so_ap, so_res = kb.cap("stat_onesf" if f32 else "stat_ones")
        banks = kb.take_banks(4)
        ar = kb.af if f32 else kb.ab
        x2 = [ar.alloc(T, f"x2_{tag}") for _ in range(2)]
        for kt in range(NKT):
            xb, owned = tile_fn(kt)
            x2b = x2[kt % 2]
            kb.tt("vector", x2b.ap(), xb.ap(), xb.ap(),
                  ALU.mult, [xb.res], [x2b.res], tag=f"sq_{tag}{kt}")
            for c in range(NCH):
                bk = banks[c]
                first = kt == 0
                last = kt == NKT - 1
                kb.mm(kb.bank_ap(bk, 0, CW, 0, 1), so_ap[:, 1:2],
                      xb.ap(0, 128, c * CW, (c + 1) * CW),
                      first, last,
                      [so_res, xb.res], [kb.pbank[bk]],
                      tag=f"st_m_{tag}")
                kb.s.op("tensor",
                        lambda t, o=kb.bank_ap(bk, 0, CW, 32, 33),
                        l=so_ap[:, 0:1],
                        r=x2b.ap(0, 128, c * CW, (c + 1) * CW), fst=first,
                        lst=last:
                            t.matmul(o, l, r, start=fst, stop=lst,
                                     tile_position=(0, 32)),
                        reads=[so_res, x2b.res], writes=[kb.pbank[bk]],
                        tag=f"st_v_{tag}")
            if owned:
                xb.free()
        negm = kb.af.alloc(T, f"negm_{tag}")
        ex2 = kb.af.alloc(T, f"ex2_{tag}")
        for c in range(NCH):
            bk = banks[c]
            kb.copy("scalar", negm.ap(0, 1, c * CW, (c + 1) * CW),
                    kb.bank_ap(bk, 0, CW, 0, 1), [kb.pbank[bk]], [negm.res],
                    tag=f"ev_m_{tag}")
            kb.copy("scalar", ex2.ap(0, 1, c * CW, (c + 1) * CW),
                    kb.bank_ap(bk, 0, CW, 32, 33), [kb.pbank[bk]], [ex2.res],
                    tag=f"ev_v_{tag}")
        for b in x2:
            b.free()
        msq = kb.af.alloc(T, f"msq_{tag}")
        kb.tt("vector", msq.ap(0, 1), negm.ap(0, 1), negm.ap(0, 1), ALU.mult,
              [negm.res], [msq.res], tag=f"msq_{tag}")
        var = msq
        kb.tt("vector", var.ap(0, 1), ex2.ap(0, 1), msq.ap(0, 1), ALU.subtract,
              [ex2.res, msq.res], [var.res], tag=f"var_{tag}")
        ex2.free()
        sd = kb.af.alloc(T, f"sd_{tag}")
        eps_ap, eps_res = kb.cap("eps_row")
        kb.act(sd.ap(0, 1), var.ap(0, 1), AF.Ln, [var.res], [sd.res],
               bias=eps_ap[0:1, 0:1], scale=1.0, sreads=[eps_res],
               tag=f"lnv_{tag}")
        var.free()
        rr = kb.af.alloc(T, f"r_{tag}")
        kb.act(rr.ap(0, 1), sd.ap(0, 1), AF.Exp, [sd.res], [rr.res],
               scale=-0.5, tag=f"rex_{tag}")
        sd.free()
        return negm, rr

    def bcast_row(self, row, tag, out_dtype_f32=True):
        kb = self.kb
        hilo = kb.ab.alloc(T * 2, f"hilo_{tag}")
        kb.copy("vector", hilo.ap(0, 1, 0, T), row.ap(0, 1), [row.res],
                [hilo.res], tag=f"hi_{tag}")
        kb.tt("vector", hilo.ap(0, 1, T, 2 * T), row.ap(0, 1),
              hilo.ap(0, 1, 0, T), ALU.subtract, [row.res, hilo.res],
              [hilo.res], tag=f"lo_{tag}")
        ones1b, o1res = kb.cap("ones1b")
        out = kb.af.alloc(T, f"bc_{tag}")
        banks = kb.take_banks(2)
        for c in range(NCH):
            bk = banks[c % 2]
            kb.mm(kb.bank_ap(bk), ones1b, hilo.ap(0, 1, c * CW, (c + 1) * CW),
                  True, False, [o1res, hilo.res], [kb.pbank[bk]],
                  tag=f"bcm_{tag}")
            kb.mm(kb.bank_ap(bk), ones1b,
                  hilo.ap(0, 1, T + c * CW, T + (c + 1) * CW),
                  False, True, [o1res, hilo.res], [kb.pbank[bk]],
                  tag=f"bcm2_{tag}")
            kb.copy("scalar", out.ap(0, 128, c * CW, (c + 1) * CW),
                    kb.bank_ap(bk), [kb.pbank[bk]], [out.res],
                    tag=f"bce_{tag}")
        hilo.free()
        return out

    def materialize_ln(self, src_f32_tile_fn, stats_tile_fn, g_b=None,
                       tag="ln", out_f32=False):
        kb = self.kb
        negm, rr = self.stats_rows(stats_tile_fn, tag)
        negm_b = self.bcast_row(negm, tag + "m")
        r_b = self.bcast_row(rr, tag + "r")
        negm.free()
        rr.free()
        outs = []
        outs32 = []
        tmp = kb.af.alloc(T, tag + "_t")
        for kt in range(NKT):
            sap, sres = src_f32_tile_fn(kt)
            kb.tt("vector", tmp.ap(), sap, negm_b.ap(), ALU.add,
                  sres + [negm_b.res], [tmp.res], tag=f"{tag}_a{kt}")
            if g_b is None:
                ob = kb.ab.alloc(T, tag + f"_o{kt}")
                kb.tt("vector", ob.ap(), tmp.ap(), r_b.ap(), ALU.mult,
                      [tmp.res, r_b.res], [ob.res], tag=f"{tag}_m{kt}")
                if out_f32:
                    o32 = kb.af.alloc(T, tag + f"_f{kt}")
                    kb.tt("scalar" if False else "vector", o32.ap(), tmp.ap(),
                          r_b.ap(), ALU.mult, [tmp.res, r_b.res], [o32.res],
                          tag=f"{tag}_f{kt}")
                    outs32.append(o32)
                outs.append(ob)
            else:
                gb_buf, g_col, b_col = g_b
                kb.tt("vector", tmp.ap(), tmp.ap(), r_b.ap(), ALU.mult,
                      [tmp.res, r_b.res], [tmp.res], tag=f"{tag}_m{kt}")
                ob = kb.ab.alloc(T, tag + f"_o{kt}")
                kb.ts("vector", ob.ap(), tmp.ap(),
                      gb_buf.ap(0, 128, g_col + kt, g_col + kt + 1),
                      gb_buf.ap(0, 128, b_col + kt, b_col + kt + 1),
                      ALU.mult, ALU.add, [tmp.res, gb_buf.res], [ob.res],
                      sreads=[gb_buf.res], tag=f"{tag}_gb{kt}")
                outs.append(ob)
        tmp.free()
        negm_b.free()
        r_b.free()
        if out_f32:
            return outs, outs32
        return outs

    def sm_prewarm(self, wname, n_kt, m_widths, dram_row0=0, dram_col0=0,
                   layer_idx=None, tag="pw", mb_cols=1024, k_group=None,
                   k_rows=None, chunks_len=4, nk_pre=None):
        kb = self.kb
        if wname in kb.wbf:
            dap, dres = kb.wbf[wname]
        else:
            dap, dres = kb.dram[wname]
        if layer_idx is not None:
            dap = dap[layer_idx]
        if k_rows is None:
            k_rows = [(kt * 128, 128) for kt in range(n_kt)]
        moff = [0]
        for w in m_widths:
            moff.append(moff[-1] + w)
        kg = k_group if k_group is not None else n_kt
        kgroups = [list(range(i, min(i + kg, n_kt)))
                   for i in range(0, n_kt, kg)]
        max_mt = len(m_widths)
        if len(kgroups) > 1:
            max_mt = max(1, 4 // chunks_len)
        cur, cw = [], 0
        for mi, w in enumerate(m_widths):
            if cur and (cw + w > mb_cols or len(cur) >= max_mt):
                break
            cur.append(mi)
            cw += w
        c0 = moff[cur[0]]
        c1 = moff[cur[-1] + 1]
        slabs = {}
        for kt in kgroups[0][:nk_pre] if nk_pre is not None else kgroups[0]:
            r0, nr = k_rows[kt]
            sb = kb.ab.alloc(c1 - c0, f"{tag}_w0k{kt}")
            kb.dma("gpsimd", sb.ap(0, nr, 0, c1 - c0),
                   dap[dram_row0 + r0:dram_row0 + r0 + nr,
                       dram_col0 + c0:dram_col0 + c1],
                   [dres], [sb.res], tag=f"{tag}_pwd{kt}")
            slabs[kt] = sb
        return (slabs, c0)

    def stream_matmul(self, wname, n_kt, m_widths, rhs_fn, evac_fn,
                      dram_row0=0, dram_col0=0, layer_idx=None, tag="sm",
                      aug=None, mb_cols=1024, k_group=None, k_rows=None,
                      chunks=None, pre=None):
        kb = self.kb
        if wname in kb.wbf:
            dap, dres = kb.wbf[wname]
        else:
            dap, dres = kb.dram[wname]
        if layer_idx is not None:
            dap = dap[layer_idx]
        if chunks is None:
            chunks = [(c * CW, CW) for c in range(NCH)]
        if k_rows is None:
            k_rows = [(kt * 128, 128) for kt in range(n_kt)]
        moff = [0]
        for w in m_widths:
            moff.append(moff[-1] + w)
        kg = k_group if k_group is not None else n_kt
        kgroups = [list(range(i, min(i + kg, n_kt))) for i in range(0, n_kt, kg)]
        multi_kg = len(kgroups) > 1
        max_mt = len(m_widths)
        if multi_kg:
            max_mt = max(1, 4 // len(chunks))
        blocks = []
        cur, cw = [], 0
        for mi, w in enumerate(m_widths):
            if cur and (cw + w > mb_cols or len(cur) >= max_mt):
                blocks.append(cur)
                cur, cw = [], 0
            cur.append(mi)
            cw += w
        if cur:
            blocks.append(cur)

        def load_slabs(bi, kts, have=None):
            mis = blocks[bi]
            c0 = moff[mis[0]]
            c1 = moff[mis[-1] + 1]
            slabs = dict(have) if have else {}
            for kt in kts:
                if kt in slabs:
                    continue
                r0, nr = k_rows[kt]
                sb = kb.ab.alloc(c1 - c0, f"{tag}_w{bi}k{kt}")
                kb.dma(kb.wqueue(), sb.ap(0, nr, 0, c1 - c0),
                       dap[dram_row0 + r0:dram_row0 + r0 + nr,
                           dram_col0 + c0:dram_col0 + c1],
                       [dres], [sb.res], tag=f"{tag}_wd{bi}_{kt}")
                slabs[kt] = sb
            return slabs, c0

        rrbanks = kb.take_banks(2) + kb.take_banks(2)
        bank_i = 0
        n_aug = 1 if aug is not None else 0
        for bi in range(len(blocks)):
            mis = blocks[bi]
            if multi_kg:
                bmap = {}
                for j, mi in enumerate(mis):
                    for ci in range(len(chunks)):
                        bmap[(mi, ci)] = rrbanks[(j * len(chunks) + ci) % 4]
                n_mm_total = sum(len(g) for g in kgroups) + n_aug
                i_base = 0
                if bi == 0 and pre is not None:
                    nxt = load_slabs(0, kgroups[0], have=pre[0])
                else:
                    nxt = load_slabs(bi, kgroups[0])
                for gi, kts in enumerate(kgroups):
                    slabs, base = nxt
                    if gi + 1 < len(kgroups):
                        nxt = load_slabs(bi, kgroups[gi + 1])
                    elif bi + 1 < len(blocks):
                        pass
                    for mi in mis:
                        mw = m_widths[mi]
                        for ci, (t0, tw) in enumerate(chunks):
                            bk = bmap[(mi, ci)]
                            pres = kb.pbank[bk]
                            for jj, kt in enumerate(kts):
                                r0, nr = k_rows[kt]
                                rap, rres = rhs_fn(kt, ci)
                                sb = slabs[kt]
                                i_mm = i_base + jj
                                kb.mm(kb.bank_ap(bk, 0, tw, 0, mw),
                                      sb.ap(0, nr, moff[mi] - base,
                                            moff[mi] - base + mw),
                                      rap, i_mm == 0,
                                      i_mm == n_mm_total - 1,
                                      [sb.res] + rres, [pres],
                                      tag=f"{tag}_mm{mi}_{ci}_{kt}")
                            if gi == len(kgroups) - 1:
                                if aug is not None:
                                    lap, lres = aug[0](mi, mw)
                                    rap2, rres2 = aug[1](ci)
                                    kb.mm(kb.bank_ap(bk, 0, tw, 0, mw),
                                          lap, rap2, False, True,
                                          [lres] + rres2, [pres],
                                          tag=f"{tag}_aug{mi}_{ci}")
                                evac_fn(mi, ci, kb.bank_ap(bk, 0, tw, 0, mw),
                                        pres)
                    i_base += len(kts)
                    for sb in slabs.values():
                        sb.free()
            else:
                if bi == 0:
                    if pre is not None:
                        self._sm_slabs = load_slabs(0, kgroups[0],
                                                    have=pre[0])[0]
                        self._sm_base = pre[1]
                    else:
                        self._sm_slabs, self._sm_base = load_slabs(
                            0, kgroups[0])
                slabs, base = self._sm_slabs, self._sm_base
                if bi + 1 < len(blocks):
                    self._sm_slabs, self._sm_base = load_slabs(bi + 1,
                                                               kgroups[0])
                n_mm = n_kt + n_aug
                for mi in mis:
                    mw = m_widths[mi]
                    for ci, (t0, tw) in enumerate(chunks):
                        bk = rrbanks[bank_i % 4]
                        bank_i += 1
                        pres = kb.pbank[bk]
                        for kt in kgroups[0]:
                            r0, nr = k_rows[kt]
                            rap, rres = rhs_fn(kt, ci)
                            sb = slabs[kt]
                            kb.mm(kb.bank_ap(bk, 0, tw, 0, mw),
                                  sb.ap(0, nr, moff[mi] - base,
                                        moff[mi] - base + mw),
                                  rap, kt == kgroups[0][0],
                                  kt == kgroups[0][-1] and aug is None,
                                  [sb.res] + rres, [pres],
                                  tag=f"{tag}_mm{mi}_{ci}_{kt}")
                        if aug is not None:
                            lap, lres = aug[0](mi, mw)
                            rap2, rres2 = aug[1](ci)
                            kb.mm(kb.bank_ap(bk, 0, tw, 0, mw), lap, rap2,
                                  False, True, [lres] + rres2, [pres],
                                  tag=f"{tag}_aug{mi}_{ci}")
                        evac_fn(mi, ci, kb.bank_ap(bk, 0, tw, 0, mw), pres)
                for sb in slabs.values():
                    sb.free()


    def load_x_tile(self, src, kt, tag, queue=None):
        kb = self.kb
        dap, dres = kb.dram[src]
        b = kb.af.alloc(T, tag)
        kb.dma(queue or kb.wqueue(), b.ap(),
               dap[kt * 128:(kt + 1) * 128, :], [dres],
               [b.res], tag=tag)
        return b

    def cast_tile(self, src_buf, tag, eng="vector"):
        kb = self.kb
        b = kb.ab.alloc(T, tag)
        kb.copy(eng, b.ap(), src_buf.ap(), [src_buf.res], [b.res], tag=tag)
        return b


def _chunks(total, step):
    out = []
    c = 0
    while c < total:
        out.append((c, min(step, total - c)))
        c += step
    return out



DEV_STAGE = 99


class Model(Model):

    def emit_floor(self, buf, tmp, p1=128, w=None, tag="fl"):
        kb = self.kb
        if w is None:
            w = buf.w
        kb.copy("vector", tmp.ap(0, p1, 0, w).bitcast(mybir.dt.int32),
                buf.ap(0, p1, 0, w), [buf.res], [tmp.res], tag=tag + "_i")
        kb.copy("vector", tmp.ap(0, p1, 0, w),
                tmp.ap(0, p1, 0, w).bitcast(mybir.dt.int32),
                [tmp.res], [tmp.res], tag=tag + "_f")
        kb.tt("vector", buf.ap(0, p1, 0, w), tmp.ap(0, p1, 0, w),
              buf.ap(0, p1, 0, w), ALU.is_gt, [tmp.res, buf.res], [buf.res],
              tag=tag + "_g")
        kb.tt("vector", buf.ap(0, p1, 0, w), tmp.ap(0, p1, 0, w),
              buf.ap(0, p1, 0, w), ALU.subtract, [tmp.res, buf.res],
              [buf.res], tag=tag + "_s")

    def sincos_from_angle(self, ang, tag, want_cos=True):
        kb = self.kb
        inv2pi = 1.0 / (2.0 * math.pi)
        c_hi = 6.28125
        c_lo = 2.0 * math.pi - c_hi
        q = kb.af.alloc(T, tag + "_q")
        kb.ts("vector", q.ap(), ang.ap(), inv2pi, 0.5, ALU.mult, ALU.add,
              [ang.res], [q.res], tag=tag + "_q")
        qi = kb.af.alloc(T, tag + "_qi")
        self.emit_floor(q, qi, tag=tag + "_fl")
        qi.free()
        red = kb.af.alloc(T, tag + "_red")
        kb.stt("vector", red.ap(), q.ap(), -c_hi, ang.ap(), ALU.mult, ALU.add,
               [q.res, ang.res], [red.res], tag=tag + "_r1")
        kb.stt("vector", red.ap(), q.ap(), -c_lo, red.ap(), ALU.mult, ALU.add,
               [q.res, red.res], [red.res], tag=tag + "_r2")
        q.free()
        kb.ts("vector", red.ap(), red.ap(), math.pi, -math.pi, ALU.min,
              ALU.max, [red.res], [red.res], tag=tag + "_cl")
        sin_b = kb.af.alloc(T, tag + "_sin")
        kb.act(sin_b.ap(), red.ap(), AF.Sin, [red.res], [sin_b.res],
               tag=tag + "_sin")
        cos_b = None
        if want_cos:
            half = kb.af.alloc(T, tag + "_h2")
            kb.act(half.ap(), red.ap(), AF.Sin, [red.res], [half.res],
                   scale=0.5, tag=tag + "_sh")
            cos_b = kb.af.alloc(T, tag + "_cos")
            kb.tt("vector", cos_b.ap(), half.ap(), half.ap(), ALU.mult,
                  [half.res], [cos_b.res], tag=tag + "_sq")
            kb.ts("vector", cos_b.ap(), cos_b.ap(), -2.0, 1.0, ALU.mult,
                  ALU.add, [cos_b.res], [cos_b.res], tag=tag + "_c")
            half.free()
        red.free()
        return sin_b, cos_b

    def iota_row_tile(self, tag):
        kb = self.kb
        it = kb.af.alloc(T, tag)
        kb.iota(it.ap(), [[1, T]], 0, 0, [it.res], tag=tag)
        return it


    def load_gb_cols(self, gname, bname, tag):
        kb = self.kb
        b = kb.af.alloc(16, tag)
        for j, nm in enumerate((gname, bname)):
            dap, dres = kb.dram[nm]
            n = dap.tensor.shape[1]
            kb.dma("sync", kb.af.t[0:128, b.c0 + 8 * j:b.c0 + 8 * j + n],
                   dap, [dres], [b.res], tag=tag + nm)
        return b

    def attn_a(self, xa, xa_bf):
        kb = self.kb
        n1gb = self.load_gb_cols("a_n1_g", "a_n1_b", "n1gb")
        xn_bf = self.materialize_ln(
            lambda kt: (xa[kt].ap(), [xa[kt].res]),
            lambda kt: (xa_bf[kt], False),
            g_b=(n1gb, 0, 8), tag="n1")
        for b in xa_bf:
            b.free()
        n1gb.free()
        if self.stage == 101:
            self.early_out = [(b, 0, 128) for b in xn_bf]
            return xa
        sp_ap, sp_res = kb.dram["x_spill"]
        for kt in range(NKT):
            kb.dma("sync", sp_ap[kt * 128:(kt + 1) * 128, :], xa[kt].ap(),
                   [xa[kt].res], [sp_res], tag=f"aspl{kt}")
            xa[kt].free()
        xa = None

        beta = kb.af.alloc(T, "beta")
        pre_inp = self.sm_prewarm("a_inproj", NKT, [128], tag="pwain",
                                  chunks_len=4)

        def evac_beta(mi, c, ps_ap, pres):
            kb.copy("scalar", beta.ap(0, 128, c * CW, (c + 1) * CW), ps_ap,
                    [pres], [beta.res], tag="ev_beta")

        self.stream_matmul("a_inproj", NKT, [128],
                           lambda kt, c: (xn_bf[kt].ap(0, 128, c * CW, (c + 1) * CW),
                                          [xn_bf[kt].res]),
                           evac_beta, tag="inproj", pre=pre_inp)
        if self.stage == 102:
            self.early_out = [(beta, 0, 128)]
            return xa

        ld_col = kb.af.alloc(16, "ldcol")
        fq_col = kb.af.alloc(16, "fqcol")
        for nm, buf in (("a_log_decay", ld_col), ("a_frequency", fq_col)):
            dap, dres = kb.dram[nm]
            src = dap[0:1, 0:64].rearrange("a b -> b a")
            kb.dma("sync", kb.af.t[0:64, buf.c0:buf.c0 + 1], src, [dres],
                   [buf.res], tag="ld_" + nm)
            kb.dma("sync", kb.af.t[64:128, buf.c0:buf.c0 + 1], src, [dres],
                   [buf.res], tag="ld2_" + nm)
        mag_col = kb.af.alloc(16, "magcol")
        kb.act(mag_col.ap(0, 128, 0, 1), ld_col.ap(0, 128, 0, 1), AF.Sigmoid,
               [ld_col.res], [mag_col.res], tag="mag")
        ld_col.free()
        it = self.iota_row_tile("ssm_iota")
        ang = kb.af.alloc(T, "ssm_ang")
        kb.ts("vector", ang.ap(), it.ap(), fq_col.ap(0, 128, 0, 1), None,
              ALU.mult, ALU.bypass, [it.res, fq_col.res], [ang.res],
              sreads=[fq_col.res], tag="ssm_ang")
        fq_col.free()
        it.free()
        ST, CT = self.sincos_from_angle(ang, "ssm")
        ang.free()
        MT = kb.af.alloc(T, "ssm_mt")
        kb.memset("vector", MT.ap(), 1.0, [MT.res], tag="mt0")
        kb.ts("vector", MT.ap(), MT.ap(), mag_col.ap(0, 128, 0, 1), None,
              ALU.mult, ALU.bypass, [MT.res, mag_col.res], [MT.res],
              sreads=[mag_col.res], tag="mt1")
        mag_col.free()
        if self.stage in (1031, 1032, 1033):
            b = {1031: ST, 1032: CT, 1033: MT}[self.stage]
            self.early_out = [(b, 0, 128)]
            return xa

        G = kb.af.alloc(T, "ssm_g")
        t1 = kb.af.alloc(T, "ssm_t1")
        t2 = kb.af.alloc(T, "ssm_t2")
        kb.tt("vector", t1.ap(0, 64), beta.ap(64, 128), ST.ap(64, 128),
              ALU.mult, [beta.res, ST.res], [t1.res], tag="g1")
        kb.tt("vector", t2.ap(0, 64), beta.ap(0, 64), CT.ap(0, 64),
              ALU.mult, [beta.res, CT.res], [t2.res], tag="g2")
        kb.tt("vector", G.ap(0, 64), t1.ap(0, 64), t2.ap(0, 64), ALU.add,
              [t1.res, t2.res], [G.res], tag="g3")
        kb.tt("vector", t1.ap(64, 128), beta.ap(0, 64), ST.ap(0, 64),
              ALU.mult, [beta.res, ST.res], [t1.res], tag="g4")
        kb.tt("vector", t2.ap(64, 128), beta.ap(64, 128), CT.ap(64, 128),
              ALU.mult, [beta.res, CT.res], [t2.res], tag="g5")
        kb.tt("vector", G.ap(64, 128), t2.ap(64, 128), t1.ap(64, 128),
              ALU.subtract, [t1.res, t2.res], [G.res], tag="g6")
        beta.free()
        if self.stage == 1034:
            self.early_out = [(G, 0, 128)]
            return xa

        DD = kb.af.alloc(T, "ssm_d")
        kb.scan(DD.ap(), MT.ap(), G.ap(), 0.0, [MT.res, G.res], [DD.res],
                tag="scan")
        G.free()
        MT.free()

        C = kb.ab.alloc(T, "ssm_c")
        kb.tt("vector", t1.ap(0, 64), DD.ap(64, 128), ST.ap(64, 128),
              ALU.mult, [DD.res, ST.res], [t1.res], tag="u1")
        kb.tt("vector", t2.ap(0, 64), DD.ap(0, 64), CT.ap(0, 64),
              ALU.mult, [DD.res, CT.res], [t2.res], tag="u2")
        kb.tt("vector", C.ap(0, 64), t2.ap(0, 64), t1.ap(0, 64),
              ALU.subtract, [t1.res, t2.res], [C.res], tag="u3")
        kb.tt("vector", t1.ap(64, 128), DD.ap(0, 64), ST.ap(0, 64),
              ALU.mult, [DD.res, ST.res], [t1.res], tag="u4")
        kb.tt("vector", t2.ap(64, 128), DD.ap(64, 128), CT.ap(64, 128),
              ALU.mult, [DD.res, CT.res], [t2.res], tag="u5")
        kb.tt("vector", C.ap(64, 128), t1.ap(64, 128), t2.ap(64, 128),
              ALU.add, [t1.res, t2.res], [C.res], tag="u6")
        DD.free()
        ST.free()
        CT.free()
        t1.free()
        t2.free()
        if self.stage == 103:
            self.early_out = [(C, 0, 128)]
            return xa

        cpl = kb.ab.alloc(64, "cpl")
        dap, dres = kb.wbf["a_coupling_bd"]
        kb.dma("sync", cpl.ap(0, 64, 0, 64), dap, [dres], [cpl.res],
               tag="cpl_ld")
        kb.dma("sync", cpl.ap(64, 128, 0, 64), dap, [dres], [cpl.res],
               tag="cpl_ld2")
        EIG = kb.ab.alloc(T, "eig")
        bks = kb.take_banks(2)
        for c in range(NCH):
            for half in range(2):
                bk = bks[half]
                kb.mm(kb.bank_ap(bk, 0, CW, 0, 64),
                      cpl.ap(64 * half, 64 * half + 64, 0, 64),
                      C.ap(64 * half, 64 * half + 64, c * CW, (c + 1) * CW),
                      True, True, [cpl.res, C.res], [kb.pbank[bk]],
                      tag=f"cplmm{c}_{half}")
                kb.copy("scalar",
                        EIG.ap(64 * half, 64 * half + 64, c * CW, (c + 1) * CW),
                        kb.bank_ap(bk, 0, CW, 0, 64), [kb.pbank[bk]],
                        [EIG.res], tag=f"cplev{c}_{half}")
        C.free()
        cpl.free()
        if self.stage == 104:
            self.early_out = [(EIG, 0, 128)]
            return xa

        gateb = self.load_gb_cols("a_gate_b", "a_gate_b", "gateb")
        gdap, gdres = kb.wbf["a_gate"]
        odap, odres = kb.wbf["a_outproj"]
        sig = kb.af.alloc(CW, "sig")
        htmp = kb.af.alloc(CW, "htmp")
        xa2 = [None] * NKT
        for mb in range(4):
            gslabs = []
            for kt in range(NKT):
                sb = kb.ab.alloc(256, f"gw{mb}_{kt}")
                kb.dma(kb.wqueue(), sb.ap(0, 128, 0, 256),
                       gdap[kt * 128:(kt + 1) * 128, mb * 256:(mb + 1) * 256],
                       [gdres], [sb.res], tag=f"gwd{mb}_{kt}")
                gslabs.append(sb)
            oslab = kb.ab.alloc(256, f"ow{mb}")
            kb.dma(kb.wqueue(), oslab.ap(0, 128, 0, 256),
                   odap[0:128, mb * 256:(mb + 1) * 256], [odres],
                   [oslab.res], tag=f"owd{mb}")
            for mi2 in range(2):
                mi = 2 * mb + mi2
                xs = self.load_x_tile("x_spill", mi, f"gxs{mi}")
                xa2[mi] = kb.af.alloc(T, f"xa2_{mi}")
                for c in range(NCH):
                    bkg = kb.take_banks(1)[0]
                    bko = kb.take_banks(1)[0]
                    for kt in range(NKT):
                        kb.mm(kb.bank_ap(bkg),
                              gslabs[kt].ap(0, 128, mi2 * 128, mi2 * 128 + 128),
                              xn_bf[kt].ap(0, 128, c * CW, (c + 1) * CW),
                              kt == 0, kt == NKT - 1,
                              [gslabs[kt].res, xn_bf[kt].res],
                              [kb.pbank[bkg]], tag=f"gmm{mi}_{c}_{kt}")
                    kb.mm(kb.bank_ap(bko),
                          oslab.ap(0, 128, mi2 * 128, mi2 * 128 + 128),
                          EIG.ap(0, 128, c * CW, (c + 1) * CW),
                          True, True, [oslab.res, EIG.res], [kb.pbank[bko]],
                          tag=f"omm{mi}_{c}")
                    kb.act(sig.ap(), kb.bank_ap(bkg), AF.Sigmoid,
                           [kb.pbank[bkg], gateb.res], [sig.res],
                           bias=gateb.ap(0, 128, mi, mi + 1),
                           sreads=[gateb.res], tag=f"sig{mi}_{c}")
                    kb.tt("vector", htmp.ap(), sig.ap(), kb.bank_ap(bko),
                          ALU.mult, [sig.res, kb.pbank[bko]], [htmp.res],
                          tag=f"hm{mi}_{c}")
                    kb.tt("vector", xa2[mi].ap(0, 128, c * CW, (c + 1) * CW),
                          xs.ap(0, 128, c * CW, (c + 1) * CW), htmp.ap(),
                          ALU.add, [xs.res, htmp.res], [xa2[mi].res],
                          tag=f"hr{mi}_{c}")
                xs.free()
            for sb in gslabs:
                sb.free()
            oslab.free()
        sig.free()
        htmp.free()
        gateb.free()
        EIG.free()
        for b in xn_bf:
            b.free()

        xa = xa2
        if self.stage == 105:
            self.early_out = [(b, 0, 128) for b in xa2]
            return xa
        n2gb = self.load_gb_cols("a_n2_g", "a_n2_b", "n2gb")
        xa_bf2 = [self.cast_tile(xa[kt], f"xabf2_{kt}") for kt in range(NKT)]
        x2n_bf = self.materialize_ln(
            lambda kt: (xa[kt].ap(), [xa[kt].res]),
            lambda kt: (xa_bf2[kt], False),
            g_b=(n2gb, 0, 8), tag="n2")
        for b in xa_bf2:
            b.free()
        n2gb.free()

        b1c = kb.af.alloc(32, "b1c")
        dap, dres = kb.dram["a_mlp_b1"]
        kb.dma("sync", kb.af.t[0:128, b1c.c0:b1c.c0 + 32], dap, [dres],
               [b1c.res], tag="b1c")
        b2c = self.load_gb_cols("a_mlp_b2", "a_mlp_b2", "b2c")

        TCW = 512
        pre_aw1 = self.sm_prewarm("a_w1", NKT, [128] * 32, tag="pwaw1",
                                  mb_cols=512, chunks_len=1, nk_pre=4)
        for tc0 in range(0, T, TCW):
            hid = [kb.ab.alloc(TCW, f"ahid{m}") for m in range(32)]
            chunks = [(tc0, TCW)]

            def rhs_w1(kt, ci, t0=tc0):
                return (x2n_bf[kt].ap(0, 128, t0, t0 + TCW),
                        [x2n_bf[kt].res])

            def evac_w1(mi, ci, ps_ap, pres):
                kb.act(hid[mi].ap(), ps_ap, AF.Silu,
                       [pres, b1c.res], [hid[mi].res],
                       bias=b1c.ap(0, 128, mi, mi + 1), sreads=[b1c.res],
                       tag=f"silu{mi}")

            pre_aw2 = self.sm_prewarm("a_w2", 32, [128] * 8,
                                      tag=f"pwaw2_{tc0}", mb_cols=1024,
                                      k_group=8, chunks_len=1, nk_pre=2)
            self.stream_matmul("a_w1", NKT, [128] * 32, rhs_w1, evac_w1,
                               chunks=chunks, tag="aw1", mb_cols=512,
                               pre=pre_aw1)
            pre_aw1 = None

            def rhs_w2(kt, ci, t0=tc0):
                return (hid[kt].ap(), [hid[kt].res])

            def evac_w2(mi, ci, ps_ap, pres, t0=tc0):
                kb.stt("vector", xa[mi].ap(0, 128, t0, t0 + TCW), ps_ap,
                       b2c.ap(0, 128, mi, mi + 1),
                       xa[mi].ap(0, 128, t0, t0 + TCW), ALU.add, ALU.add,
                       [pres, xa[mi].res, b2c.res], [xa[mi].res],
                       sreads=[b2c.res], tag=f"w2r{mi}")

            self.stream_matmul("a_w2", 32, [128] * 8, rhs_w2, evac_w2,
                               chunks=chunks, tag="aw2", mb_cols=1024,
                               k_group=8, pre=pre_aw2)
            for b in hid:
                b.free()
        for b in x2n_bf:
            b.free()
        b1c.free()
        b2c.free()
        return xa


    def build_ffn_tables(self):
        kb = self.kb
        it = self.iota_row_tile("ft_iota")
        icol_ap, icol_res = self.kb.cap("mod32_col")
        invf = kb.af.alloc(16, "ft_if")
        kb.act(invf.ap(0, 128, 0, 1), icol_ap, AF.Exp,
               [icol_res], [invf.res],
               scale=-2.0 * math.log(10000.0) / 64.0, tag="ft_inv")
        lnb = kb.af.alloc(16, "ft_lb")
        lnb2 = kb.af.alloc(16, "ft_lb2")
        kb.ts("vector", lnb.ap(0, 128, 0, 1), icol_ap,
              2.0 / (1.4 * 64.0), (0.4 * 64.0) / (1.4 * 64.0), ALU.mult,
              ALU.add, [icol_res], [lnb.res], tag="ft_b")
        kb.act(lnb2.ap(0, 128, 0, 1), lnb.ap(0, 128, 0, 1), AF.Ln,
               [lnb.res], [lnb2.res], tag="ft_lnb")
        lnb.free()
        lnb = lnb2
        ang = kb.af.alloc(T, "ft_ang")
        kb.ts("vector", ang.ap(), it.ap(), invf.ap(0, 128, 0, 1), None,
              ALU.mult, ALU.bypass, [it.res, invf.res], [ang.res],
              sreads=[invf.res], tag="ft_ang")
        invf.free()
        sinb, cosb = self.sincos_from_angle(ang, "ftsc")
        ang.free()
        pw = kb.af.alloc(T, "ft_pw")
        kb.ts("vector", pw.ap(), it.ap(), 1.0 / 256.0, -(T // 2) / 256.0,
              ALU.mult, ALU.add, [it.res], [pw.res], tag="ft_pw")
        it.free()
        sc = kb.af.alloc(T, "ft_sc")
        kb.ts("vector", sc.ap(), pw.ap(), lnb.ap(0, 128, 0, 1), None,
              ALU.mult, ALU.bypass, [pw.res, lnb.res], [sc.res],
              sreads=[lnb.res], tag="ft_scm")
        pw.free()
        lnb.free()
        kb.act(sc.ap(), sc.ap(), AF.Exp, [sc.res], [sc.res], tag="ft_sce")
        dap, dres = kb.dram["ftab"]
        tmp = kb.ab.alloc(T, "ft_t")
        for j, (trig, trig_res) in enumerate(((cosb.ap(), cosb.res),
                                              (sinb.ap(), sinb.res))):
            kb.tt("vector", tmp.ap(), trig, sc.ap(), ALU.mult,
                  [trig_res, sc.res], [tmp.res], tag=f"ft_q{j}")
            kb.dma("sync", dap[j], tmp.ap(), [tmp.res], [dres],
                   tag=f"ft_sq{j}")
        rsc = kb.af.alloc(T, "ft_rs")
        kb.act(rsc.ap(), sc.ap(), AF.Ln, [sc.res], [rsc.res], tag="ft_lsc")
        kb.act(rsc.ap(), rsc.ap(), AF.Exp, [rsc.res], [rsc.res], scale=-1.0,
               sreads=[rsc.res], tag="ft_esc")
        sc.free()
        for j, (trig, trig_res) in enumerate(((cosb.ap(), cosb.res),
                                              (sinb.ap(), sinb.res))):
            kb.tt("vector", tmp.ap(), trig, rsc.ap(), ALU.mult,
                  [trig_res, rsc.res], [tmp.res], tag=f"ft_k{j}")
            kb.dma("sync", dap[2 + j], tmp.ap(), [tmp.res], [dres],
                   tag=f"ft_sk{j}")
        rsc.free()
        tmp.free()
        sinb.free()
        cosb.free()

    def build_attnb_tables(self):
        kb = self.kb
        it = self.iota_row_tile("bt_iota")
        pcol_ap, pcol_res = self.kb.cap("mod16_col")
        invf = kb.af.alloc(16, "bt_if")
        kb.act(invf.ap(0, 128, 0, 1), pcol_ap, AF.Exp,
               [pcol_res], [invf.res],
               scale=-2.0 * math.log(10000.0) / 32.0, tag="bt_inv")
        ang = kb.af.alloc(T, "bt_ang")
        kb.ts("vector", ang.ap(), it.ap(), invf.ap(0, 128, 0, 1), None,
              ALU.mult, ALU.bypass, [it.res, invf.res], [ang.res],
              sreads=[invf.res], tag="bt_ang")
        invf.free()
        it.free()
        sinb, cosb = self.sincos_from_angle(ang, "btsc")
        ang.free()
        dap, dres = kb.dram["btab"]
        kb.dma("sync", dap[0], cosb.ap(), [cosb.res], [dres], tag="bt_sc")
        kb.dma("sync", dap[1], sinb.ap(), [sinb.res], [dres], tag="bt_ss")
        sinb.free()
        cosb.free()


    def ffn_layer(self, y, l, inst):
        kb = self.kb
        dbg = (inst == 0 and l == 0)
        tg = f"F{inst}{l}"
        pre_qkv = [self.sm_prewarm("f_wqkv", NKT, [128] * 12, layer_idx=l,
                                   tag=f"{tg}pwq0", mb_cols=512,
                                   chunks_len=2), None]
        qkv = [kb.ab.alloc(T, f"{tg}qkv{m}") for m in range(12)]
        HT = T // 2
        for th in range(2):
            t0 = th * HT
            h_bf = []
            for kt in range(NKT):
                b = kb.ab.alloc(HT, f"{tg}h{th}_{kt}")
                eng = "scalar" if kt % 2 else "vector"
                if kt < 4:
                    kb.copy(eng, b.ap(), y[kt].ap(0, 128, t0, t0 + HT),
                            [y[kt].res], [b.res], tag=f"{tg}hc{th}{kt}")
                elif t0 == 0:
                    kb.memset("vector", b.ap(0, 128, 0, 1), 0.0, [b.res],
                              tag=f"{tg}hz{kt}")
                    kb.copy(eng, b.ap(0, 128, 1, HT),
                            y[kt].ap(0, 128, 0, HT - 1), [y[kt].res],
                            [b.res], tag=f"{tg}hs{th}{kt}")
                else:
                    kb.copy(eng, b.ap(0, 128, 0, HT),
                            y[kt].ap(0, 128, t0 - 1, t0 + HT - 1),
                            [y[kt].res], [b.res], tag=f"{tg}hs{th}{kt}")
                h_bf.append(b)
            chunks_h = [(t0, CW), (t0 + CW, CW)]

            def evac_qkv(mi, ci, ps_ap, pres, chunks_h=chunks_h):
                cc = chunks_h[ci][0]
                kb.copy("scalar", qkv[mi].ap(0, 128, cc, cc + CW),
                        ps_ap, [pres], [qkv[mi].res],
                        tag=f"{tg}qkve{th}{mi}_{ci}")

            self.stream_matmul("f_wqkv", NKT, [128] * 12,
                               lambda kt, ci, h_bf=h_bf: (
                                   h_bf[kt].ap(0, 128, ci * CW,
                                               (ci + 1) * CW),
                                   [h_bf[kt].res]),
                               evac_qkv, layer_idx=l, chunks=chunks_h,
                               tag=f"{tg}qkv{th}", mb_cols=512,
                               pre=pre_qkv[th])
            if th == 0:
                pre_qkv[1] = self.sm_prewarm(
                    "f_wqkv", NKT, [128] * 12, layer_idx=l,
                    tag=f"{tg}pwq1", mb_cols=512, chunks_len=2)
            for b in h_bf:
                b.free()

        if dbg and self.stage == 201:
            self.early_out = [(qkv[i], 0, 128) for i in range(8)]
            return
        qs_col = kb.af.alloc(16, f"{tg}qsc")
        ks_col = kb.af.alloc(16, f"{tg}ksc")
        for nm, buf in (("f_qs", qs_col), ("f_ks", ks_col)):
            dap, dres = kb.dram[nm]
            kb.dma("sync", kb.af.t[0:128, buf.c0:buf.c0 + 4], dap[l],
                   [dres], [buf.res], tag=f"{tg}ld{nm}")
        tab_dap, tab_dres = kb.dram["ftab"]
        qt_out = []
        for side in range(2):
            tiles = qkv[side * 4:side * 4 + 4]
            scol = qs_col if side == 0 else ks_col
            oh_ap, oh_res, _ = kb.cst["l2_onehot"]
            banks = kb.take_banks(4)
            q2 = [kb.ab.alloc(T, f"{tg}q2_{side}{i}") for i in range(2)]
            for i in range(4):
                q2b = q2[i % 2]
                kb.tt("vector", q2b.ap(), tiles[i].ap(), tiles[i].ap(),
                      ALU.mult, [tiles[i].res], [q2b.res],
                      tag=f"{tg}sq{side}{i}")
                for c in range(NCH):
                    bk = banks[c]
                    kb.mm(kb.bank_ap(bk, 0, CW, 0, 8),
                          kb.cst["l2_onehot"][0][:, i * 8:(i + 1) * 8],
                          q2b.ap(0, 128, c * CW, (c + 1) * CW),
                          i == 0, i == 3, [oh_res, q2b.res],
                          [kb.pbank[bk]], tag=f"{tg}l2m{side}{i}{c}")
            for b in q2:
                b.free()
            nrm = kb.af.alloc(T, f"{tg}nrm{side}")
            for c in range(NCH):
                bk = banks[c]
                kb.ts("vector", nrm.ap(0, 8, c * CW, (c + 1) * CW),
                      kb.bank_ap(bk, 0, CW, 0, 8), 1e-24, None,
                      ALU.max, ALU.bypass, [kb.pbank[bk]], [nrm.res],
                      tag=f"{tg}l2c{side}{c}")
            rqf = kb.af.alloc(T, f"{tg}rqf{side}")
            kb.act(rqf.ap(0, 8), nrm.ap(0, 8), AF.Ln, [nrm.res], [rqf.res],
                   tag=f"{tg}l2l{side}")
            rq = kb.ab.alloc(T, f"{tg}rq{side}")
            kb.act(rq.ap(0, 8), rqf.ap(0, 8), AF.Exp, [rqf.res], [rq.res],
                   scale=-0.5, tag=f"{tg}l2e{side}")
            nrm.free()
            rqf.free()
            cs = kb.ab.alloc(T, f"{tg}cs{side}")
            ss = kb.ab.alloc(T, f"{tg}ss{side}")
            kb.dma("sync", cs.ap(), tab_dap[2 * side + 0], [tab_dres],
                   [cs.res], tag=f"{tg}tc{side}")
            kb.dma("sync", ss.ap(), tab_dap[2 * side + 1], [tab_dres],
                   [ss.res], tag=f"{tg}ts{side}")
            if dbg and self.stage == 2021 and side == 0:
                self.early_out = [(cs, 0, 128), (ss, 0, 128)]
                return
            a_t = [kb.ab.alloc(T, f"{tg}a{side}{i}") for i in range(4)]
            ohT_ap, ohT_res, _ = kb.cst["l2_onehot_T"]
            bks2 = kb.take_banks(2)
            for i in range(4):
                for c in range(NCH):
                    bk = bks2[c % 2]
                    kb.mm(kb.bank_ap(bk), kb.cst["l2_onehot_T"][0][:, i * 128:(i + 1) * 128],
                          rq.ap(0, 8, c * CW, (c + 1) * CW), True, True,
                          [ohT_res, rq.res], [kb.pbank[bk]],
                          tag=f"{tg}rqb{side}{i}{c}")
                    kb.stt("vector", a_t[i].ap(0, 128, c * CW, (c + 1) * CW),
                           tiles[i].ap(0, 128, c * CW, (c + 1) * CW),
                           scol.ap(0, 128, i, i + 1), kb.bank_ap(bk),
                           ALU.mult, ALU.mult,
                           [tiles[i].res, scol.res, kb.pbank[bk]],
                           [a_t[i].res], sreads=[scol.res],
                           tag=f"{tg}am{side}{i}{c}")
                tiles[i].free()
            rq.free()
            tsw = kb.ab.alloc(T, f"{tg}tsw{side}")
            for i in range(4):
                prt = (i + 2) % 4
                ob = kb.ab.alloc(T, f"{tg}rp{side}{i}")
                kb.tt("vector", tsw.ap(), a_t[prt].ap(), ss.ap(), ALU.mult,
                      [a_t[prt].res, ss.res], [tsw.res],
                      tag=f"{tg}rs{side}{i}")
                kb.tt("vector", ob.ap(), a_t[i].ap(), cs.ap(), ALU.mult,
                      [a_t[i].res, cs.res], [ob.res], tag=f"{tg}rc{side}{i}")
                kb.tt("vector", ob.ap(), ob.ap(), tsw.ap(),
                      ALU.subtract if i < 2 else ALU.add,
                      [ob.res, tsw.res], [ob.res], tag=f"{tg}ra{side}{i}")
                qt_out.append(ob)
            tsw.free()
            for b in a_t:
                b.free()
            cs.free()
            ss.free()
        qs_col.free()
        ks_col.free()
        if self.early_out is not None:
            return
        qr_t = qt_out[0:4]
        kr_t = qt_out[4:8]
        if dbg and self.stage == 202:
            self.early_out = [(b, 0, 128) for b in qt_out]
            return

        v_t = qkv[8:12]
        vtok = [kb.ab.alloc(16 * 65 + 16, f"{tg}vt{h}") for h in range(8)]
        for h in range(8):
            kb.memset("vector", vtok[h].ap(0, 128, 0, 16 * 65), 1.0,
                      [vtok[h].res], tag=f"{tg}vz{h}")
        bks = kb.take_banks(2)
        for i in range(4):
            for tc in range(16):
                bk = bks[tc % 2]
                kb.transp(kb.bank_bap(bk, 0, 128),
                          v_t[i].ap(0, 128, tc * 128, (tc + 1) * 128),
                          [v_t[i].res], [kb.pbank[bk]], tag=f"{tg}vt{i}_{tc}",
                          bf16=True)
                for hh in range(2):
                    h = 2 * i + hh
                    kb.copy("vector",
                            vtok[h].ap(0, 128, tc * 65, tc * 65 + 64),
                            kb.bank_bap(bk, hh * 64, hh * 64 + 64, 0, 128),
                            [kb.pbank[bk]], [vtok[h].res],
                            tag=f"{tg}ve{h}_{tc}")
        for b in v_t:
            b.free()

        att = [kb.ab.alloc(T, f"{tg}att{i}") for i in range(4)]
        self.local_attention(qr_t, kr_t, vtok, att, tg)
        for b in qr_t + kr_t:
            b.free()
        for b in vtok:
            b.free()
        if dbg and self.stage == 203:
            self.early_out = [(b, 0, 128) for b in att]
            return

        m_widths_w1 = [128] * 42 + [42, 42]
        pre_w1 = self.sm_prewarm("f_w1", NKT, m_widths_w1, layer_idx=l,
                                 tag=f"{tg}pw1a", mb_cols=512, chunks_len=1)

        def evac_wo(mi, ci, ps_ap, pres):
            kb.tt("vector", y[mi].ap(0, 128, ci * CW, (ci + 1) * CW),
                  y[mi].ap(0, 128, ci * CW, (ci + 1) * CW), ps_ap, ALU.add,
                  [y[mi].res, pres], [y[mi].res], tag=f"{tg}wor{mi}_{ci}")

        self.stream_matmul("f_wo", 4, [128] * 8,
                           lambda kt, ci: (att[kt].ap(0, 128, ci * CW,
                                                      (ci + 1) * CW),
                                           [att[kt].res]),
                           evac_wo, layer_idx=l, tag=f"{tg}wo")
        for b in att:
            b.free()
        if dbg and self.stage == 204:
            self.early_out = [(b, 0, 128) for b in y]
            return

        h2_bf = []
        for kt in range(NKT):
            b = kb.ab.alloc(T, f"{tg}h2{kt}")
            eng = "scalar" if kt % 2 else "vector"
            if kt < 4:
                kb.copy(eng, b.ap(), y[kt].ap(), [y[kt].res], [b.res],
                        tag=f"{tg}h2c{kt}")
            else:
                kb.memset("vector", b.ap(0, 128, 0, 1), 0.0, [b.res],
                          tag=f"{tg}h2z{kt}")
                kb.copy(eng, b.ap(0, 128, 1, T),
                        y[kt].ap(0, 128, 0, T - 1), [y[kt].res], [b.res],
                        tag=f"{tg}h2s{kt}")
            h2_bf.append(b)
        lngb = kb.af.alloc(16, f"{tg}lngb")
        for j, nm in enumerate(("f_lng", "f_lnb")):
            dap, dres = kb.dram[nm]
            kb.dma("sync", kb.af.t[0:128, lngb.c0 + 8 * j:lngb.c0 + 8 * j + 8],
                   dap[l], [dres], [lngb.res], tag=f"{tg}lngb{j}")
        hn_bf = self.materialize_ln(
            lambda kt: (h2_bf[kt].ap(), [h2_bf[kt].res]),
            lambda kt: (h2_bf[kt], False),
            g_b=(lngb, 0, 8), tag=f"{tg}ln")
        for b in h2_bf:
            b.free()
        lngb.free()

        b1c = kb.af.alloc(48, f"{tg}b1c")
        dap, dres = kb.dram["f_b1"]
        kb.dma("sync", kb.af.t[0:128, b1c.c0:b1c.c0 + 44], dap[l], [dres],
               [b1c.res], tag=f"{tg}b1c")
        b2c = kb.af.alloc(16, f"{tg}b2c")
        dap, dres = kb.dram["f_b2"]
        kb.dma("sync", kb.af.t[0:128, b2c.c0:b2c.c0 + 8], dap[l], [dres],
               [b2c.res], tag=f"{tg}b2c")

        k_rows_w2 = [(i * 128, 128) for i in range(21)] + [(2688, 42)]
        TCW2 = 512
        for tc0 in range(0, T, TCW2):
            chunks = [(tc0, CW)]
            hid = [kb.ab.alloc(TCW2, f"{tg}hid{p}") for p in range(22)]
            atmp = [kb.ab.alloc(TCW2, f"{tg}at{p % 2}") for p in range(2)]

            def evac_w1(mi, ci, ps_ap, pres, t0=tc0, hid=hid, atmp=atmp):
                pair = mi // 2
                pw = 128 if pair < 21 else 42
                cc0 = ci * CW
                if mi % 2 == 0:
                    kb.act(atmp[pair % 2].ap(0, pw, cc0, cc0 + CW), ps_ap,
                           AF.Identity, [pres, b1c.res],
                           [atmp[pair % 2].res],
                           bias=b1c.ap(0, pw, mi, mi + 1), sreads=[b1c.res],
                           tag=f"{tg}w1a{mi}_{ci}")
                else:
                    gg = kb.ab.alloc(CW, f"{tg}gg")
                    kb.act(gg.ap(0, pw), ps_ap, AF.Gelu, [pres, b1c.res],
                           [gg.res], bias=b1c.ap(0, pw, mi, mi + 1),
                           sreads=[b1c.res], tag=f"{tg}w1g{mi}_{ci}")
                    kb.tt("vector", hid[pair].ap(0, pw, cc0, cc0 + CW),
                          atmp[pair % 2].ap(0, pw, cc0, cc0 + CW),
                          gg.ap(0, pw), ALU.mult,
                          [atmp[pair % 2].res, gg.res], [hid[pair].res],
                          tag=f"{tg}w1h{mi}_{ci}")
                    gg.free()

            pre_w2 = self.sm_prewarm("f_w2", 22, [128] * 8, layer_idx=l,
                                     tag=f"{tg}pw2_{tc0}", mb_cols=256,
                                     k_group=8, k_rows=k_rows_w2,
                                     chunks_len=1)
            self.stream_matmul("f_w1", NKT, m_widths_w1,
                               lambda kt, ci: (hn_bf[kt].ap(0, 128,
                                                            chunks[ci][0],
                                                            chunks[ci][0] + CW),
                                               [hn_bf[kt].res]),
                               evac_w1, layer_idx=l, chunks=chunks,
                               tag=f"{tg}w1", mb_cols=512, pre=pre_w1)
            if tc0 + TCW2 < T:
                pre_w1 = self.sm_prewarm("f_w1", NKT, m_widths_w1,
                                         layer_idx=l, tag=f"{tg}pw1_{tc0}",
                                         mb_cols=512, chunks_len=1)

            def evac_w2(mi, ci, ps_ap, pres, hid=hid):
                t0 = chunks[ci][0]
                kb.stt("vector", y[mi].ap(0, 128, t0, t0 + CW), ps_ap,
                       b2c.ap(0, 128, mi, mi + 1),
                       y[mi].ap(0, 128, t0, t0 + CW), ALU.add, ALU.add,
                       [pres, y[mi].res, b2c.res], [y[mi].res],
                       sreads=[b2c.res], tag=f"{tg}w2r{mi}_{ci}")

            self.stream_matmul("f_w2", 22, [128] * 8,
                               lambda kt, ci: (hid[kt].ap(0, k_rows_w2[kt][1],
                                                          ci * CW,
                                                          (ci + 1) * CW),
                                               [hid[kt].res]),
                               evac_w2, layer_idx=l, chunks=chunks,
                               k_rows=k_rows_w2, k_group=8, tag=f"{tg}w2",
                               mb_cols=256, pre=pre_w2)
            for b in hid:
                b.free()
            for b in atmp:
                b.free()
        for b in hn_bf:
            b.free()
        b1c.free()
        b2c.free()

    def local_attention(self, qr_t, kr_t, vtok, att, tg):
        kb = self.kb
        mle, mle_res = kb.cap("m_leb")
        mgt, mgt_res = kb.cap("m_gtb")
        mlt, mlt_res = kb.cap("m_ltb")
        mge, mge_res = kb.cap("m_geb")
        idb, idb_res = kb.cap("identb")
        fl_dap, fl_res = kb.dram["flip"]
        for h in range(8):
            ti = h // 4
            r0 = 32 * (h % 4)
            kaug = kb.ab.alloc(2560, f"{tg}ka{h}")
            kb.memset("vector", kaug.ap(0, 64, 0, 512), 0.0, [kaug.res],
                      tag=f"{tg}kz{h}")
            kb.memset("vector", kaug.ap(64, 65, 0, 2560), -1.0, [kaug.res],
                      tag=f"{tg}kn{h}")
            kb.copy("vector", kaug.ap(0, 32, 512, 2560),
                    kr_t[ti].ap(r0, r0 + 32, 0, T), [kr_t[ti].res],
                    [kaug.res], tag=f"{tg}kc1{h}")
            kb.copy("vector", kaug.ap(32, 64, 512, 2560),
                    kr_t[2 + ti].ap(r0, r0 + 32, 0, T), [kr_t[2 + ti].res],
                    [kaug.res], tag=f"{tg}kc2{h}")
            qaug = kb.ab.alloc(T, f"{tg}qa{h}")
            kb.copy("vector", qaug.ap(0, 32, 0, T),
                    qr_t[ti].ap(r0, r0 + 32, 0, T), [qr_t[ti].res],
                    [qaug.res], tag=f"{tg}qc1{h}")
            kb.copy("vector", qaug.ap(32, 64, 0, T),
                    qr_t[2 + ti].ap(r0, r0 + 32, 0, T), [qr_t[2 + ti].res],
                    [qaug.res], tag=f"{tg}qc2{h}")
            coll = kb.af.alloc(16, f"{tg}cl{h}")
            bkp = [kb.take_banks(2), kb.take_banks(2)]
            for qt in range(16):
                bks = bkp[qt % 2]
                qt_ap = qaug.ap(0, 64, qt * 128, (qt + 1) * 128)
                if qt >= 4:
                    kb.mm(kb.wide_ap(bks, 0, 128), qt_ap,
                          kaug.ap(0, 64, qt * 128, qt * 128 + 128),
                          True, False, [qaug.res, kaug.res],
                          [kb.pbank[bks[0]]], tag=f"{tg}sp{h}_{qt}a0")
                    kb.mm(kb.wide_ap(bks, 0, 128), idb, mle,
                          False, True, [idb_res, mle_res],
                          [kb.pbank[bks[0]]], tag=f"{tg}sm1{h}_{qt}")
                    kb.mm(kb.wide_ap(bks, 128, 512), qt_ap,
                          kaug.ap(0, 64, qt * 128 + 128, qt * 128 + 512),
                          True, True, [qaug.res, kaug.res],
                          [kb.pbank[bks[0]]], tag=f"{tg}sp{h}_{qt}a1")
                else:
                    kb.mm(kb.wide_ap(bks, 0, 512), qt_ap,
                          kaug.ap(0, 64, qt * 128, qt * 128 + 512),
                          True, True, [qaug.res, kaug.res],
                          [kb.pbank[bks[0]]], tag=f"{tg}sp{h}_{qt}a")
                kb.mm(kb.wide_ap(bks, 512, 640), qt_ap,
                      kaug.ap(0, 64, qt * 128 + 512, qt * 128 + 640),
                      True, False, [qaug.res, kaug.res],
                      [kb.pbank[bks[1]]], tag=f"{tg}sp{h}_{qt}b")
                kb.mm(kb.wide_ap(bks, 512, 640), idb, mgt,
                      False, True, [idb_res, mgt_res],
                      [kb.pbank[bks[1]]], tag=f"{tg}sm2{h}_{qt}")
                c0 = 128 * max(0, 4 - qt)
                kb.reduce("vector", coll.ap(0, 128, qt, qt + 1),
                          kb.wide_ap(bks, c0, 640), ALU.max,
                          [kb.pbank[bks[0]], kb.pbank[bks[1]]], [coll.res],
                          tag=f"{tg}rmx{h}_{qt}")
            bkf = kb.take_banks(1)[0]
            kb.transp(kb.bank_ap(bkf, 0, 128, 0, 16), coll.ap(0, 128, 0, 16),
                      [coll.res], [kb.pbank[bkf]], tag=f"{tg}tp{h}")
            ftmp = kb.af.alloc(128, f"{tg}ft{h}")
            kb.copy("vector", ftmp.ap(0, 16, 0, 128),
                    kb.bank_ap(bkf, 0, 128, 0, 16), [kb.pbank[bkf]],
                    [ftmp.res], tag=f"{tg}fc{h}")
            kb.dma("sync", fl_dap, ftmp.ap(0, 16, 0, 128), [ftmp.res],
                   [fl_res], tag=f"{tg}fd{h}")
            kb.dma("gpsimd", qaug.ap(64, 65, 0, T),
                   fl_dap.rearrange("a b -> (a b)"), [fl_res], [qaug.res],
                   tag=f"{tg}fl{h}")
            ftmp.free()
            coll.free()
            pt_bufs = [kb.ab.alloc(640, f"{tg}pt{j}") for j in range(7)]
            bsp = [kb.take_banks(2), kb.take_banks(2)]
            pvb2 = [kb.take_banks(1)[0], kb.take_banks(1)[0]]
            bcb = kb.take_banks(1)[0]
            rrs2 = [kb.af.alloc(128, f"{tg}rrs{h}{j}") for j in range(2)]
            pvs = kb.ab.alloc(128, f"{tg}pvs{h}")

            def do_pv(qt):
                pvb = pvb2[qt % 2]
                kts = list(range(max(0, qt - 4), qt + 1))
                n = len(kts)
                for j, kta in enumerate(kts):
                    src_pt = pt_bufs[(kta + 4) % 7]
                    qoff = 128 * (qt - kta)
                    kb.mm(kb.bank_ap(pvb, 0, 128, 0, 65),
                          vtok[h].ap(0, 128, kta * 65, kta * 65 + 65),
                          src_pt.ap(0, 128, qoff, qoff + 128),
                          j == 0, j == n - 1,
                          [vtok[h].res, src_pt.res], [kb.pbank[pvb]],
                          tag=f"{tg}pv{h}_{qt}_{kta}")
                self._pv_ln(pvb, rrs2[qt % 2], h, qt * 128, 128, tg)
                if qt >= 1:
                    self._pv_norm(pvb2[(qt - 1) % 2], bcb, rrs2[(qt - 1) % 2],
                                  pvs, att, h, (qt - 1) * 128, 128, tg)
                if qt == 15:
                    self._pv_norm(pvb2[15 % 2], bcb, rrs2[15 % 2], pvs, att,
                                  h, 15 * 128, 128, tg)

            for KT in range(4, 20):
                bse = bsp[KT % 2]
                qlo = 128 * (KT - 4)
                qhi = min(T, 128 * (KT + 1))
                W = qhi - qlo
                kt_ap = kaug.ap(0, 65, KT * 128, KT * 128 + 128)
                kb.mm(kb.wide_ap(bse, 0, 128), kt_ap,
                      qaug.ap(0, 65, qlo, qlo + 128),
                      True, False, [kaug.res, qaug.res],
                      [kb.pbank[bse[0]]], tag=f"{tg}st{h}_{KT}_0")
                kb.mm(kb.wide_ap(bse, 0, 128), idb, mlt,
                      False, True, [idb_res, mlt_res],
                      [kb.pbank[bse[0]]], tag=f"{tg}em1{h}_{KT}")
                if W > 128:
                    pw = min(W, 512)
                    kb.mm(kb.wide_ap(bse, 128, pw), kt_ap,
                          qaug.ap(0, 65, qlo + 128, qlo + pw),
                          True, True, [kaug.res, qaug.res],
                          [kb.pbank[bse[0]]], tag=f"{tg}st{h}_{KT}_1")
                if W == 640:
                    kb.mm(kb.wide_ap(bse, 512, 640), kt_ap,
                          qaug.ap(0, 65, qlo + 512, qlo + 640),
                          True, False, [kaug.res, qaug.res],
                          [kb.pbank[bse[1]]], tag=f"{tg}st{h}_{KT}_2")
                if W == 640:
                    kb.mm(kb.wide_ap(bse, 512, 640), idb, mge,
                          False, True, [idb_res, mge_res],
                          [kb.pbank[bse[1]]], tag=f"{tg}em2{h}_{KT}")
                pt = pt_bufs[KT % 7]
                kb.act(pt.ap(0, 128, 0, W), kb.wide_ap(bse, 0, W), AF.Exp,
                       [kb.pbank[bse[0]], kb.pbank[bse[1]]], [pt.res],
                       scale=8.0, tag=f"{tg}ex{h}_{KT}")
                if KT >= 5:
                    do_pv(KT - 5)
            do_pv(15)
            for b in rrs2:
                b.free()
            pvs.free()
            for b in pt_bufs:
                b.free()
            kaug.free()
            qaug.free()

    def _pv_ln(self, pvb, rrs, h, q0, qw, tg):
        kb = self.kb
        kb.act(rrs.ap(0, 1, 0, qw), kb.bank_ap(pvb, 0, qw, 64, 65), AF.Ln,
               [kb.pbank[pvb]], [rrs.res], tag=f"{tg}rl{h}_{q0}")
        kb.act(rrs.bap(0, 1, 2 * qw, 3 * qw), rrs.ap(0, 1, 0, qw), AF.Exp,
               [rrs.res], [rrs.res], scale=-1.0, tag=f"{tg}re{h}_{q0}")

    def _pv_norm(self, pvb, bcb, rrs, pvs, att, h, q0, qw, tg):
        kb = self.kb
        ones1b, o1b_res = kb.cap("ones1b")
        kb.mm(kb.bank_ap(bcb, 0, qw, 0, 64), ones1b[0:1, 0:64],
              rrs.bap(0, 1, 2 * qw, 3 * qw), True, True, [o1b_res, rrs.res],
              [kb.pbank[bcb]], tag=f"{tg}bc{h}_{q0}")
        kb.copy("vector", pvs.ap(0, 64, 0, qw),
                kb.bank_ap(pvb, 0, qw, 0, 64), [kb.pbank[pvb]], [pvs.res],
                tag=f"{tg}pe{h}_{q0}")
        r0 = 64 * (h % 2)
        kb.tt("vector", att[h // 2].ap(r0, r0 + 64, q0, q0 + qw),
              pvs.ap(0, 64, 0, qw), kb.bank_ap(bcb, 0, qw, 0, 64), ALU.mult,
              [pvs.res, kb.pbank[bcb]], [att[h // 2].res],
              tag=f"{tg}pn{h}_{q0}")


    def attn_b(self, x_bf, x3):
        kb = self.kb
        tg = "B"
        negm, rr = self.stats_rows(lambda kt: (x_bf[kt], False), tg + "st")
        r_b = self.bcast_row(rr, tg + "r")
        rr.free()
        negm_bf = kb.ab.alloc(T, tg + "nm")
        kb.copy("vector", negm_bf.ap(0, 1), negm.ap(0, 1), [negm.res],
                [negm_bf.res], tag=tg + "nmc")
        negm.free()

        ocol, ocol_res = kb.cap("ones_col")
        projs = {}
        for nm, mtot in (("b_dq", 256), ("b_dkv", 256), ("b_kr", 64)):
            dap, dres = kb.wbf[nm]
            slabs = []
            for kt in range(NKT):
                sb = kb.ab.alloc(mtot, f"{tg}{nm}{kt}")
                kb.dma(kb.wqueue(), sb.ap(0, 128, 0, mtot),
                       dap[kt * 128:(kt + 1) * 128, :], [dres], [sb.res],
                       tag=f"{tg}{nm}d{kt}")
                slabs.append(sb)
            wsum = kb.ab.alloc(mtot, f"{tg}{nm}ws")
            bk = kb.take_banks(1)[0]
            for kt in range(NKT):
                kb.mm(kb.bank_ap(bk, 0, mtot, 0, 1), ocol,
                      slabs[kt].ap(0, 128, 0, mtot), kt == 0, kt == NKT - 1,
                      [ocol_res, slabs[kt].res], [kb.pbank[bk]],
                      tag=f"{tg}{nm}ws{kt}")
            kb.copy("vector", wsum.ap(0, 1, 0, mtot),
                    kb.bank_ap(bk, 0, mtot, 0, 1), [kb.pbank[bk]],
                    [wsum.res], tag=f"{tg}{nm}wse")
            projs[nm] = (slabs, wsum)

        def run_proj(nm, mtot, evac):
            slabs, wsum = projs[nm]
            n_mt = (mtot + 127) // 128
            bks = kb.take_banks(2) + kb.take_banks(2)
            bi = 0
            for mi in range(n_mt):
                mw = min(128, mtot - mi * 128)
                for c in range(NCH):
                    bk = bks[bi % 4]
                    bi += 1
                    for kt in range(NKT):
                        kb.mm(kb.bank_ap(bk, 0, CW, 0, mw),
                              slabs[kt].ap(0, 128, mi * 128, mi * 128 + mw),
                              x_bf[kt].ap(0, 128, c * CW, (c + 1) * CW),
                              kt == 0, False,
                              [slabs[kt].res, x_bf[kt].res], [kb.pbank[bk]],
                              tag=f"{tg}{nm}m{mi}_{c}_{kt}")
                    kb.mm(kb.bank_ap(bk, 0, CW, 0, mw),
                          wsum.ap(0, 1, mi * 128, mi * 128 + mw),
                          negm_bf.ap(0, 1, c * CW, (c + 1) * CW), False, True,
                          [wsum.res, negm_bf.res], [kb.pbank[bk]],
                          tag=f"{tg}{nm}a{mi}_{c}")
                    evac(mi, c, kb.bank_ap(bk, 0, CW, 0, mw), kb.pbank[bk])

        cq = kb.ab.alloc(2 * T, tg + "cq")
        ckv = kb.ab.alloc(2 * T, tg + "ckv")
        krf = kb.af.alloc(T, tg + "krf")

        def ev_cq(mi, c, ps, pres):
            kb.tt("vector", cq.ap(0, 128, mi * T + c * CW, mi * T + (c + 1) * CW),
                  ps, r_b.ap(0, 128, c * CW, (c + 1) * CW), ALU.mult,
                  [pres, r_b.res], [cq.res], tag=f"{tg}cqe{mi}{c}")

        def ev_ckv(mi, c, ps, pres):
            kb.tt("vector", ckv.ap(0, 128, mi * T + c * CW, mi * T + (c + 1) * CW),
                  ps, r_b.ap(0, 128, c * CW, (c + 1) * CW), ALU.mult,
                  [pres, r_b.res], [ckv.res], tag=f"{tg}ckve{mi}{c}")

        def ev_kr(mi, c, ps, pres):
            kb.tt("vector", krf.ap(0, 64, c * CW, (c + 1) * CW), ps,
                  r_b.ap(0, 64, c * CW, (c + 1) * CW), ALU.mult,
                  [pres, r_b.res], [krf.res], tag=f"{tg}kre{c}")

        run_proj("b_dq", 256, ev_cq)
        run_proj("b_dkv", 256, ev_ckv)
        run_proj("b_kr", 64, ev_kr)
        for nm in ("b_dq", "b_dkv", "b_kr"):
            slabs, wsum = projs[nm]
            for sb in slabs:
                sb.free()
            wsum.free()
        negm_bf.free()
        r_b.free()
        for b in x_bf:
            b.free()

        cosb = kb.af.alloc(T, tg + "cos")
        sinb = kb.af.alloc(T, tg + "sin")
        bt_dap, bt_res = kb.dram["btab"]
        kb.dma("sync", cosb.ap(), bt_dap[0], [bt_res], [cosb.res],
               tag=tg + "tc")
        kb.dma("sync", sinb.ap(), bt_dap[1], [bt_res], [sinb.res],
               tag=tg + "ts")
        krb = kb.ab.alloc(T, tg + "krb")
        tmp = kb.af.alloc(T, tg + "krt")
        kb.tt("vector", tmp.ap(0, 16), krf.ap(32, 48), sinb.ap(32, 48),
              ALU.mult, [krf.res, sinb.res], [tmp.res], tag=tg + "kr1")
        kb.tt("vector", krb.ap(0, 16), krf.ap(0, 16), cosb.ap(0, 16),
              ALU.mult, [krf.res, cosb.res], [krb.res], tag=tg + "kr2")
        kb.tt("vector", krb.ap(0, 16), krb.ap(0, 16), tmp.ap(0, 16),
              ALU.subtract, [krb.res, tmp.res], [krb.res], tag=tg + "kr3")
        kb.tt("vector", tmp.ap(32, 48), krf.ap(0, 16), sinb.ap(0, 16),
              ALU.mult, [krf.res, sinb.res], [tmp.res], tag=tg + "kr4")
        kb.tt("vector", krb.ap(32, 48), krf.ap(32, 48), cosb.ap(32, 48),
              ALU.mult, [krf.res, cosb.res], [krb.res], tag=tg + "kr5")
        kb.tt("vector", krb.ap(32, 48), krb.ap(32, 48), tmp.ap(32, 48),
              ALU.add, [krb.res, tmp.res], [krb.res], tag=tg + "kr6")
        tmp.free()
        krf.free()

        sp2_ap, sp2_res = kb.dram["x_spill2"]
        for grp in range(2):
            att = [kb.ab.alloc(T, f"{tg}att{grp}{i}") for i in range(4)]
            pre_bwo = self.sm_prewarm("b_wo", 4, [128] * 8,
                                      dram_row0=grp * 512,
                                      tag=f"{tg}pwwo{grp}", chunks_len=4)
            self._attnb_group(grp, cq, ckv, krb, cosb, sinb, att, tg)
            xs = {}

            def evac_wo(mi, ci, ps_ap, pres, grp=grp):
                if ci == 0:
                    src_nm = "x_spill" if grp == 0 else "x_spill2"
                    xs[mi] = self.load_x_tile(src_nm, mi, f"{tg}xs{grp}{mi}")
                    x3[mi] = kb.af.alloc(T, f"{tg}x3_{grp}{mi}")
                kb.tt("vector", x3[mi].ap(0, 128, ci * CW, (ci + 1) * CW),
                      xs[mi].ap(0, 128, ci * CW, (ci + 1) * CW), ps_ap,
                      ALU.add, [xs[mi].res, pres], [x3[mi].res],
                      tag=f"{tg}wor{grp}{mi}_{ci}")
                if ci == NCH - 1:
                    xs[mi].free()
                    if grp == 0:
                        kb.dma("sync", sp2_ap[mi * 128:(mi + 1) * 128, :],
                               x3[mi].ap(), [x3[mi].res], [sp2_res],
                               tag=f"{tg}sp2_{mi}")
                        x3[mi].free()
                        x3[mi] = None

            self.stream_matmul("b_wo", 4, [128] * 8,
                               lambda kt, ci: (att[kt].ap(0, 128, ci * CW,
                                                          (ci + 1) * CW),
                                               [att[kt].res]),
                               evac_wo, dram_row0=grp * 512,
                               tag=f"{tg}wo{grp}", pre=pre_bwo)
            for b in att:
                b.free()
        cosb.free()
        sinb.free()
        krb.free()
        cq.free()
        ckv.free()

    def _attnb_group(self, grp, cq, ckv, krb, cosb, sinb, att, tg):
        kb = self.kb
        SCL = 1.0 / math.sqrt(96.0)
        CHI = math.exp(80.0 / math.sqrt(96.0))
        CLO = math.exp(-80.0 / math.sqrt(96.0))
        m01, m01_res = kb.cap("m01_ge")
        qasm = [kb.af.alloc(T // 2, f"{tg}qa{grp}{i}") for i in range(8)]
        kasm = [kb.af.alloc(T // 2, f"{tg}ka{grp}{i}") for i in range(8)]

        uq_dap, uq_res = kb.wbf["b_uq"]
        uk_dap, uk_res = kb.wbf["b_uk"]
        uv_dap, uv_res = kb.wbf["b_uv"]
        qr_dap, qr_res = kb.wbf["b_qr"]

        def up_proj(dap, dres, mcol0, n_mt, evac, tagp, src):
            slabs = []
            for kt in range(2):
                sb = kb.ab.alloc(n_mt * 128, f"{tagp}w{kt}")
                kb.dma(kb.wqueue(), sb.ap(0, 128, 0, n_mt * 128),
                       dap[kt * 128:(kt + 1) * 128,
                           mcol0:mcol0 + n_mt * 128],
                       [dres], [sb.res], tag=f"{tagp}d{kt}")
                slabs.append(sb)
            bks = kb.take_banks(2) + kb.take_banks(2)
            bi = 0
            for mi in range(n_mt):
                for c in range(NCH):
                    bk = bks[bi % 4]
                    bi += 1
                    for kt in range(2):
                        kb.mm(kb.bank_ap(bk),
                              slabs[kt].ap(0, 128, mi * 128, (mi + 1) * 128),
                              src.ap(0, 128, kt * T + c * CW,
                                     kt * T + (c + 1) * CW),
                              kt == 0, kt == 1,
                              [slabs[kt].res, src.res], [kb.pbank[bk]],
                              tag=f"{tagp}m{mi}{c}{kt}")
                    evac(mi, c, bk, kb.pbank[bk])
            for sb in slabs:
                sb.free()

        def ev_uq2(mi, c, bk, pres):
            for hh in range(2):
                a = qasm[2 * mi + hh]
                kb.copy("scalar", a.bap(0, 64, c * CW, (c + 1) * CW),
                        kb.bank_ap(bk, 0, CW, 64 * hh, 64 * hh + 64),
                        [pres], [a.res], tag=f"{tg}uqe{grp}{mi}{c}{hh}")

        def ev_uk2(mi, c, bk, pres):
            for hh in range(2):
                a = kasm[2 * mi + hh]
                kb.copy("scalar", a.bap(0, 64, c * CW, (c + 1) * CW),
                        kb.bank_ap(bk, 0, CW, 64 * hh, 64 * hh + 64),
                        [pres], [a.res], tag=f"{tg}uke{grp}{mi}{c}{hh}")

        up_proj(uq_dap, uq_res, grp * 512, 4, ev_uq2, f"{tg}uq{grp}", cq)
        up_proj(uk_dap, uk_res, grp * 512, 4, ev_uk2, f"{tg}uk{grp}", ckv)

        v_fm = [kb.ab.alloc(T, f"{tg}v{grp}{i}") for i in range(4)]

        def ev_uv(mi, c, bk, pres):
            kb.copy("scalar", v_fm[mi].ap(0, 128, c * CW, (c + 1) * CW),
                    kb.bank_ap(bk), [pres], [v_fm[mi].res],
                    tag=f"{tg}uve{grp}{mi}{c}")

        up_proj(uv_dap, uv_res, grp * 512, 4, ev_uv, f"{tg}uv{grp}", ckv)

        vtok = [kb.ab.alloc(16 * 65 + 16, f"{tg}vt{grp}{h}") for h in range(8)]
        for h in range(8):
            kb.memset("vector", vtok[h].ap(0, 128, 0, 16 * 65), 1.0,
                      [vtok[h].res], tag=f"{tg}vz{grp}{h}")
        bks = kb.take_banks(2)
        for i in range(4):
            for tc in range(16):
                bk = bks[tc % 2]
                kb.transp(kb.bank_bap(bk, 0, 128),
                          v_fm[i].ap(0, 128, tc * 128, (tc + 1) * 128),
                          [v_fm[i].res], [kb.pbank[bk]],
                          tag=f"{tg}vt{grp}{i}_{tc}", bf16=True)
                for hh in range(2):
                    h = 2 * i + hh
                    kb.copy("vector",
                            vtok[h].ap(0, 128, tc * 65, tc * 65 + 64),
                            kb.bank_bap(bk, hh * 64, hh * 64 + 64, 0, 128),
                            [kb.pbank[bk]], [vtok[h].res],
                            tag=f"{tg}ve{grp}{h}_{tc}")
        for b in v_fm:
            b.free()

        qr_top = kb.af.alloc(T, f"{tg}qrt{grp}")
        qr_bot = kb.af.alloc(T, f"{tg}qrb{grp}")

        def ev_qr(mi, c, bk, pres):
            dst = qr_top if mi == 0 else qr_bot
            kb.copy("scalar", dst.ap(0, 128, c * CW, (c + 1) * CW),
                    kb.bank_ap(bk), [pres], [dst.res],
                    tag=f"{tg}qre{grp}{mi}{c}")

        def up_proj_qr():
            slabs = []
            for kt in range(2):
                sb = kb.ab.alloc(256, f"{tg}qr{grp}w{kt}")
                for j, c0 in enumerate((grp * 128, 256 + grp * 128)):
                    kb.dma(kb.wqueue(), sb.ap(0, 128, j * 128, (j + 1) * 128),
                           qr_dap[kt * 128:(kt + 1) * 128, c0:c0 + 128],
                           [qr_res], [sb.res], tag=f"{tg}qrd{grp}{kt}{j}")
                slabs.append(sb)
            bks2 = kb.take_banks(2)
            for mi in range(2):
                for c in range(NCH):
                    bk = bks2[c % 2]
                    for kt in range(2):
                        kb.mm(kb.bank_ap(bk),
                              slabs[kt].ap(0, 128, mi * 128, (mi + 1) * 128),
                              cq.ap(0, 128, kt * T + c * CW,
                                    kt * T + (c + 1) * CW),
                              kt == 0, kt == 1, [slabs[kt].res, cq.res],
                              [kb.pbank[bk]], tag=f"{tg}qrm{grp}{mi}{c}{kt}")
                    ev_qr(mi, c, bk, kb.pbank[bk])
            for sb in slabs:
                sb.free()

        up_proj_qr()
        rt = kb.af.alloc(T, f"{tg}rt{grp}")
        rb_ = kb.af.alloc(T, f"{tg}rb{grp}")
        kb.tt("vector", rt.ap(), qr_top.ap(), cosb.ap(), ALU.mult,
              [qr_top.res, cosb.res], [rt.res], tag=f"{tg}ro1{grp}")
        kb.tt("vector", rb_.ap(), qr_top.ap(), sinb.ap(), ALU.mult,
              [qr_top.res, sinb.res], [rb_.res], tag=f"{tg}ro2{grp}")
        qr_top.free()
        t1 = kb.af.alloc(T, f"{tg}t1{grp}")
        kb.tt("vector", t1.ap(), qr_bot.ap(), sinb.ap(), ALU.mult,
              [qr_bot.res, sinb.res], [t1.res], tag=f"{tg}ro3{grp}")
        kb.tt("vector", rt.ap(), rt.ap(), t1.ap(), ALU.subtract,
              [rt.res, t1.res], [rt.res], tag=f"{tg}ro4{grp}")
        kb.tt("vector", t1.ap(), qr_bot.ap(), cosb.ap(), ALU.mult,
              [qr_bot.res, cosb.res], [t1.res], tag=f"{tg}ro5{grp}")
        kb.tt("vector", rb_.ap(), rb_.ap(), t1.ap(), ALU.add,
              [rb_.res, t1.res], [rb_.res], tag=f"{tg}ro6{grp}")
        t1.free()
        qr_bot.free()
        rt_bf = kb.ab.alloc(T, f"{tg}rtb{grp}")
        rb_bf = kb.ab.alloc(T, f"{tg}rbb{grp}")
        kb.copy("vector", rt_bf.ap(), rt.ap(), [rt.res], [rt_bf.res],
                tag=f"{tg}rtc{grp}")
        kb.copy("vector", rb_bf.ap(), rb_.ap(), [rb_.res], [rb_bf.res],
                tag=f"{tg}rbc{grp}")
        rt.free()
        rb_.free()
        for h in range(8):
            kb.dma("scalar", qasm[h].bap(64, 80, 0, T),
                   rt_bf.ap(h * 16, h * 16 + 16, 0, T), [rt_bf.res],
                   [qasm[h].res], tag=f"{tg}qat{grp}{h}")
            kb.dma("scalar", qasm[h].bap(80, 96, 0, T),
                   rb_bf.ap(h * 16, h * 16 + 16, 0, T), [rb_bf.res],
                   [qasm[h].res], tag=f"{tg}qab{grp}{h}")
            kb.dma("scalar", kasm[h].bap(64, 80, 0, T),
                   krb.ap(0, 16, 0, T), [krb.res], [kasm[h].res],
                   tag=f"{tg}kat{grp}{h}")
            kb.dma("scalar", kasm[h].bap(80, 96, 0, T),
                   krb.ap(32, 48, 0, T), [krb.res], [kasm[h].res],
                   tag=f"{tg}kab{grp}{h}")
        rt_bf.free()
        rb_bf.free()

        pbuf = [kb.ab.alloc(1024, f"{tg}p{grp}{j}") for j in range(3)]
        pvs = kb.ab.alloc(CW, f"{tg}pvs{grp}")
        rrs2 = [kb.af.alloc(CW, f"{tg}rrs{grp}{j}") for j in range(2)]
        ones1b_, o1b_res_ = kb.cap("ones1b")
        pending = []

        def flush_pending():
            while pending:
                fin = pending.pop(0)
                fin()

        pvb2 = [kb.take_banks(1)[0], kb.take_banks(1)[0]]
        sbk = kb.take_banks(2)
        bcb_ded = kb.take_banks(1)[0]
        for hh in range(8):
            h = 8 * grp + hh
            qa, ka = qasm[hh], kasm[hh]
            for qc in range(NCH):
                pvb = pvb2[qc % 2]
                n_kt = 4 * qc + 4

                def do_pv_b(kt, n_kt=n_kt, pvb=pvb):
                    pb = pbuf[kt % 3]
                    kb.mm(kb.bank_ap(pvb, 0, CW, 0, 65),
                          vtok[hh].ap(0, 128, kt * 65, kt * 65 + 65),
                          pb.ap(0, 128, 0, CW), kt == 0, kt == n_kt - 1,
                          [vtok[hh].res, pb.res], [kb.pbank[pvb]],
                          tag=f"{tg}pv{h}_{qc}_{kt}")

                for kt in range(n_kt):
                    bk = sbk[kt % 2]
                    kb.mm(kb.bank_ap(bk),
                          ka.bap(0, 96, kt * 128, (kt + 1) * 128),
                          qa.bap(0, 96, qc * CW, (qc + 1) * CW), True, True,
                          [ka.res, qa.res], [kb.pbank[bk]],
                          tag=f"{tg}st{h}_{qc}_{kt}")
                    if kt == 1:
                        flush_pending()
                    pb = pbuf[kt % 3]
                    kb.act(pb.ap(0, 128, 0, CW), kb.bank_ap(bk), AF.Exp,
                           [kb.pbank[bk]], [pb.res], scale=SCL,
                           tag=f"{tg}ex{h}_{qc}_{kt}")
                    kb.ts("gpsimd", pb.ap(0, 128, 0, CW),
                          pb.ap(0, 128, 0, CW), CHI, CLO, ALU.min, ALU.max,
                          [pb.res], [pb.res], tag=f"{tg}cl{h}_{qc}_{kt}")
                    dq0 = kt * 128 - qc * CW
                    if 0 <= dq0:
                        if dq0 > 0:
                            kb.memset("gpsimd", pb.ap(0, 128, 0, dq0), 0.0,
                                      [pb.res], tag=f"{tg}z{h}_{qc}_{kt}")
                        kb.tt("gpsimd", pb.ap(0, 128, dq0, dq0 + 128),
                              pb.ap(0, 128, dq0, dq0 + 128), m01, ALU.mult,
                              [pb.res, m01_res], [pb.res],
                              tag=f"{tg}dm{h}_{qc}_{kt}")
                    if kt >= 2:
                        do_pv_b(kt - 2)
                do_pv_b(n_kt - 2)
                do_pv_b(n_kt - 1)

                rrs = rrs2[qc % 2]
                kb.act(rrs.ap(0, 1, 0, CW),
                       kb.bank_ap(pvb, 0, CW, 64, 65), AF.Ln,
                       [kb.pbank[pvb]], [rrs.res], tag=f"{tg}rl{h}_{qc}")
                kb.act(rrs.bap(0, 1, 2 * CW, 3 * CW), rrs.ap(0, 1, 0, CW),
                       AF.Exp, [rrs.res], [rrs.res], scale=-1.0,
                       tag=f"{tg}re{h}_{qc}")

                def finish(pvb=pvb, qc=qc, hh=hh, h=h, rrs=rrs):
                    bcb = bcb_ded
                    kb.mm(kb.bank_ap(bcb, 0, CW, 0, 64), ones1b_[0:1, 0:64],
                          rrs.bap(0, 1, 2 * CW, 3 * CW), True, True,
                          [o1b_res_, rrs.res], [kb.pbank[bcb]],
                          tag=f"{tg}bc{h}_{qc}")
                    kb.copy("vector", pvs.ap(0, 64, 0, CW),
                            kb.bank_ap(pvb, 0, CW, 0, 64), [kb.pbank[pvb]],
                            [pvs.res], tag=f"{tg}pe{h}_{qc}")
                    r0 = 64 * (hh % 2)
                    kb.tt("vector", att[hh // 2].ap(r0, r0 + 64, qc * CW,
                                                    (qc + 1) * CW),
                          pvs.ap(0, 64, 0, CW), kb.bank_ap(bcb, 0, CW, 0, 64),
                          ALU.mult, [pvs.res, kb.pbank[bcb]],
                          [att[hh // 2].res], tag=f"{tg}pn{h}_{qc}")
                pending.append(finish)
        flush_pending()
        pvs.free()
        for b in rrs2:
            b.free()
        for b in pbuf:
            b.free()
        for b in qasm + kasm + vtok:
            b.free()



def build_full(weights_shapes_only, stage=99):
    consts = host_constants()
    kb = KB(weights_shapes_only, consts)
    nc = kb.nc
    ft = nc.dram_tensor("ftab", [4, 128, T], BF16, kind="Internal")
    kb.dram["ftab"] = (ft.ap(), Res("ftab"))
    bt = nc.dram_tensor("btab", [2, 128, T], F32, kind="Internal")
    kb.dram["btab"] = (bt.ap(), Res("btab"))
    m = Model(kb)
    kb.make_bf16_twins(["a_inproj", "a_gate", "a_outproj", "a_coupling_bd",
                        "a_w1", "a_w2", "f_wqkv", "f_wo", "f_w1", "f_w2",
                        "b_dq", "b_dkv", "b_kr", "b_uq", "b_uk", "b_uv",
                        "b_qr", "b_wo"], weights_shapes_only)

    xout_ap, xout_res = kb.dram["x_out"]
    spill_ap, spill_res = kb.dram["x_spill"]

    def store_out(bufs):
        for kt in range(NKT):
            kb.dma("sync", xout_ap[kt * 128:(kt + 1) * 128, :], bufs[kt].ap(),
                   [bufs[kt].res], [xout_res], tag=f"out{kt}")

    xls = [m.load_x_tile("x_in", kt, f"txa{kt}") for kt in range(NKT)]
    negm, rr = m.stats_rows(lambda kt: (xls[kt], False), "tln0", f32=True)
    negm_b = m.bcast_row(negm, "tl0m")
    r_b = m.bcast_row(rr, "tl0r")
    negm.free()
    rr.free()
    xa = []
    for kt in range(NKT):
        b = kb.af.alloc(T, f"xa{kt}")
        kb.tt("vector", b.ap(), xls[kt].ap(), negm_b.ap(), ALU.add,
              [xls[kt].res, negm_b.res], [b.res], tag=f"xa_a{kt}")
        kb.tt("vector", b.ap(), b.ap(), r_b.ap(), ALU.mult,
              [b.res, r_b.res], [b.res], tag=f"xa_m{kt}")
        xls[kt].free()
        xa.append(b)
    negm_b.free()
    r_b.free()
    if stage == 0:
        store_out(xa)
        return kb, m
    m.stage = stage
    xa_bf = [m.cast_tile(xa[kt], f"xabf{kt}") for kt in range(NKT)]
    xa = m.attn_a(xa, xa_bf)
    if m.early_out is not None:
        for i, (b, p0, p1) in enumerate(m.early_out[:NKT]):
            if b.arena is kb.ab:
                f = kb.af.alloc(b.w, f"eo{i}")
                kb.copy("vector", f.ap(p0, p1, 0, b.w), b.ap(p0, p1, 0, b.w),
                        [b.res], [f.res], tag=f"eo{i}")
                kb.dma("sync", xout_ap[i * 128:(i + 1) * 128, 0:min(f.w, T)],
                       f.ap(0, 128, 0, min(f.w, T)), [f.res], [xout_res],
                       tag=f"eout{i}")
                f.free()
            else:
                kb.dma("sync", xout_ap[i * 128:(i + 1) * 128, 0:min(b.w, T)],
                       b.ap(0, 128, 0, min(b.w, T)), [b.res], [xout_res],
                       tag=f"eout{i}")
        return kb, m

    x1_bf = []
    for w0 in range(0, NKT, 4):
        xw = [m.load_x_tile("x_in", kt, f"x1l{kt}")
              for kt in range(w0, w0 + 4)]
        for j, kt in enumerate(range(w0, w0 + 4)):
            xl = xw[j]
            kb.tt("vector", xa[kt].ap(), xl.ap(), xa[kt].ap(), ALU.add,
                  [xl.res, xa[kt].res], [xa[kt].res], tag=f"x1a{kt}")
            xl.free()
            kb.dma("sync", spill_ap[kt * 128:(kt + 1) * 128, :], xa[kt].ap(),
                   [xa[kt].res], [spill_res], tag=f"sp1_{kt}")
            x1_bf.append(m.cast_tile(xa[kt], f"x1b{kt}",
                                     eng="scalar" if kt % 2 else "vector"))
    if stage == 1:
        store_out(xa)
        return kb, m

    m.build_ffn_tables()
    m.build_attnb_tables()

    negm, rr = m.stats_rows(lambda kt: (xa[kt], False), "tln1", f32=True)
    negm_b = m.bcast_row(negm, "tl1m")
    r_b = m.bcast_row(rr, "tl1r")
    negm.free()
    rr.free()
    for b in x1_bf:
        b.free()
    y = []
    for kt in range(NKT):
        b = kb.af.alloc(T, f"y{kt}")
        kb.tt("vector", b.ap(), xa[kt].ap(), negm_b.ap(), ALU.add,
              [xa[kt].res, negm_b.res], [b.res], tag=f"y_a{kt}")
        kb.tt("vector", b.ap(), b.ap(), r_b.ap(), ALU.mult,
              [b.res, r_b.res], [b.res], tag=f"y_m{kt}")
        xa[kt].free()
        y.append(b)
    negm_b.free()
    r_b.free()
    m.ffn_layer(y, 0, 0)
    if m.early_out is not None:
        for i, (b, p0, p1) in enumerate(m.early_out[:NKT]):
            if b.arena is kb.ab:
                f = kb.af.alloc(b.w, f"feo{i}")
                kb.copy("vector", f.ap(p0, p1, 0, b.w), b.ap(p0, p1, 0, b.w),
                        [b.res], [f.res], tag=f"feo{i}")
                kb.dma("sync", xout_ap[i * 128:(i + 1) * 128, 0:min(f.w, T)],
                       f.ap(0, 128, 0, min(f.w, T)), [f.res], [xout_res],
                       tag=f"feout{i}")
                f.free()
            else:
                kb.dma("sync", xout_ap[i * 128:(i + 1) * 128, 0:min(b.w, T)],
                       b.ap(0, 128, 0, min(b.w, T)), [b.res], [xout_res],
                       tag=f"feout{i}")
        return kb, m
    m.ffn_layer(y, 1, 0)

    x2_bf = []
    for w0 in range(0, NKT, 4):
        xw = [m.load_x_tile("x_spill", kt, f"x2l{kt}")
              for kt in range(w0, w0 + 4)]
        for j, kt in enumerate(range(w0, w0 + 4)):
            xl = xw[j]
            kb.tt("vector", y[kt].ap(), xl.ap(), y[kt].ap(), ALU.add,
                  [xl.res, y[kt].res], [y[kt].res], tag=f"x2a{kt}")
            xl.free()
            kb.dma("sync", spill_ap[kt * 128:(kt + 1) * 128, :], y[kt].ap(),
                   [y[kt].res], [spill_res], tag=f"sp2_{kt}")
            x2_bf.append(m.cast_tile(y[kt], f"x2b{kt}",
                                         eng="scalar" if kt % 2 else "vector"))
    if stage == 2:
        store_out(y)
        return kb, m
    for b in y:
        b.free()

    x3 = [None] * NKT
    m.attn_b(x2_bf, x3)
    x3_bf = []
    for kt in range(NKT):
        kb.dma("sync", spill_ap[kt * 128:(kt + 1) * 128, :], x3[kt].ap(),
               [x3[kt].res], [spill_res], tag=f"sp3_{kt}")
        x3_bf.append(m.cast_tile(x3[kt], f"x3b{kt}",
                                     eng="scalar" if kt % 2 else "vector"))
    if stage == 3:
        store_out(x3)
        return kb, m

    negm, rr = m.stats_rows(lambda kt: (x3[kt], False), "tln3", f32=True)
    negm_b = m.bcast_row(negm, "tl3m")
    r_b = m.bcast_row(rr, "tl3r")
    negm.free()
    rr.free()
    for b in x3_bf:
        b.free()
    y2 = []
    for kt in range(NKT):
        b = kb.af.alloc(T, f"y2_{kt}")
        kb.tt("vector", b.ap(), x3[kt].ap(), negm_b.ap(), ALU.add,
              [x3[kt].res, negm_b.res], [b.res], tag=f"y2a{kt}")
        kb.tt("vector", b.ap(), b.ap(), r_b.ap(), ALU.mult,
              [b.res, r_b.res], [b.res], tag=f"y2m{kt}")
        x3[kt].free()
        y2.append(b)
    negm_b.free()
    r_b.free()
    m.ffn_layer(y2, 0, 1)
    m.ffn_layer(y2, 1, 1)

    for w0 in range(0, NKT, 4):
        xw = [m.load_x_tile("x_spill", kt, f"fo{kt}")
              for kt in range(w0, w0 + 4)]
        for j, kt in enumerate(range(w0, w0 + 4)):
            xl = xw[j]
            kb.tt("vector", y2[kt].ap(), xl.ap(), y2[kt].ap(), ALU.add,
                  [xl.res, y2[kt].res], [y2[kt].res], tag=f"fo_a{kt}")
            xl.free()
    store_out(y2)
    return kb, m


_CACHE = {}


def _alloc_sems(kb):
    sems = {}
    st = kb.st
    for e in COMPUTE_ENGS:
        sems[("c", e)] = st.enter_context(kb.nc.semaphore("sc_" + e))
        if kb.s.dseq[e]:
            for i in range(min(kb.s.dseq[e], DMA_POOL)):
                sems[("q", e, i)] = st.enter_context(
                    kb.nc.semaphore(f"sq_{e}{i}"))
    return sems


def _get_program(stage):
    if stage not in _CACHE:
        dummy_shapes = None
        raise RuntimeError("use _build_with_weights")
    return _CACHE[stage]


def kernel(**inputs):
    stage = DEV_STAGE
    w = prepare_weights(inputs)
    key = ("prog", stage)
    if key not in _CACHE:
        kb, m = build_full(w, stage)
        sems = _alloc_sems(kb)
        with kb.nc.Block() as block:
            kb.s.emit(kb.nc, block, sems)
        _CACHE[key] = kb.nc
    nc = _CACHE[key]

    x = np.asarray(inputs["L0_attn_a_x"], dtype=np.float32)
    in_maps = []
    for c in range(NCORES):
        im = {"x_in": np.ascontiguousarray(x[c].T)}
        for name, arr in w.items():
            im["w_" + name] = arr
        in_maps.append(im)
    res = run_bass_kernel_spmd(nc, in_maps, core_ids=list(range(NCORES)))
    out = np.stack([np.asarray(r["x_out"]).T for r in res.results])
    return np.ascontiguousarray(out.astype(np.float32))


# version 23
# speedup vs baseline: 1.0110x; 1.0110x over previous
import contextlib
import math

import numpy as np
import ml_dtypes

import concourse.bass as bass
import concourse.mybir as mybir
from concourse.bass_utils import run_bass_kernel_spmd

F32 = mybir.dt.float32
BF16 = mybir.dt.bfloat16
AF = mybir.ActivationFunctionType
ALU = mybir.AluOpType
AX = mybir.AxisListType

NCORES = 8
T = 2048
D = 1024
NKT = 8
NCH = 4
CW = 512

NEG = -1.0e6
LN_EPS = 1e-5
SIM_SAFE = False


COMPUTE_ENGS = ("tensor", "vector", "scalar", "gpsimd", "sync")
DMA_POOL = 10


class Res:
    __slots__ = ("name", "w", "rs", "init_deps")

    def __init__(self, name="r"):
        self.name = name
        self.w = None
        self.rs = []
        self.init_deps = ()


class _Op:
    __slots__ = ("eng", "fn", "waits", "inc", "is_dma", "ev", "tag",
                 "dma_slot", "dma_epoch", "always_sem")

    def __init__(self, eng, fn, is_dma, tag):
        self.eng = eng
        self.fn = fn
        self.waits = []
        self.inc = False
        self.is_dma = is_dma
        self.ev = None
        self.tag = tag
        self.dma_slot = 0
        self.dma_epoch = 0
        self.always_sem = False


class Sched:
    def __init__(self):
        self.ops = {e: [] for e in COMPUTE_ENGS}
        self.dseq = {e: 0 for e in COMPUTE_ENGS}
        self.all_ops = []

    def op(self, eng, fn, reads=(), writes=(), sreads=(), tag=""):
        o = _Op(eng, fn, False, tag)
        idx = len(self.ops[eng])
        ev = ("c", eng, idx, o)
        o.ev = ev
        self._edges(o, ev, reads, writes, sreads)
        self.ops[eng].append(o)
        self.all_ops.append(o)
        return o

    def dma(self, eng, fn, reads=(), writes=(), tag=""):
        o = _Op(eng, fn, True, tag)
        idx0 = self.dseq[eng]
        self.dseq[eng] += 1
        o.dma_slot = idx0 % DMA_POOL
        o.dma_epoch = idx0 // DMA_POOL + 1
        ev = ("d", eng, idx0, o)
        o.ev = ev
        self._edges(o, ev, reads, writes, ())
        self.ops[eng].append(o)
        self.all_ops.append(o)
        return o

    @staticmethod
    def _need_wait(my_eng, ev, forced):
        kind, eng, idx, op = ev
        if (kind == "c" and eng == my_eng and not forced and not SIM_SAFE
                and not op.always_sem):
            return False
        return True

    def _edges(self, o, ev, reads, writes, sreads):
        for r in reads:
            if r.w is not None and self._need_wait(o.eng, r.w, False):
                o.waits.append(r.w)
            r.rs.append(ev)
        for r in sreads:
            if r.w is not None and self._need_wait(o.eng, r.w, True):
                o.waits.append(r.w)
            r.rs.append(ev)
        for r in writes:
            if r.w is not None and self._need_wait(o.eng, r.w, False):
                o.waits.append(r.w)
            for rev in r.rs:
                if rev is not ev and self._need_wait(o.eng, rev, False):
                    o.waits.append(rev)
            for rev in r.init_deps:
                if rev is not ev and self._need_wait(o.eng, rev, False):
                    o.waits.append(rev)
            r.init_deps = ()
            r.w = ev
            r.rs = []

    def emit(self, nc, block, sems, final_waits=True):
        for o in self.all_ops:
            for ev in o.waits:
                kind, eng, idx, op = ev
                if kind == "c":
                    op.inc = True
        inc_at = {}
        for e in COMPUTE_ENGS:
            c = 0
            ci = 0
            for o in self.ops[e]:
                if not o.is_dma:
                    if o.inc:
                        c += 1
                    inc_at[(e, ci)] = c
                    ci += 1
        cidx_of = {}
        for e in COMPUTE_ENGS:
            ci = 0
            for i, o in enumerate(self.ops[e]):
                if not o.is_dma:
                    cidx_of[id(o)] = ci
                    ci += 1

        def run_stream(eng_obj, e):
            watermark = {}

            def do_wait(key, sem, val):
                if watermark.get(key, -1) < val:
                    eng_obj.wait_ge(sem, val)
                    watermark[key] = val

            for o in self.ops[e]:
                for ev in o.waits:
                    kind, peng, pidx, pop = ev
                    if kind == "c":
                        sem = sems[("c", peng)]
                        val = inc_at[(peng, cidx_of[id(pop)])]
                        do_wait(("c", peng), sem, val)
                    else:
                        sem = sems[("q", peng, pop.dma_slot)]
                        do_wait(("q", peng, pop.dma_slot), sem,
                                16 * pop.dma_epoch)
                if o.is_dma:
                    if o.dma_epoch > 1:
                        do_wait(("q", e, o.dma_slot),
                                sems[("q", e, o.dma_slot)],
                                16 * (o.dma_epoch - 1))
                    ins = o.fn(eng_obj)
                    ins.then_inc(sems[("q", e, o.dma_slot)], 16)
                else:
                    ins = o.fn(eng_obj)
                    if o.inc:
                        ins.then_inc(sems[("c", e)], 1)

        def run_sync(s):
            run_stream(s, "sync")
            if final_waits:
                for e in COMPUTE_ENGS:
                    n = self.dseq[e]
                    for slot in range(min(n, DMA_POOL)):
                        last_epoch = (n - 1 - slot) // DMA_POOL + 1
                        s.wait_ge(sems[("q", e, slot)], 16 * last_epoch)

        block.tensor(lambda t: run_stream(t, "tensor"))
        block.vector(lambda v: run_stream(v, "vector"))
        block.scalar(lambda s: run_stream(s, "scalar"))
        block.gpsimd(lambda g: run_stream(g, "gpsimd"))
        block.sync(run_sync)



class Buf:
    __slots__ = ("arena", "c0", "w", "res")

    def __init__(self, arena, c0, w, res):
        self.arena = arena
        self.c0 = c0
        self.w = w
        self.res = res

    def ap(self, p0=0, p1=128, c0=0, c1=None):
        if c1 is None:
            c1 = self.w
        assert 0 <= c0 < c1 <= self.w
        return self.arena.t[p0:p1, self.c0 + c0:self.c0 + c1]

    def bap(self, p0=0, p1=128, cb0=0, cb1=None):
        if cb1 is None:
            cb1 = 2 * self.w
        assert cb0 % 2 == 0 and cb1 % 2 == 0
        return self.arena.t[p0:p1,
                            self.c0 + cb0 // 2:self.c0 + cb1 // 2].bitcast(BF16)

    def free(self):
        self.arena.free(self)


class Arena:
    GRAN = 16

    def __init__(self, tensor, cols, name):
        self.t = tensor
        self.cols = cols
        self.name = name
        self.free_list = [(0, cols)]
        self.live = {}
        self.hwm = 0
        self.ghosts = {}

    def alloc(self, w, tag=""):
        w = (w + 15) & ~15
        best = None
        for i, (c0, sz) in enumerate(self.free_list):
            if sz >= w and (best is None or sz < self.free_list[best][1]):
                best = i
        if best is None:
            raise RuntimeError(f"arena {self.name}: no space for {w} ({tag}); "
                               f"free={self.free_list}")
        c0, sz = self.free_list[best]
        self.free_list[best] = (c0 + w, sz - w)
        if self.free_list[best][1] == 0:
            del self.free_list[best]
        r = Res(f"{self.name}:{tag}@{c0}")
        deps = []
        for g in range(c0 // self.GRAN, (c0 + w) // self.GRAN):
            evs = self.ghosts.pop(g, None)
            if evs:
                deps.extend(evs)
        if deps:
            seen = set()
            uniq = []
            for e in deps:
                if id(e) not in seen:
                    seen.add(id(e))
                    uniq.append(e)
            r.init_deps = tuple(uniq)
        self.live[c0] = (w, r)
        self.hwm = max(self.hwm, max((a + b for a, b in
                                      [(k, v[0]) for k, v in self.live.items()]),
                                     default=0))
        return Buf(self, c0, w, r)

    def free(self, buf):
        w, r = self.live.pop(buf.c0)
        evs = tuple(([r.w] if r.w is not None else []) + r.rs +
                    list(r.init_deps))
        for g in range(buf.c0 // self.GRAN, (buf.c0 + w) // self.GRAN):
            self.ghosts[g] = evs
        self.free_list.append((buf.c0, w))
        self.free_list.sort()
        out = []
        for c0, sz in self.free_list:
            if out and out[-1][0] + out[-1][1] == c0:
                out[-1] = (out[-1][0], out[-1][1] + sz)
            else:
                out.append((c0, sz))
        self.free_list = out



def _perm_qk_cols():
    idx = np.zeros(512, np.int64)
    for half in range(2):
        for h in range(8):
            for i in range(32):
                idx[half * 256 + h * 32 + i] = h * 64 + half * 32 + i
    return idx


def _perm_w1_cols():
    idx = []
    for blk in range(21):
        idx.extend(range(128 * blk, 128 * blk + 128))
        idx.extend(range(2730 + 128 * blk, 2730 + 128 * blk + 128))
    idx.extend(range(2688, 2730))
    idx.extend(range(2730 + 2688, 5460))
    return np.array(idx, np.int64)


def _perm_rope_b(n_pairs, n_heads):
    idx = np.zeros(2 * n_pairs * n_heads, np.int64)
    for h in range(n_heads):
        for p in range(n_pairs):
            idx[h * n_pairs + p] = h * 2 * n_pairs + 2 * p
            idx[n_pairs * n_heads + h * n_pairs + p] = h * 2 * n_pairs + 2 * p + 1
    return idx


def prepare_weights(inp):
    C = lambda a: np.ascontiguousarray(np.asarray(a, dtype=np.float32))
    w = {}
    w["a_inproj"] = C(np.asarray(inp["L0_attn_a_in_proj_w"]).T)
    w["a_gate"] = C(np.asarray(inp["L0_attn_a_gate_w"]).T)
    w["a_outproj"] = C(np.asarray(inp["L0_attn_a_out_proj_w"]).T)
    w["a_w1"] = C(np.asarray(inp["L0_attn_a_mlp_w1"]).T)
    w["a_w2"] = C(np.asarray(inp["L0_attn_a_mlp_w2"]).T)
    cp = np.asarray(inp["L0_attn_a_coupling"])
    bd = np.zeros((64, 64), np.float32)
    for h in range(4):
        bd[16 * h:16 * h + 16, 16 * h:16 * h + 16] = cp[h].T
    w["a_coupling_bd"] = bd
    for k in ("log_decay", "frequency"):
        w["a_" + k] = C(np.asarray(inp["L0_attn_a_" + k]).reshape(1, -1))
    for k in ("gate_b", "n1_g", "n1_b", "n2_g", "n2_b", "mlp_b2"):
        w["a_" + k] = C(np.asarray(inp["L0_attn_a_" + k]).reshape(-1, 128).T)
    w["a_mlp_b1"] = C(np.asarray(inp["L0_attn_a_mlp_b1"]).reshape(-1, 128).T)
    w["b_dq"] = C(inp["L0_attn_b_W_DQ"])
    w["b_uq"] = C(inp["L0_attn_b_W_UQ"])
    w["b_qr"] = C(np.asarray(inp["L0_attn_b_W_QR"])[:, _perm_rope_b(16, 16)])
    w["b_dkv"] = C(inp["L0_attn_b_W_DKV"])
    w["b_uk"] = C(inp["L0_attn_b_W_UK"])
    w["b_uv"] = C(inp["L0_attn_b_W_UV"])
    kr = np.asarray(inp["L0_attn_b_W_KR"])
    krp = np.zeros((1024, 64), np.float32)
    krp[:, 0:16] = kr[:, 0::2]
    krp[:, 32:48] = kr[:, 1::2]
    w["b_kr"] = C(krp)
    w["b_wo"] = C(inp["L0_attn_b_W_O"])
    pqk = _perm_qk_cols()
    wqkv = np.asarray(inp["L0_ffn_wqkv"], dtype=np.float32)
    wq = wqkv[:, :, 0:512][:, :, pqk]
    wk = wqkv[:, :, 512:1024][:, :, pqk]
    wv = wqkv[:, :, 1024:1536]
    w["f_wqkv"] = C(np.concatenate([wq, wk, wv], axis=2))
    w["f_wo"] = C(inp["L0_ffn_wo"])
    p1 = _perm_w1_cols()
    w["f_w1"] = C(np.asarray(inp["L0_ffn_w1"])[:, :, p1])
    b1p = np.asarray(inp["L0_ffn_b1"])[:, p1]
    b1_pad = np.zeros((2, 44 * 128), np.float32)
    b1_pad[:, :42 * 128] = b1p[:, :42 * 128]
    b1_pad[:, 42 * 128:42 * 128 + 42] = b1p[:, 5376:5418]
    b1_pad[:, 43 * 128:43 * 128 + 42] = b1p[:, 5418:5460]
    w["f_b1"] = C(np.stack([b1_pad[l].reshape(-1, 128).T for l in range(2)]))
    w["f_w2"] = C(inp["L0_ffn_w2"])
    w["f_b2"] = C(np.stack([np.asarray(inp["L0_ffn_b2"])[l].reshape(-1, 128).T
                            for l in range(2)]))
    w["f_lng"] = C(np.stack([np.asarray(inp["L0_ffn_ln_g"])[l].reshape(-1, 128).T
                             for l in range(2)]))
    w["f_lnb"] = C(np.stack([np.asarray(inp["L0_ffn_ln_b"])[l].reshape(-1, 128).T
                             for l in range(2)]))
    qs = np.asarray(inp["L0_ffn_q_scale"], dtype=np.float32)
    ks = np.asarray(inp["L0_ffn_k_scale"], dtype=np.float32)

    def expand(s):
        out = np.zeros((2, 512), np.float32)
        for l in range(2):
            for half in range(2):
                for h in range(8):
                    for i in range(32):
                        out[l, half * 256 + h * 32 + i] = s[l, half * 32 + i]
        return out
    w["f_qs"] = C(np.stack([expand(qs)[l].reshape(-1, 128).T for l in range(2)]))
    w["f_ks"] = C(np.stack([expand(ks)[l].reshape(-1, 128).T for l in range(2)]))
    return w


def host_constants():
    c = {}
    c["ident"] = np.eye(128, dtype=np.float32)
    c["identb"] = np.eye(128).astype(ml_dtypes.bfloat16)
    onescol = np.zeros((128, 2), np.float32)
    onescol[:, 0] = 1.0 / D
    onescol[:, 1] = -1.0 / D
    c["stat_ones"] = onescol.astype(ml_dtypes.bfloat16)
    c["stat_onesf"] = onescol.astype(np.float32)
    c["ones2"] = np.ones((2, 128), dtype=ml_dtypes.bfloat16)
    c["ones1f"] = np.ones((1, 128), dtype=np.float32)
    c["ones1b"] = np.ones((1, 128), dtype=ml_dtypes.bfloat16)
    k = np.arange(128)[:, None]
    q = np.arange(128)[None, :]
    c["m_lt"] = np.where(q < k, np.float32(NEG), np.float32(0.0)).astype(np.float32)
    c["m_ge"] = np.where(q >= k, np.float32(NEG), np.float32(0.0)).astype(np.float32)
    c["m_le"] = np.where(q <= k, np.float32(NEG), np.float32(0.0)).astype(np.float32)
    c["m_gt"] = np.where(q > k, np.float32(NEG), np.float32(0.0)).astype(np.float32)
    c["m01_ge"] = (q >= k).astype(np.float32).astype(ml_dtypes.bfloat16)
    for nm in ("m_lt", "m_ge", "m_le", "m_gt"):
        c[nm + "b"] = c[nm].astype(ml_dtypes.bfloat16)
    c["ones_col"] = np.ones((128, 1), dtype=ml_dtypes.bfloat16)
    oh = np.zeros((4, 128, 8), np.float32)
    for kt in range(4):
        for r in range(128):
            rr = kt * 128 + r
            half = rr // 256
            h = (rr - half * 256) // 32
            oh[kt, r, h] = 1.0
    c["l2_onehot"] = oh.astype(ml_dtypes.bfloat16)
    ohT = np.transpose(oh, (0, 2, 1))
    c["l2_onehot_T"] = np.ascontiguousarray(ohT).astype(ml_dtypes.bfloat16)
    c["eps_row"] = np.full((1, 1), LN_EPS, np.float32)
    c["mod32_col"] = (np.arange(128, dtype=np.float32) % 32).reshape(128, 1)
    c["mod16_col"] = (np.arange(128, dtype=np.float32) % 16).reshape(128, 1)
    c["small"] = np.full((1, 1), 1e-12, np.float32)
    return c



class KB:
    def __init__(self, weights, consts):
        nc = bass.Bass("TRN2", target_bir_lowering=False, debug=False,
                       detect_race_conditions=SIM_SAFE)
        self.nc = nc
        self.s = Sched()
        self.dram = {}
        self.cst_dram = {}
        x = nc.dram_tensor("x_in", [D, T], F32, kind="ExternalInput")
        self.dram["x_in"] = (x.ap(), Res("x_in"))
        y = nc.dram_tensor("x_out", [D, T], F32, kind="ExternalOutput")
        self.dram["x_out"] = (y.ap(), Res("x_out"))
        for name, arr in weights.items():
            t = nc.dram_tensor("w_" + name, list(arr.shape), F32,
                               kind="ExternalInput")
            self.dram[name] = (t.ap(), Res(name))
        sp = nc.dram_tensor("x_spill", [D, T], F32, kind="Internal")
        self.dram["x_spill"] = (sp.ap(), Res("x_spill"))
        sp2 = nc.dram_tensor("x_spill2", [D, T], F32, kind="Internal")
        self.dram["x_spill2"] = (sp2.ap(), Res("x_spill2"))
        fl = nc.dram_tensor("flip_scratch", [16, 128], F32, kind="Internal")
        self.dram["flip"] = (fl.ap(), Res("flip"))
        for name, arr in consts.items():
            t = nc.inline_tensor(np.ascontiguousarray(arr), "c_" + name)
            self.cst_dram[name] = (t.ap(), Res("c_" + name))

        self.st = contextlib.ExitStack()
        st = self.st
        af_t = st.enter_context(nc.sbuf_tensor("arena_f32", [128, 30208], F32))
        ab_t = st.enter_context(nc.sbuf_tensor("arena_bf16", [128, 43008], BF16))
        self.af = Arena(af_t, 30208, "f32")
        self.ab = Arena(ab_t, 43008, "bf16")
        self.psA = st.enter_context(nc.psum_tensor("psA", [128, 2048], F32))
        self.psB = st.enter_context(nc.psum_tensor("psB", [128, 2048], F32))
        self.pbank = [Res(f"pb{i}") for i in range(8)]
        self._bank_rr = 0
        self.cst = {}
        self._load_consts(consts)
        self.tap_bufs = {}
        self.wbf = {}
        self._wq_rr = 0

    def wqueue(self):
        self._wq_rr += 1
        return "sync" if self._wq_rr % 2 else "scalar"

    def make_bf16_twins(self, names, weights):
        nc = self.nc
        for n in names:
            arr = weights[n]
            t = nc.dram_tensor("wbf_" + n, list(arr.shape), BF16,
                               kind="Internal")
            r = Res("wbf_" + n)
            dap, dres = self.dram[n]
            self.s.dma("gpsimd",
                       lambda e, o=t.ap(), i=dap: e.dma_start(out=o, in_=i),
                       reads=[dres], writes=[r], tag="wcast_" + n)
            self.wbf[n] = (t.ap(), r)

    def bank_ap(self, i, c0=0, c1=512, p0=0, p1=128):
        t = self.psA if i < 4 else self.psB
        base = (i % 4) * 512
        return t[p0:p1, base + c0:base + c1]

    def bank_bap(self, i, cb0=0, cb1=1024, p0=0, p1=128):
        t = self.psA if i < 4 else self.psB
        base = (i % 4) * 512
        return t[p0:p1, base + cb0 // 2:base + cb1 // 2].bitcast(BF16)

    def take_banks(self, n):
        for _ in range(8):
            s0 = self._bank_rr % 8
            if (s0 % 4) + n <= 4:
                self._bank_rr = (s0 + n) % 8
                return list(range(s0, s0 + n))
            self._bank_rr = ((s0 // 4) * 4 + 4) % 8
        raise RuntimeError("bank alloc")

    def wide_ap(self, banks, c0, c1, p0=0, p1=128):
        t = self.psA if banks[0] < 4 else self.psB
        base = (banks[0] % 4) * 512
        return t[p0:p1, base + c0:base + c1]

    def _load_consts(self, consts):
        nc, st = self.nc, self.st
        specs = {
            "ident": ([128, 128], F32),
            "identb": ([128, 128], BF16),
            "stat_ones": ([128, 2], BF16),
            "stat_onesf": ([128, 2], F32),
            "ones2": ([2, 128], BF16),
            "ones1f": ([1, 128], F32),
            "ones1b": ([1, 128], BF16),
            "m_ltb": ([128, 128], BF16),
            "m_geb": ([128, 128], BF16),
            "m_leb": ([128, 128], BF16),
            "m_gtb": ([128, 128], BF16),
            "m01_ge": ([128, 128], BF16),
            "ones_col": ([128, 1], BF16),
            "l2_onehot": ([4, 128, 8], BF16),
            "l2_onehot_T": ([4, 8, 128], BF16),
            "eps_row": ([1, 1], F32),
            "mod32_col": ([128, 1], F32),
            "mod16_col": ([128, 1], F32),
            "small": ([1, 1], F32),
        }
        for name, (shape, dt) in specs.items():
            if len(shape) == 3:
                tens = st.enter_context(
                    nc.sbuf_tensor("k_" + name, [shape[1], shape[0] * shape[2]], dt))
                r = Res("k_" + name)
                dap, dres = self.cst_dram[name]
                for i in range(shape[0]):
                    self.dma("sync", tens[:, i * shape[2]:(i + 1) * shape[2]],
                             dap[i], [dres], [r], tag=f"cst_{name}{i}")
                self.cst[name] = (tens, r, shape[2])
            else:
                tens = st.enter_context(nc.sbuf_tensor("k_" + name, shape, dt))
                r = Res("k_" + name)
                dap, dres = self.cst_dram[name]
                self.dma("sync", tens[:, :], dap, [dres], [r], tag="cst_" + name)
                self.cst[name] = (tens, r, None)

    def cap(self, name, i=None):
        tens, r, w = self.cst[name]
        if i is None:
            return tens[:, :], r
        return tens[:, i * w:(i + 1) * w], r

    def dma(self, eng, out_ap, in_ap, reads, writes, tag=""):
        return self.s.dma(eng, lambda e, o=out_ap, i=in_ap: e.dma_start(out=o, in_=i),
                          reads=reads, writes=writes, tag=tag)

    def mm(self, out_ap, lhsT_ap, rhs_ap, start, stop, reads, writes, tag=""):
        return self.s.op(
            "tensor",
            lambda t, o=out_ap, l=lhsT_ap, r=rhs_ap, a=start, b=stop:
                t.matmul(o, l, r, start=a, stop=b),
            reads=reads, writes=writes, tag=tag)

    def transp(self, out_ap, in_ap, reads, writes, tag="", bf16=False):
        ident, ires = self.cap("identb" if bf16 else "ident")
        nc = self.nc

        def fn(t, o=out_ap, i=in_ap, d=ident):
            if bf16:
                with nc.allow_low_precision("bf16 transpose, no accumulation"):
                    return t.transpose(o, i, d)
            return t.transpose(o, i, d)
        return self.s.op("tensor", fn, reads=list(reads) + [ires],
                         writes=writes, tag=tag)

    def act(self, out_ap, in_ap, func, reads, writes, bias=0.0, scale=1.0,
            sreads=(), accum_out=None, tag=""):
        def fn(s, o=out_ap, i=in_ap, f=func, b=bias, sc=scale, ao=accum_out):
            if ao is not None:
                return s.activation(o, i, f, bias=b, scale=sc, accum_out=ao)
            return s.activation(o, i, f, bias=b, scale=sc)
        return self.s.op("scalar", fn, reads=reads, writes=writes,
                         sreads=sreads, tag=tag)

    def tt(self, eng, out_ap, in0, in1, op, reads, writes, tag=""):
        return self.s.op(eng,
                         lambda e, o=out_ap, a=in0, b=in1, p=op:
                             e.tensor_tensor(o, a, b, p),
                         reads=reads, writes=writes, tag=tag)

    def ts(self, eng, out_ap, in0, s1, s2, op0, op1, reads, writes,
           sreads=(), tag=""):
        return self.s.op(eng,
                         lambda e, o=out_ap, a=in0, x=s1, y=s2, p0=op0, p1=op1:
                             e.tensor_scalar(o, a, x, y, p0, p1),
                         reads=reads, writes=writes, sreads=sreads, tag=tag)

    def stt(self, eng, out_ap, in0, scalar, in1, op0, op1, reads, writes,
            sreads=(), tag=""):
        return self.s.op(eng,
                         lambda e, o=out_ap, a=in0, sc=scalar, b=in1, p0=op0, p1=op1:
                             e.scalar_tensor_tensor(o, a, sc, b, p0, p1),
                         reads=reads, writes=writes, sreads=sreads, tag=tag)

    def scan(self, out_ap, d0, d1, init, reads, writes, tag=""):
        return self.s.op("vector",
                         lambda e, o=out_ap, a=d0, b=d1, i=init:
                             e.tensor_tensor_scan(o, a, b, i, ALU.mult, ALU.add),
                         reads=reads, writes=writes, tag=tag)

    def copy(self, eng, out_ap, in_ap, reads, writes, tag=""):
        if eng == "scalar":
            return self.s.op(eng, lambda e, o=out_ap, i=in_ap: e.copy(o, i),
                             reads=reads, writes=writes, tag=tag)
        return self.s.op(eng, lambda e, o=out_ap, i=in_ap: e.tensor_copy(o, i),
                         reads=reads, writes=writes, tag=tag)

    def memset(self, eng, ap, val, writes, tag=""):
        o = self.s.op(eng, lambda e, a=ap, v=val: e.memset(a, v),
                      writes=writes, tag=tag)
        o.always_sem = True
        return o

    def reduce(self, eng, out_ap, in_ap, op, reads, writes, negate=False, tag=""):
        return self.s.op(eng,
                         lambda e, o=out_ap, i=in_ap, p=op, n=negate:
                             e.tensor_reduce(o, i, AX.X, p, negate=n),
                         reads=reads, writes=writes, tag=tag)

    def recip(self, out_ap, in_ap, reads, writes, tag=""):
        return self.s.op("vector",
                         lambda e, o=out_ap, i=in_ap: e.reciprocal(o, i),
                         reads=reads, writes=writes, tag=tag)

    def iota(self, out_ap, pattern, base, cm, writes, tag=""):
        return self.s.op("gpsimd",
                         lambda e, o=out_ap, p=pattern, b=base, c=cm:
                             e.iota(o, p, b, c, allow_small_or_imprecise_dtypes=True),
                         writes=writes, tag=tag)

    def tap(self, name, src_bufs, width=T, p1=128):
        nc = self.nc
        n = len(src_bufs)
        t = nc.dram_tensor("tap_" + name, [n * 128, width], F32,
                           kind="ExternalOutput")
        r = Res("tap_" + name)
        for i, b in enumerate(src_bufs):
            tmp = self.af.alloc(width, "tapt")
            self.copy("vector", tmp.ap(0, p1, 0, width), b.ap(0, p1, 0, width),
                      [b.res], [tmp.res], tag="tapc")
            self.dma("sync", t.ap()[i * 128:i * 128 + p1, :],
                     tmp.ap(0, p1, 0, width), [tmp.res], [r], tag="tapd")
            tmp.free()



class Model:
    def __init__(self, kb: KB):
        self.kb = kb
        self.stage = 99
        self.early_out = None
        self.weights_np = None


    def load_param_cols(self, wname, n, tag, rows_total=None):
        kb = self.kb
        dap, dres = kb.dram[wname]
        total = rows_total if rows_total is not None else n * 128
        buf = kb.ab
        b = kb.af.alloc(n, tag)
        src = dap[0:1, 0:total].rearrange("o (j p) -> (o p) j", p=128)
        kb.dma("sync", b.ap(0, 128, 0, n), src, [dres], [b.res], tag=tag)
        return b

    def stats_rows(self, tile_fn, tag, f32=False):
        kb = self.kb
        so_ap, so_res = kb.cap("stat_onesf" if f32 else "stat_ones")
        banks = kb.take_banks(4)
        ar = kb.af if f32 else kb.ab
        x2 = [ar.alloc(T, f"x2_{tag}") for _ in range(2)]
        for kt in range(NKT):
            xb, owned = tile_fn(kt)
            x2b = x2[kt % 2]
            kb.tt("vector", x2b.ap(), xb.ap(), xb.ap(),
                  ALU.mult, [xb.res], [x2b.res], tag=f"sq_{tag}{kt}")
            for c in range(NCH):
                bk = banks[c]
                first = kt == 0
                last = kt == NKT - 1
                kb.mm(kb.bank_ap(bk, 0, CW, 0, 1), so_ap[:, 1:2],
                      xb.ap(0, 128, c * CW, (c + 1) * CW),
                      first, last,
                      [so_res, xb.res], [kb.pbank[bk]],
                      tag=f"st_m_{tag}")
                kb.s.op("tensor",
                        lambda t, o=kb.bank_ap(bk, 0, CW, 32, 33),
                        l=so_ap[:, 0:1],
                        r=x2b.ap(0, 128, c * CW, (c + 1) * CW), fst=first,
                        lst=last:
                            t.matmul(o, l, r, start=fst, stop=lst,
                                     tile_position=(0, 32)),
                        reads=[so_res, x2b.res], writes=[kb.pbank[bk]],
                        tag=f"st_v_{tag}")
            if owned:
                xb.free()
        negm = kb.af.alloc(T, f"negm_{tag}")
        ex2 = kb.af.alloc(T, f"ex2_{tag}")
        for c in range(NCH):
            bk = banks[c]
            kb.copy("scalar", negm.ap(0, 1, c * CW, (c + 1) * CW),
                    kb.bank_ap(bk, 0, CW, 0, 1), [kb.pbank[bk]], [negm.res],
                    tag=f"ev_m_{tag}")
            kb.copy("scalar", ex2.ap(0, 1, c * CW, (c + 1) * CW),
                    kb.bank_ap(bk, 0, CW, 32, 33), [kb.pbank[bk]], [ex2.res],
                    tag=f"ev_v_{tag}")
        for b in x2:
            b.free()
        msq = kb.af.alloc(T, f"msq_{tag}")
        kb.tt("vector", msq.ap(0, 1), negm.ap(0, 1), negm.ap(0, 1), ALU.mult,
              [negm.res], [msq.res], tag=f"msq_{tag}")
        var = msq
        kb.tt("vector", var.ap(0, 1), ex2.ap(0, 1), msq.ap(0, 1), ALU.subtract,
              [ex2.res, msq.res], [var.res], tag=f"var_{tag}")
        ex2.free()
        sd = kb.af.alloc(T, f"sd_{tag}")
        eps_ap, eps_res = kb.cap("eps_row")
        kb.act(sd.ap(0, 1), var.ap(0, 1), AF.Ln, [var.res], [sd.res],
               bias=eps_ap[0:1, 0:1], scale=1.0, sreads=[eps_res],
               tag=f"lnv_{tag}")
        var.free()
        rr = kb.af.alloc(T, f"r_{tag}")
        kb.act(rr.ap(0, 1), sd.ap(0, 1), AF.Exp, [sd.res], [rr.res],
               scale=-0.5, tag=f"rex_{tag}")
        sd.free()
        return negm, rr

    def bcast_row(self, row, tag, out_dtype_f32=True):
        kb = self.kb
        hilo = kb.ab.alloc(T * 2, f"hilo_{tag}")
        kb.copy("vector", hilo.ap(0, 1, 0, T), row.ap(0, 1), [row.res],
                [hilo.res], tag=f"hi_{tag}")
        kb.tt("vector", hilo.ap(0, 1, T, 2 * T), row.ap(0, 1),
              hilo.ap(0, 1, 0, T), ALU.subtract, [row.res, hilo.res],
              [hilo.res], tag=f"lo_{tag}")
        ones1b, o1res = kb.cap("ones1b")
        out = kb.af.alloc(T, f"bc_{tag}")
        banks = kb.take_banks(2)
        for c in range(NCH):
            bk = banks[c % 2]
            kb.mm(kb.bank_ap(bk), ones1b, hilo.ap(0, 1, c * CW, (c + 1) * CW),
                  True, False, [o1res, hilo.res], [kb.pbank[bk]],
                  tag=f"bcm_{tag}")
            kb.mm(kb.bank_ap(bk), ones1b,
                  hilo.ap(0, 1, T + c * CW, T + (c + 1) * CW),
                  False, True, [o1res, hilo.res], [kb.pbank[bk]],
                  tag=f"bcm2_{tag}")
            kb.copy("scalar", out.ap(0, 128, c * CW, (c + 1) * CW),
                    kb.bank_ap(bk), [kb.pbank[bk]], [out.res],
                    tag=f"bce_{tag}")
        hilo.free()
        return out

    def materialize_ln(self, src_f32_tile_fn, stats_tile_fn, g_b=None,
                       tag="ln", out_f32=False):
        kb = self.kb
        negm, rr = self.stats_rows(stats_tile_fn, tag)
        negm_b = self.bcast_row(negm, tag + "m")
        r_b = self.bcast_row(rr, tag + "r")
        negm.free()
        rr.free()
        outs = []
        outs32 = []
        tmp = kb.af.alloc(T, tag + "_t")
        for kt in range(NKT):
            sap, sres = src_f32_tile_fn(kt)
            kb.tt("vector", tmp.ap(), sap, negm_b.ap(), ALU.add,
                  sres + [negm_b.res], [tmp.res], tag=f"{tag}_a{kt}")
            if g_b is None:
                ob = kb.ab.alloc(T, tag + f"_o{kt}")
                kb.tt("vector", ob.ap(), tmp.ap(), r_b.ap(), ALU.mult,
                      [tmp.res, r_b.res], [ob.res], tag=f"{tag}_m{kt}")
                if out_f32:
                    o32 = kb.af.alloc(T, tag + f"_f{kt}")
                    kb.tt("scalar" if False else "vector", o32.ap(), tmp.ap(),
                          r_b.ap(), ALU.mult, [tmp.res, r_b.res], [o32.res],
                          tag=f"{tag}_f{kt}")
                    outs32.append(o32)
                outs.append(ob)
            else:
                gb_buf, g_col, b_col = g_b
                kb.tt("vector", tmp.ap(), tmp.ap(), r_b.ap(), ALU.mult,
                      [tmp.res, r_b.res], [tmp.res], tag=f"{tag}_m{kt}")
                ob = kb.ab.alloc(T, tag + f"_o{kt}")
                kb.ts("vector", ob.ap(), tmp.ap(),
                      gb_buf.ap(0, 128, g_col + kt, g_col + kt + 1),
                      gb_buf.ap(0, 128, b_col + kt, b_col + kt + 1),
                      ALU.mult, ALU.add, [tmp.res, gb_buf.res], [ob.res],
                      sreads=[gb_buf.res], tag=f"{tag}_gb{kt}")
                outs.append(ob)
        tmp.free()
        negm_b.free()
        r_b.free()
        if out_f32:
            return outs, outs32
        return outs

    def sm_prewarm(self, wname, n_kt, m_widths, dram_row0=0, dram_col0=0,
                   layer_idx=None, tag="pw", mb_cols=1024, k_group=None,
                   k_rows=None, chunks_len=4, nk_pre=None):
        kb = self.kb
        if wname in kb.wbf:
            dap, dres = kb.wbf[wname]
        else:
            dap, dres = kb.dram[wname]
        if layer_idx is not None:
            dap = dap[layer_idx]
        if k_rows is None:
            k_rows = [(kt * 128, 128) for kt in range(n_kt)]
        moff = [0]
        for w in m_widths:
            moff.append(moff[-1] + w)
        kg = k_group if k_group is not None else n_kt
        kgroups = [list(range(i, min(i + kg, n_kt)))
                   for i in range(0, n_kt, kg)]
        max_mt = len(m_widths)
        if len(kgroups) > 1:
            max_mt = max(1, 4 // chunks_len)
        cur, cw = [], 0
        for mi, w in enumerate(m_widths):
            if cur and (cw + w > mb_cols or len(cur) >= max_mt):
                break
            cur.append(mi)
            cw += w
        c0 = moff[cur[0]]
        c1 = moff[cur[-1] + 1]
        slabs = {}
        for kt in kgroups[0][:nk_pre] if nk_pre is not None else kgroups[0]:
            r0, nr = k_rows[kt]
            sb = kb.ab.alloc(c1 - c0, f"{tag}_w0k{kt}")
            kb.dma("gpsimd", sb.ap(0, nr, 0, c1 - c0),
                   dap[dram_row0 + r0:dram_row0 + r0 + nr,
                       dram_col0 + c0:dram_col0 + c1],
                   [dres], [sb.res], tag=f"{tag}_pwd{kt}")
            slabs[kt] = sb
        return (slabs, c0)

    def stream_matmul(self, wname, n_kt, m_widths, rhs_fn, evac_fn,
                      dram_row0=0, dram_col0=0, layer_idx=None, tag="sm",
                      aug=None, mb_cols=1024, k_group=None, k_rows=None,
                      chunks=None, pre=None):
        kb = self.kb
        if wname in kb.wbf:
            dap, dres = kb.wbf[wname]
        else:
            dap, dres = kb.dram[wname]
        if layer_idx is not None:
            dap = dap[layer_idx]
        if chunks is None:
            chunks = [(c * CW, CW) for c in range(NCH)]
        if k_rows is None:
            k_rows = [(kt * 128, 128) for kt in range(n_kt)]
        moff = [0]
        for w in m_widths:
            moff.append(moff[-1] + w)
        kg = k_group if k_group is not None else n_kt
        kgroups = [list(range(i, min(i + kg, n_kt))) for i in range(0, n_kt, kg)]
        multi_kg = len(kgroups) > 1
        max_mt = len(m_widths)
        if multi_kg:
            max_mt = max(1, 4 // len(chunks))
        blocks = []
        cur, cw = [], 0
        for mi, w in enumerate(m_widths):
            if cur and (cw + w > mb_cols or len(cur) >= max_mt):
                blocks.append(cur)
                cur, cw = [], 0
            cur.append(mi)
            cw += w
        if cur:
            blocks.append(cur)

        def load_slabs(bi, kts, have=None):
            mis = blocks[bi]
            c0 = moff[mis[0]]
            c1 = moff[mis[-1] + 1]
            slabs = dict(have) if have else {}
            for kt in kts:
                if kt in slabs:
                    continue
                r0, nr = k_rows[kt]
                sb = kb.ab.alloc(c1 - c0, f"{tag}_w{bi}k{kt}")
                kb.dma(kb.wqueue(), sb.ap(0, nr, 0, c1 - c0),
                       dap[dram_row0 + r0:dram_row0 + r0 + nr,
                           dram_col0 + c0:dram_col0 + c1],
                       [dres], [sb.res], tag=f"{tag}_wd{bi}_{kt}")
                slabs[kt] = sb
            return slabs, c0

        rrbanks = kb.take_banks(2) + kb.take_banks(2)
        bank_i = 0
        n_aug = 1 if aug is not None else 0
        for bi in range(len(blocks)):
            mis = blocks[bi]
            if multi_kg:
                bmap = {}
                for j, mi in enumerate(mis):
                    for ci in range(len(chunks)):
                        bmap[(mi, ci)] = rrbanks[(j * len(chunks) + ci) % 4]
                n_mm_total = sum(len(g) for g in kgroups) + n_aug
                i_base = 0
                if bi == 0 and pre is not None:
                    nxt = load_slabs(0, kgroups[0], have=pre[0])
                else:
                    nxt = load_slabs(bi, kgroups[0])
                for gi, kts in enumerate(kgroups):
                    slabs, base = nxt
                    if gi + 1 < len(kgroups):
                        nxt = load_slabs(bi, kgroups[gi + 1])
                    elif bi + 1 < len(blocks):
                        pass
                    for mi in mis:
                        mw = m_widths[mi]
                        for ci, (t0, tw) in enumerate(chunks):
                            bk = bmap[(mi, ci)]
                            pres = kb.pbank[bk]
                            for jj, kt in enumerate(kts):
                                r0, nr = k_rows[kt]
                                rap, rres = rhs_fn(kt, ci)
                                sb = slabs[kt]
                                i_mm = i_base + jj
                                kb.mm(kb.bank_ap(bk, 0, tw, 0, mw),
                                      sb.ap(0, nr, moff[mi] - base,
                                            moff[mi] - base + mw),
                                      rap, i_mm == 0,
                                      i_mm == n_mm_total - 1,
                                      [sb.res] + rres, [pres],
                                      tag=f"{tag}_mm{mi}_{ci}_{kt}")
                            if gi == len(kgroups) - 1:
                                if aug is not None:
                                    lap, lres = aug[0](mi, mw)
                                    rap2, rres2 = aug[1](ci)
                                    kb.mm(kb.bank_ap(bk, 0, tw, 0, mw),
                                          lap, rap2, False, True,
                                          [lres] + rres2, [pres],
                                          tag=f"{tag}_aug{mi}_{ci}")
                                evac_fn(mi, ci, kb.bank_ap(bk, 0, tw, 0, mw),
                                        pres)
                    i_base += len(kts)
                    for sb in slabs.values():
                        sb.free()
            else:
                if bi == 0:
                    if pre is not None:
                        self._sm_slabs = load_slabs(0, kgroups[0],
                                                    have=pre[0])[0]
                        self._sm_base = pre[1]
                    else:
                        self._sm_slabs, self._sm_base = load_slabs(
                            0, kgroups[0])
                slabs, base = self._sm_slabs, self._sm_base
                if bi + 1 < len(blocks):
                    self._sm_slabs, self._sm_base = load_slabs(bi + 1,
                                                               kgroups[0])
                n_mm = n_kt + n_aug
                for mi in mis:
                    mw = m_widths[mi]
                    for ci, (t0, tw) in enumerate(chunks):
                        bk = rrbanks[bank_i % 4]
                        bank_i += 1
                        pres = kb.pbank[bk]
                        for kt in kgroups[0]:
                            r0, nr = k_rows[kt]
                            rap, rres = rhs_fn(kt, ci)
                            sb = slabs[kt]
                            kb.mm(kb.bank_ap(bk, 0, tw, 0, mw),
                                  sb.ap(0, nr, moff[mi] - base,
                                        moff[mi] - base + mw),
                                  rap, kt == kgroups[0][0],
                                  kt == kgroups[0][-1] and aug is None,
                                  [sb.res] + rres, [pres],
                                  tag=f"{tag}_mm{mi}_{ci}_{kt}")
                        if aug is not None:
                            lap, lres = aug[0](mi, mw)
                            rap2, rres2 = aug[1](ci)
                            kb.mm(kb.bank_ap(bk, 0, tw, 0, mw), lap, rap2,
                                  False, True, [lres] + rres2, [pres],
                                  tag=f"{tag}_aug{mi}_{ci}")
                        evac_fn(mi, ci, kb.bank_ap(bk, 0, tw, 0, mw), pres)
                for sb in slabs.values():
                    sb.free()


    def load_x_tile(self, src, kt, tag, queue=None):
        kb = self.kb
        dap, dres = kb.dram[src]
        b = kb.af.alloc(T, tag)
        kb.dma(queue or kb.wqueue(), b.ap(),
               dap[kt * 128:(kt + 1) * 128, :], [dres],
               [b.res], tag=tag)
        return b

    def cast_tile(self, src_buf, tag, eng="vector"):
        kb = self.kb
        b = kb.ab.alloc(T, tag)
        kb.copy(eng, b.ap(), src_buf.ap(), [src_buf.res], [b.res], tag=tag)
        return b


def _chunks(total, step):
    out = []
    c = 0
    while c < total:
        out.append((c, min(step, total - c)))
        c += step
    return out



DEV_STAGE = 99


class Model(Model):

    def emit_floor(self, buf, tmp, p1=128, w=None, tag="fl"):
        kb = self.kb
        if w is None:
            w = buf.w
        kb.copy("vector", tmp.ap(0, p1, 0, w).bitcast(mybir.dt.int32),
                buf.ap(0, p1, 0, w), [buf.res], [tmp.res], tag=tag + "_i")
        kb.copy("vector", tmp.ap(0, p1, 0, w),
                tmp.ap(0, p1, 0, w).bitcast(mybir.dt.int32),
                [tmp.res], [tmp.res], tag=tag + "_f")
        kb.tt("vector", buf.ap(0, p1, 0, w), tmp.ap(0, p1, 0, w),
              buf.ap(0, p1, 0, w), ALU.is_gt, [tmp.res, buf.res], [buf.res],
              tag=tag + "_g")
        kb.tt("vector", buf.ap(0, p1, 0, w), tmp.ap(0, p1, 0, w),
              buf.ap(0, p1, 0, w), ALU.subtract, [tmp.res, buf.res],
              [buf.res], tag=tag + "_s")

    def sincos_from_angle(self, ang, tag, want_cos=True):
        kb = self.kb
        inv2pi = 1.0 / (2.0 * math.pi)
        c_hi = 6.28125
        c_lo = 2.0 * math.pi - c_hi
        q = kb.af.alloc(T, tag + "_q")
        kb.ts("vector", q.ap(), ang.ap(), inv2pi, 0.5, ALU.mult, ALU.add,
              [ang.res], [q.res], tag=tag + "_q")
        qi = kb.af.alloc(T, tag + "_qi")
        self.emit_floor(q, qi, tag=tag + "_fl")
        qi.free()
        red = kb.af.alloc(T, tag + "_red")
        kb.stt("vector", red.ap(), q.ap(), -c_hi, ang.ap(), ALU.mult, ALU.add,
               [q.res, ang.res], [red.res], tag=tag + "_r1")
        kb.stt("vector", red.ap(), q.ap(), -c_lo, red.ap(), ALU.mult, ALU.add,
               [q.res, red.res], [red.res], tag=tag + "_r2")
        q.free()
        kb.ts("vector", red.ap(), red.ap(), math.pi, -math.pi, ALU.min,
              ALU.max, [red.res], [red.res], tag=tag + "_cl")
        sin_b = kb.af.alloc(T, tag + "_sin")
        kb.act(sin_b.ap(), red.ap(), AF.Sin, [red.res], [sin_b.res],
               tag=tag + "_sin")
        cos_b = None
        if want_cos:
            half = kb.af.alloc(T, tag + "_h2")
            kb.act(half.ap(), red.ap(), AF.Sin, [red.res], [half.res],
                   scale=0.5, tag=tag + "_sh")
            cos_b = kb.af.alloc(T, tag + "_cos")
            kb.tt("vector", cos_b.ap(), half.ap(), half.ap(), ALU.mult,
                  [half.res], [cos_b.res], tag=tag + "_sq")
            kb.ts("vector", cos_b.ap(), cos_b.ap(), -2.0, 1.0, ALU.mult,
                  ALU.add, [cos_b.res], [cos_b.res], tag=tag + "_c")
            half.free()
        red.free()
        return sin_b, cos_b

    def iota_row_tile(self, tag):
        kb = self.kb
        it = kb.af.alloc(T, tag)
        kb.iota(it.ap(), [[1, T]], 0, 0, [it.res], tag=tag)
        return it


    def load_gb_cols(self, gname, bname, tag):
        kb = self.kb
        b = kb.af.alloc(16, tag)
        for j, nm in enumerate((gname, bname)):
            dap, dres = kb.dram[nm]
            n = dap.tensor.shape[1]
            kb.dma("sync", kb.af.t[0:128, b.c0 + 8 * j:b.c0 + 8 * j + n],
                   dap, [dres], [b.res], tag=tag + nm)
        return b

    def attn_a(self, xa, xa_bf):
        kb = self.kb
        n1gb = self.load_gb_cols("a_n1_g", "a_n1_b", "n1gb")
        xn_bf = self.materialize_ln(
            lambda kt: (xa[kt].ap(), [xa[kt].res]),
            lambda kt: (xa_bf[kt], False),
            g_b=(n1gb, 0, 8), tag="n1")
        for b in xa_bf:
            b.free()
        n1gb.free()
        if self.stage == 101:
            self.early_out = [(b, 0, 128) for b in xn_bf]
            return xa
        sp_ap, sp_res = kb.dram["x_spill"]
        for kt in range(NKT):
            kb.dma("sync", sp_ap[kt * 128:(kt + 1) * 128, :], xa[kt].ap(),
                   [xa[kt].res], [sp_res], tag=f"aspl{kt}")
            xa[kt].free()
        xa = None

        beta = kb.af.alloc(T, "beta")
        pre_inp = self.sm_prewarm("a_inproj", NKT, [128], tag="pwain",
                                  chunks_len=4)

        def evac_beta(mi, c, ps_ap, pres):
            kb.copy("scalar", beta.ap(0, 128, c * CW, (c + 1) * CW), ps_ap,
                    [pres], [beta.res], tag="ev_beta")

        self.stream_matmul("a_inproj", NKT, [128],
                           lambda kt, c: (xn_bf[kt].ap(0, 128, c * CW, (c + 1) * CW),
                                          [xn_bf[kt].res]),
                           evac_beta, tag="inproj", pre=pre_inp)
        if self.stage == 102:
            self.early_out = [(beta, 0, 128)]
            return xa

        ld_col = kb.af.alloc(16, "ldcol")
        fq_col = kb.af.alloc(16, "fqcol")
        for nm, buf in (("a_log_decay", ld_col), ("a_frequency", fq_col)):
            dap, dres = kb.dram[nm]
            src = dap[0:1, 0:64].rearrange("a b -> b a")
            kb.dma("sync", kb.af.t[0:64, buf.c0:buf.c0 + 1], src, [dres],
                   [buf.res], tag="ld_" + nm)
            kb.dma("sync", kb.af.t[64:128, buf.c0:buf.c0 + 1], src, [dres],
                   [buf.res], tag="ld2_" + nm)
        mag_col = kb.af.alloc(16, "magcol")
        kb.act(mag_col.ap(0, 128, 0, 1), ld_col.ap(0, 128, 0, 1), AF.Sigmoid,
               [ld_col.res], [mag_col.res], tag="mag")
        ld_col.free()
        it = self.iota_row_tile("ssm_iota")
        ang = kb.af.alloc(T, "ssm_ang")
        kb.ts("vector", ang.ap(), it.ap(), fq_col.ap(0, 128, 0, 1), None,
              ALU.mult, ALU.bypass, [it.res, fq_col.res], [ang.res],
              sreads=[fq_col.res], tag="ssm_ang")
        fq_col.free()
        it.free()
        ST, CT = self.sincos_from_angle(ang, "ssm")
        ang.free()
        MT = kb.af.alloc(T, "ssm_mt")
        kb.memset("vector", MT.ap(), 1.0, [MT.res], tag="mt0")
        kb.ts("vector", MT.ap(), MT.ap(), mag_col.ap(0, 128, 0, 1), None,
              ALU.mult, ALU.bypass, [MT.res, mag_col.res], [MT.res],
              sreads=[mag_col.res], tag="mt1")
        mag_col.free()
        if self.stage in (1031, 1032, 1033):
            b = {1031: ST, 1032: CT, 1033: MT}[self.stage]
            self.early_out = [(b, 0, 128)]
            return xa

        G = kb.af.alloc(T, "ssm_g")
        t1 = kb.af.alloc(T, "ssm_t1")
        t2 = kb.af.alloc(T, "ssm_t2")
        kb.tt("vector", t1.ap(0, 64), beta.ap(64, 128), ST.ap(64, 128),
              ALU.mult, [beta.res, ST.res], [t1.res], tag="g1")
        kb.tt("vector", t2.ap(0, 64), beta.ap(0, 64), CT.ap(0, 64),
              ALU.mult, [beta.res, CT.res], [t2.res], tag="g2")
        kb.tt("vector", G.ap(0, 64), t1.ap(0, 64), t2.ap(0, 64), ALU.add,
              [t1.res, t2.res], [G.res], tag="g3")
        kb.tt("vector", t1.ap(64, 128), beta.ap(0, 64), ST.ap(0, 64),
              ALU.mult, [beta.res, ST.res], [t1.res], tag="g4")
        kb.tt("vector", t2.ap(64, 128), beta.ap(64, 128), CT.ap(64, 128),
              ALU.mult, [beta.res, CT.res], [t2.res], tag="g5")
        kb.tt("vector", G.ap(64, 128), t2.ap(64, 128), t1.ap(64, 128),
              ALU.subtract, [t1.res, t2.res], [G.res], tag="g6")
        beta.free()
        if self.stage == 1034:
            self.early_out = [(G, 0, 128)]
            return xa

        DD = kb.af.alloc(T, "ssm_d")
        kb.scan(DD.ap(), MT.ap(), G.ap(), 0.0, [MT.res, G.res], [DD.res],
                tag="scan")
        G.free()
        MT.free()

        C = kb.ab.alloc(T, "ssm_c")
        kb.tt("vector", t1.ap(0, 64), DD.ap(64, 128), ST.ap(64, 128),
              ALU.mult, [DD.res, ST.res], [t1.res], tag="u1")
        kb.tt("vector", t2.ap(0, 64), DD.ap(0, 64), CT.ap(0, 64),
              ALU.mult, [DD.res, CT.res], [t2.res], tag="u2")
        kb.tt("vector", C.ap(0, 64), t2.ap(0, 64), t1.ap(0, 64),
              ALU.subtract, [t1.res, t2.res], [C.res], tag="u3")
        kb.tt("vector", t1.ap(64, 128), DD.ap(0, 64), ST.ap(0, 64),
              ALU.mult, [DD.res, ST.res], [t1.res], tag="u4")
        kb.tt("vector", t2.ap(64, 128), DD.ap(64, 128), CT.ap(64, 128),
              ALU.mult, [DD.res, CT.res], [t2.res], tag="u5")
        kb.tt("vector", C.ap(64, 128), t1.ap(64, 128), t2.ap(64, 128),
              ALU.add, [t1.res, t2.res], [C.res], tag="u6")
        DD.free()
        ST.free()
        CT.free()
        t1.free()
        t2.free()
        if self.stage == 103:
            self.early_out = [(C, 0, 128)]
            return xa

        cpl = kb.ab.alloc(64, "cpl")
        dap, dres = kb.wbf["a_coupling_bd"]
        kb.dma("sync", cpl.ap(0, 64, 0, 64), dap, [dres], [cpl.res],
               tag="cpl_ld")
        kb.dma("sync", cpl.ap(64, 128, 0, 64), dap, [dres], [cpl.res],
               tag="cpl_ld2")
        EIG = kb.ab.alloc(T, "eig")
        bks = kb.take_banks(2)
        for c in range(NCH):
            for half in range(2):
                bk = bks[half]
                kb.mm(kb.bank_ap(bk, 0, CW, 0, 64),
                      cpl.ap(64 * half, 64 * half + 64, 0, 64),
                      C.ap(64 * half, 64 * half + 64, c * CW, (c + 1) * CW),
                      True, True, [cpl.res, C.res], [kb.pbank[bk]],
                      tag=f"cplmm{c}_{half}")
                kb.copy("scalar",
                        EIG.ap(64 * half, 64 * half + 64, c * CW, (c + 1) * CW),
                        kb.bank_ap(bk, 0, CW, 0, 64), [kb.pbank[bk]],
                        [EIG.res], tag=f"cplev{c}_{half}")
        C.free()
        cpl.free()
        if self.stage == 104:
            self.early_out = [(EIG, 0, 128)]
            return xa

        gateb = self.load_gb_cols("a_gate_b", "a_gate_b", "gateb")
        gdap, gdres = kb.wbf["a_gate"]
        odap, odres = kb.wbf["a_outproj"]
        sig = kb.af.alloc(CW, "sig")
        htmp = kb.af.alloc(CW, "htmp")
        xa2 = [None] * NKT
        for mb in range(4):
            gslabs = []
            for kt in range(NKT):
                sb = kb.ab.alloc(256, f"gw{mb}_{kt}")
                kb.dma(kb.wqueue(), sb.ap(0, 128, 0, 256),
                       gdap[kt * 128:(kt + 1) * 128, mb * 256:(mb + 1) * 256],
                       [gdres], [sb.res], tag=f"gwd{mb}_{kt}")
                gslabs.append(sb)
            oslab = kb.ab.alloc(256, f"ow{mb}")
            kb.dma(kb.wqueue(), oslab.ap(0, 128, 0, 256),
                   odap[0:128, mb * 256:(mb + 1) * 256], [odres],
                   [oslab.res], tag=f"owd{mb}")
            for mi2 in range(2):
                mi = 2 * mb + mi2
                xs = self.load_x_tile("x_spill", mi, f"gxs{mi}")
                xa2[mi] = kb.af.alloc(T, f"xa2_{mi}")
                for c in range(NCH):
                    bkg = kb.take_banks(1)[0]
                    bko = kb.take_banks(1)[0]
                    for kt in range(NKT):
                        kb.mm(kb.bank_ap(bkg),
                              gslabs[kt].ap(0, 128, mi2 * 128, mi2 * 128 + 128),
                              xn_bf[kt].ap(0, 128, c * CW, (c + 1) * CW),
                              kt == 0, kt == NKT - 1,
                              [gslabs[kt].res, xn_bf[kt].res],
                              [kb.pbank[bkg]], tag=f"gmm{mi}_{c}_{kt}")
                    kb.mm(kb.bank_ap(bko),
                          oslab.ap(0, 128, mi2 * 128, mi2 * 128 + 128),
                          EIG.ap(0, 128, c * CW, (c + 1) * CW),
                          True, True, [oslab.res, EIG.res], [kb.pbank[bko]],
                          tag=f"omm{mi}_{c}")
                    kb.act(sig.ap(), kb.bank_ap(bkg), AF.Sigmoid,
                           [kb.pbank[bkg], gateb.res], [sig.res],
                           bias=gateb.ap(0, 128, mi, mi + 1),
                           sreads=[gateb.res], tag=f"sig{mi}_{c}")
                    kb.tt("vector", htmp.ap(), sig.ap(), kb.bank_ap(bko),
                          ALU.mult, [sig.res, kb.pbank[bko]], [htmp.res],
                          tag=f"hm{mi}_{c}")
                    kb.tt("vector", xa2[mi].ap(0, 128, c * CW, (c + 1) * CW),
                          xs.ap(0, 128, c * CW, (c + 1) * CW), htmp.ap(),
                          ALU.add, [xs.res, htmp.res], [xa2[mi].res],
                          tag=f"hr{mi}_{c}")
                xs.free()
            for sb in gslabs:
                sb.free()
            oslab.free()
        sig.free()
        htmp.free()
        gateb.free()
        EIG.free()
        for b in xn_bf:
            b.free()

        xa = xa2
        if self.stage == 105:
            self.early_out = [(b, 0, 128) for b in xa2]
            return xa
        n2gb = self.load_gb_cols("a_n2_g", "a_n2_b", "n2gb")
        xa_bf2 = [self.cast_tile(xa[kt], f"xabf2_{kt}") for kt in range(NKT)]
        x2n_bf = self.materialize_ln(
            lambda kt: (xa[kt].ap(), [xa[kt].res]),
            lambda kt: (xa_bf2[kt], False),
            g_b=(n2gb, 0, 8), tag="n2")
        for b in xa_bf2:
            b.free()
        n2gb.free()

        b1c = kb.af.alloc(32, "b1c")
        dap, dres = kb.dram["a_mlp_b1"]
        kb.dma("sync", kb.af.t[0:128, b1c.c0:b1c.c0 + 32], dap, [dres],
               [b1c.res], tag="b1c")
        b2c = self.load_gb_cols("a_mlp_b2", "a_mlp_b2", "b2c")

        TCW = 512
        pre_aw1 = self.sm_prewarm("a_w1", NKT, [128] * 32, tag="pwaw1",
                                  mb_cols=512, chunks_len=1, nk_pre=4)
        for tc0 in range(0, T, TCW):
            hid = [kb.ab.alloc(TCW, f"ahid{m}") for m in range(32)]
            chunks = [(tc0, TCW)]

            def rhs_w1(kt, ci, t0=tc0):
                return (x2n_bf[kt].ap(0, 128, t0, t0 + TCW),
                        [x2n_bf[kt].res])

            def evac_w1(mi, ci, ps_ap, pres):
                kb.act(hid[mi].ap(), ps_ap, AF.Silu,
                       [pres, b1c.res], [hid[mi].res],
                       bias=b1c.ap(0, 128, mi, mi + 1), sreads=[b1c.res],
                       tag=f"silu{mi}")

            pre_aw2 = self.sm_prewarm("a_w2", 32, [128] * 8,
                                      tag=f"pwaw2_{tc0}", mb_cols=1024,
                                      k_group=8, chunks_len=1, nk_pre=2)
            self.stream_matmul("a_w1", NKT, [128] * 32, rhs_w1, evac_w1,
                               chunks=chunks, tag="aw1", mb_cols=512,
                               pre=pre_aw1)
            pre_aw1 = None

            def rhs_w2(kt, ci, t0=tc0):
                return (hid[kt].ap(), [hid[kt].res])

            def evac_w2(mi, ci, ps_ap, pres, t0=tc0):
                kb.stt("vector", xa[mi].ap(0, 128, t0, t0 + TCW), ps_ap,
                       b2c.ap(0, 128, mi, mi + 1),
                       xa[mi].ap(0, 128, t0, t0 + TCW), ALU.add, ALU.add,
                       [pres, xa[mi].res, b2c.res], [xa[mi].res],
                       sreads=[b2c.res], tag=f"w2r{mi}")

            self.stream_matmul("a_w2", 32, [128] * 8, rhs_w2, evac_w2,
                               chunks=chunks, tag="aw2", mb_cols=1024,
                               k_group=8, pre=pre_aw2)
            for b in hid:
                b.free()
        for b in x2n_bf:
            b.free()
        b1c.free()
        b2c.free()
        return xa


    def build_ffn_tables(self):
        kb = self.kb
        it = self.iota_row_tile("ft_iota")
        icol_ap, icol_res = self.kb.cap("mod32_col")
        invf = kb.af.alloc(16, "ft_if")
        kb.act(invf.ap(0, 128, 0, 1), icol_ap, AF.Exp,
               [icol_res], [invf.res],
               scale=-2.0 * math.log(10000.0) / 64.0, tag="ft_inv")
        lnb = kb.af.alloc(16, "ft_lb")
        lnb2 = kb.af.alloc(16, "ft_lb2")
        kb.ts("vector", lnb.ap(0, 128, 0, 1), icol_ap,
              2.0 / (1.4 * 64.0), (0.4 * 64.0) / (1.4 * 64.0), ALU.mult,
              ALU.add, [icol_res], [lnb.res], tag="ft_b")
        kb.act(lnb2.ap(0, 128, 0, 1), lnb.ap(0, 128, 0, 1), AF.Ln,
               [lnb.res], [lnb2.res], tag="ft_lnb")
        lnb.free()
        lnb = lnb2
        ang = kb.af.alloc(T, "ft_ang")
        kb.ts("vector", ang.ap(), it.ap(), invf.ap(0, 128, 0, 1), None,
              ALU.mult, ALU.bypass, [it.res, invf.res], [ang.res],
              sreads=[invf.res], tag="ft_ang")
        invf.free()
        sinb, cosb = self.sincos_from_angle(ang, "ftsc")
        ang.free()
        pw = kb.af.alloc(T, "ft_pw")
        kb.ts("vector", pw.ap(), it.ap(), 1.0 / 256.0, -(T // 2) / 256.0,
              ALU.mult, ALU.add, [it.res], [pw.res], tag="ft_pw")
        it.free()
        sc = kb.af.alloc(T, "ft_sc")
        kb.ts("vector", sc.ap(), pw.ap(), lnb.ap(0, 128, 0, 1), None,
              ALU.mult, ALU.bypass, [pw.res, lnb.res], [sc.res],
              sreads=[lnb.res], tag="ft_scm")
        pw.free()
        lnb.free()
        kb.act(sc.ap(), sc.ap(), AF.Exp, [sc.res], [sc.res], tag="ft_sce")
        dap, dres = kb.dram["ftab"]
        tmp = kb.ab.alloc(T, "ft_t")
        for j, (trig, trig_res) in enumerate(((cosb.ap(), cosb.res),
                                              (sinb.ap(), sinb.res))):
            kb.tt("vector", tmp.ap(), trig, sc.ap(), ALU.mult,
                  [trig_res, sc.res], [tmp.res], tag=f"ft_q{j}")
            kb.dma("sync", dap[j], tmp.ap(), [tmp.res], [dres],
                   tag=f"ft_sq{j}")
        rsc = kb.af.alloc(T, "ft_rs")
        kb.act(rsc.ap(), sc.ap(), AF.Ln, [sc.res], [rsc.res], tag="ft_lsc")
        kb.act(rsc.ap(), rsc.ap(), AF.Exp, [rsc.res], [rsc.res], scale=-1.0,
               sreads=[rsc.res], tag="ft_esc")
        sc.free()
        for j, (trig, trig_res) in enumerate(((cosb.ap(), cosb.res),
                                              (sinb.ap(), sinb.res))):
            kb.tt("vector", tmp.ap(), trig, rsc.ap(), ALU.mult,
                  [trig_res, rsc.res], [tmp.res], tag=f"ft_k{j}")
            kb.dma("sync", dap[2 + j], tmp.ap(), [tmp.res], [dres],
                   tag=f"ft_sk{j}")
        rsc.free()
        tmp.free()
        sinb.free()
        cosb.free()

    def build_attnb_tables(self):
        kb = self.kb
        it = self.iota_row_tile("bt_iota")
        pcol_ap, pcol_res = self.kb.cap("mod16_col")
        invf = kb.af.alloc(16, "bt_if")
        kb.act(invf.ap(0, 128, 0, 1), pcol_ap, AF.Exp,
               [pcol_res], [invf.res],
               scale=-2.0 * math.log(10000.0) / 32.0, tag="bt_inv")
        ang = kb.af.alloc(T, "bt_ang")
        kb.ts("vector", ang.ap(), it.ap(), invf.ap(0, 128, 0, 1), None,
              ALU.mult, ALU.bypass, [it.res, invf.res], [ang.res],
              sreads=[invf.res], tag="bt_ang")
        invf.free()
        it.free()
        sinb, cosb = self.sincos_from_angle(ang, "btsc")
        ang.free()
        dap, dres = kb.dram["btab"]
        kb.dma("sync", dap[0], cosb.ap(), [cosb.res], [dres], tag="bt_sc")
        kb.dma("sync", dap[1], sinb.ap(), [sinb.res], [dres], tag="bt_ss")
        sinb.free()
        cosb.free()


    def ffn_layer(self, y, l, inst):
        kb = self.kb
        dbg = (inst == 0 and l == 0)
        tg = f"F{inst}{l}"
        pre_qkv = [self.sm_prewarm("f_wqkv", NKT, [128] * 12, layer_idx=l,
                                   tag=f"{tg}pwq0", mb_cols=512,
                                   chunks_len=2), None]
        qkv = [kb.ab.alloc(T, f"{tg}qkv{m}") for m in range(12)]
        HT = T // 2
        for th in range(2):
            t0 = th * HT
            h_bf = []
            for kt in range(NKT):
                b = kb.ab.alloc(HT, f"{tg}h{th}_{kt}")
                eng = "scalar" if kt % 2 else "vector"
                if kt < 4:
                    kb.copy(eng, b.ap(), y[kt].ap(0, 128, t0, t0 + HT),
                            [y[kt].res], [b.res], tag=f"{tg}hc{th}{kt}")
                elif t0 == 0:
                    kb.memset("vector", b.ap(0, 128, 0, 1), 0.0, [b.res],
                              tag=f"{tg}hz{kt}")
                    kb.copy(eng, b.ap(0, 128, 1, HT),
                            y[kt].ap(0, 128, 0, HT - 1), [y[kt].res],
                            [b.res], tag=f"{tg}hs{th}{kt}")
                else:
                    kb.copy(eng, b.ap(0, 128, 0, HT),
                            y[kt].ap(0, 128, t0 - 1, t0 + HT - 1),
                            [y[kt].res], [b.res], tag=f"{tg}hs{th}{kt}")
                h_bf.append(b)
            chunks_h = [(t0, CW), (t0 + CW, CW)]

            def evac_qkv(mi, ci, ps_ap, pres, chunks_h=chunks_h):
                cc = chunks_h[ci][0]
                kb.copy("scalar", qkv[mi].ap(0, 128, cc, cc + CW),
                        ps_ap, [pres], [qkv[mi].res],
                        tag=f"{tg}qkve{th}{mi}_{ci}")

            self.stream_matmul("f_wqkv", NKT, [128] * 12,
                               lambda kt, ci, h_bf=h_bf: (
                                   h_bf[kt].ap(0, 128, ci * CW,
                                               (ci + 1) * CW),
                                   [h_bf[kt].res]),
                               evac_qkv, layer_idx=l, chunks=chunks_h,
                               tag=f"{tg}qkv{th}", mb_cols=512,
                               pre=pre_qkv[th])
            if th == 0:
                pre_qkv[1] = self.sm_prewarm(
                    "f_wqkv", NKT, [128] * 12, layer_idx=l,
                    tag=f"{tg}pwq1", mb_cols=512, chunks_len=2)
            for b in h_bf:
                b.free()

        if dbg and self.stage == 201:
            self.early_out = [(qkv[i], 0, 128) for i in range(8)]
            return
        qs_col = kb.af.alloc(16, f"{tg}qsc")
        ks_col = kb.af.alloc(16, f"{tg}ksc")
        for nm, buf in (("f_qs", qs_col), ("f_ks", ks_col)):
            dap, dres = kb.dram[nm]
            kb.dma("sync", kb.af.t[0:128, buf.c0:buf.c0 + 4], dap[l],
                   [dres], [buf.res], tag=f"{tg}ld{nm}")
        tab_dap, tab_dres = kb.dram["ftab"]
        qt_out = []
        for side in range(2):
            tiles = qkv[side * 4:side * 4 + 4]
            scol = qs_col if side == 0 else ks_col
            oh_ap, oh_res, _ = kb.cst["l2_onehot"]
            banks = kb.take_banks(4)
            q2 = [kb.ab.alloc(T, f"{tg}q2_{side}{i}") for i in range(2)]
            for i in range(4):
                q2b = q2[i % 2]
                kb.tt("vector", q2b.ap(), tiles[i].ap(), tiles[i].ap(),
                      ALU.mult, [tiles[i].res], [q2b.res],
                      tag=f"{tg}sq{side}{i}")
                for c in range(NCH):
                    bk = banks[c]
                    kb.mm(kb.bank_ap(bk, 0, CW, 0, 8),
                          kb.cst["l2_onehot"][0][:, i * 8:(i + 1) * 8],
                          q2b.ap(0, 128, c * CW, (c + 1) * CW),
                          i == 0, i == 3, [oh_res, q2b.res],
                          [kb.pbank[bk]], tag=f"{tg}l2m{side}{i}{c}")
            for b in q2:
                b.free()
            nrm = kb.af.alloc(T, f"{tg}nrm{side}")
            for c in range(NCH):
                bk = banks[c]
                kb.ts("vector", nrm.ap(0, 8, c * CW, (c + 1) * CW),
                      kb.bank_ap(bk, 0, CW, 0, 8), 1e-24, None,
                      ALU.max, ALU.bypass, [kb.pbank[bk]], [nrm.res],
                      tag=f"{tg}l2c{side}{c}")
            rqf = kb.af.alloc(T, f"{tg}rqf{side}")
            kb.act(rqf.ap(0, 8), nrm.ap(0, 8), AF.Ln, [nrm.res], [rqf.res],
                   tag=f"{tg}l2l{side}")
            rq = kb.ab.alloc(T, f"{tg}rq{side}")
            kb.act(rq.ap(0, 8), rqf.ap(0, 8), AF.Exp, [rqf.res], [rq.res],
                   scale=-0.5, tag=f"{tg}l2e{side}")
            nrm.free()
            rqf.free()
            cs = kb.ab.alloc(T, f"{tg}cs{side}")
            ss = kb.ab.alloc(T, f"{tg}ss{side}")
            kb.dma("sync", cs.ap(), tab_dap[2 * side + 0], [tab_dres],
                   [cs.res], tag=f"{tg}tc{side}")
            kb.dma("sync", ss.ap(), tab_dap[2 * side + 1], [tab_dres],
                   [ss.res], tag=f"{tg}ts{side}")
            if dbg and self.stage == 2021 and side == 0:
                self.early_out = [(cs, 0, 128), (ss, 0, 128)]
                return
            a_t = [kb.ab.alloc(T, f"{tg}a{side}{i}") for i in range(4)]
            ohT_ap, ohT_res, _ = kb.cst["l2_onehot_T"]
            bks2 = kb.take_banks(2)
            for i in range(4):
                for c in range(NCH):
                    bk = bks2[c % 2]
                    kb.mm(kb.bank_ap(bk), kb.cst["l2_onehot_T"][0][:, i * 128:(i + 1) * 128],
                          rq.ap(0, 8, c * CW, (c + 1) * CW), True, True,
                          [ohT_res, rq.res], [kb.pbank[bk]],
                          tag=f"{tg}rqb{side}{i}{c}")
                    kb.stt("vector", a_t[i].ap(0, 128, c * CW, (c + 1) * CW),
                           tiles[i].ap(0, 128, c * CW, (c + 1) * CW),
                           scol.ap(0, 128, i, i + 1), kb.bank_ap(bk),
                           ALU.mult, ALU.mult,
                           [tiles[i].res, scol.res, kb.pbank[bk]],
                           [a_t[i].res], sreads=[scol.res],
                           tag=f"{tg}am{side}{i}{c}")
                tiles[i].free()
            rq.free()
            tsw = kb.ab.alloc(T, f"{tg}tsw{side}")
            for i in range(4):
                prt = (i + 2) % 4
                ob = kb.ab.alloc(T, f"{tg}rp{side}{i}")
                kb.tt("vector", tsw.ap(), a_t[prt].ap(), ss.ap(), ALU.mult,
                      [a_t[prt].res, ss.res], [tsw.res],
                      tag=f"{tg}rs{side}{i}")
                kb.tt("vector", ob.ap(), a_t[i].ap(), cs.ap(), ALU.mult,
                      [a_t[i].res, cs.res], [ob.res], tag=f"{tg}rc{side}{i}")
                kb.tt("vector", ob.ap(), ob.ap(), tsw.ap(),
                      ALU.subtract if i < 2 else ALU.add,
                      [ob.res, tsw.res], [ob.res], tag=f"{tg}ra{side}{i}")
                qt_out.append(ob)
            tsw.free()
            for b in a_t:
                b.free()
            cs.free()
            ss.free()
        qs_col.free()
        ks_col.free()
        if self.early_out is not None:
            return
        qr_t = qt_out[0:4]
        kr_t = qt_out[4:8]
        if dbg and self.stage == 202:
            self.early_out = [(b, 0, 128) for b in qt_out]
            return

        v_t = qkv[8:12]
        vtok = [kb.ab.alloc(16 * 65 + 16, f"{tg}vt{h}") for h in range(8)]
        for h in range(8):
            kb.memset("vector", vtok[h].ap(0, 128, 0, 16 * 65), 1.0,
                      [vtok[h].res], tag=f"{tg}vz{h}")
        bks = kb.take_banks(2)
        for i in range(4):
            for tc in range(16):
                bk = bks[tc % 2]
                kb.transp(kb.bank_bap(bk, 0, 128),
                          v_t[i].ap(0, 128, tc * 128, (tc + 1) * 128),
                          [v_t[i].res], [kb.pbank[bk]], tag=f"{tg}vt{i}_{tc}",
                          bf16=True)
                for hh in range(2):
                    h = 2 * i + hh
                    kb.copy("vector",
                            vtok[h].ap(0, 128, tc * 65, tc * 65 + 64),
                            kb.bank_bap(bk, hh * 64, hh * 64 + 64, 0, 128),
                            [kb.pbank[bk]], [vtok[h].res],
                            tag=f"{tg}ve{h}_{tc}")
        for b in v_t:
            b.free()

        att = [kb.ab.alloc(T, f"{tg}att{i}") for i in range(4)]
        self.local_attention(qr_t, kr_t, vtok, att, tg)
        for b in qr_t + kr_t:
            b.free()
        for b in vtok:
            b.free()
        if dbg and self.stage == 203:
            self.early_out = [(b, 0, 128) for b in att]
            return

        m_widths_w1 = [128] * 42 + [42, 42]
        pre_w1 = self.sm_prewarm("f_w1", NKT, m_widths_w1, layer_idx=l,
                                 tag=f"{tg}pw1a", mb_cols=512, chunks_len=1)

        def evac_wo(mi, ci, ps_ap, pres):
            kb.tt("vector", y[mi].ap(0, 128, ci * CW, (ci + 1) * CW),
                  y[mi].ap(0, 128, ci * CW, (ci + 1) * CW), ps_ap, ALU.add,
                  [y[mi].res, pres], [y[mi].res], tag=f"{tg}wor{mi}_{ci}")

        self.stream_matmul("f_wo", 4, [128] * 8,
                           lambda kt, ci: (att[kt].ap(0, 128, ci * CW,
                                                      (ci + 1) * CW),
                                           [att[kt].res]),
                           evac_wo, layer_idx=l, tag=f"{tg}wo")
        for b in att:
            b.free()
        if dbg and self.stage == 204:
            self.early_out = [(b, 0, 128) for b in y]
            return

        h2_bf = []
        for kt in range(NKT):
            b = kb.ab.alloc(T, f"{tg}h2{kt}")
            eng = "scalar" if kt % 2 else "vector"
            if kt < 4:
                kb.copy(eng, b.ap(), y[kt].ap(), [y[kt].res], [b.res],
                        tag=f"{tg}h2c{kt}")
            else:
                kb.memset("vector", b.ap(0, 128, 0, 1), 0.0, [b.res],
                          tag=f"{tg}h2z{kt}")
                kb.copy(eng, b.ap(0, 128, 1, T),
                        y[kt].ap(0, 128, 0, T - 1), [y[kt].res], [b.res],
                        tag=f"{tg}h2s{kt}")
            h2_bf.append(b)
        lngb = kb.af.alloc(16, f"{tg}lngb")
        for j, nm in enumerate(("f_lng", "f_lnb")):
            dap, dres = kb.dram[nm]
            kb.dma("sync", kb.af.t[0:128, lngb.c0 + 8 * j:lngb.c0 + 8 * j + 8],
                   dap[l], [dres], [lngb.res], tag=f"{tg}lngb{j}")
        hn_bf = self.materialize_ln(
            lambda kt: (h2_bf[kt].ap(), [h2_bf[kt].res]),
            lambda kt: (h2_bf[kt], False),
            g_b=(lngb, 0, 8), tag=f"{tg}ln")
        for b in h2_bf:
            b.free()
        lngb.free()

        b1c = kb.af.alloc(48, f"{tg}b1c")
        dap, dres = kb.dram["f_b1"]
        kb.dma("sync", kb.af.t[0:128, b1c.c0:b1c.c0 + 44], dap[l], [dres],
               [b1c.res], tag=f"{tg}b1c")
        b2c = kb.af.alloc(16, f"{tg}b2c")
        dap, dres = kb.dram["f_b2"]
        kb.dma("sync", kb.af.t[0:128, b2c.c0:b2c.c0 + 8], dap[l], [dres],
               [b2c.res], tag=f"{tg}b2c")

        k_rows_w2 = [(i * 128, 128) for i in range(21)] + [(2688, 42)]
        TCW2 = 512
        for tc0 in range(0, T, TCW2):
            chunks = [(tc0, CW)]
            hid = [kb.ab.alloc(TCW2, f"{tg}hid{p}") for p in range(22)]
            atmp = [kb.ab.alloc(TCW2, f"{tg}at{p % 2}") for p in range(2)]

            def evac_w1(mi, ci, ps_ap, pres, t0=tc0, hid=hid, atmp=atmp):
                pair = mi // 2
                pw = 128 if pair < 21 else 42
                cc0 = ci * CW
                if mi % 2 == 0:
                    kb.act(atmp[pair % 2].ap(0, pw, cc0, cc0 + CW), ps_ap,
                           AF.Identity, [pres, b1c.res],
                           [atmp[pair % 2].res],
                           bias=b1c.ap(0, pw, mi, mi + 1), sreads=[b1c.res],
                           tag=f"{tg}w1a{mi}_{ci}")
                else:
                    gg = kb.ab.alloc(CW, f"{tg}gg")
                    kb.act(gg.ap(0, pw), ps_ap, AF.Gelu, [pres, b1c.res],
                           [gg.res], bias=b1c.ap(0, pw, mi, mi + 1),
                           sreads=[b1c.res], tag=f"{tg}w1g{mi}_{ci}")
                    kb.tt("vector", hid[pair].ap(0, pw, cc0, cc0 + CW),
                          atmp[pair % 2].ap(0, pw, cc0, cc0 + CW),
                          gg.ap(0, pw), ALU.mult,
                          [atmp[pair % 2].res, gg.res], [hid[pair].res],
                          tag=f"{tg}w1h{mi}_{ci}")
                    gg.free()

            pre_w2 = self.sm_prewarm("f_w2", 22, [128] * 8, layer_idx=l,
                                     tag=f"{tg}pw2_{tc0}", mb_cols=256,
                                     k_group=8, k_rows=k_rows_w2,
                                     chunks_len=1)
            self.stream_matmul("f_w1", NKT, m_widths_w1,
                               lambda kt, ci: (hn_bf[kt].ap(0, 128,
                                                            chunks[ci][0],
                                                            chunks[ci][0] + CW),
                                               [hn_bf[kt].res]),
                               evac_w1, layer_idx=l, chunks=chunks,
                               tag=f"{tg}w1", mb_cols=512, pre=pre_w1)
            if tc0 + TCW2 < T:
                pre_w1 = self.sm_prewarm("f_w1", NKT, m_widths_w1,
                                         layer_idx=l, tag=f"{tg}pw1_{tc0}",
                                         mb_cols=512, chunks_len=1)

            def evac_w2(mi, ci, ps_ap, pres, hid=hid):
                t0 = chunks[ci][0]
                kb.stt("vector", y[mi].ap(0, 128, t0, t0 + CW), ps_ap,
                       b2c.ap(0, 128, mi, mi + 1),
                       y[mi].ap(0, 128, t0, t0 + CW), ALU.add, ALU.add,
                       [pres, y[mi].res, b2c.res], [y[mi].res],
                       sreads=[b2c.res], tag=f"{tg}w2r{mi}_{ci}")

            self.stream_matmul("f_w2", 22, [128] * 8,
                               lambda kt, ci: (hid[kt].ap(0, k_rows_w2[kt][1],
                                                          ci * CW,
                                                          (ci + 1) * CW),
                                               [hid[kt].res]),
                               evac_w2, layer_idx=l, chunks=chunks,
                               k_rows=k_rows_w2, k_group=8, tag=f"{tg}w2",
                               mb_cols=256, pre=pre_w2)
            for b in hid:
                b.free()
            for b in atmp:
                b.free()
        for b in hn_bf:
            b.free()
        b1c.free()
        b2c.free()

    def local_attention(self, qr_t, kr_t, vtok, att, tg):
        kb = self.kb
        mle, mle_res = kb.cap("m_leb")
        mgt, mgt_res = kb.cap("m_gtb")
        mlt, mlt_res = kb.cap("m_ltb")
        mge, mge_res = kb.cap("m_geb")
        idb, idb_res = kb.cap("identb")
        fl_dap, fl_res = kb.dram["flip"]
        for h in range(8):
            ti = h // 4
            r0 = 32 * (h % 4)
            kaug = kb.ab.alloc(2560, f"{tg}ka{h}")
            kb.memset("vector", kaug.ap(0, 64, 0, 512), 0.0, [kaug.res],
                      tag=f"{tg}kz{h}")
            kb.memset("vector", kaug.ap(64, 65, 0, 2560), -1.0, [kaug.res],
                      tag=f"{tg}kn{h}")
            kb.copy("vector", kaug.ap(0, 32, 512, 2560),
                    kr_t[ti].ap(r0, r0 + 32, 0, T), [kr_t[ti].res],
                    [kaug.res], tag=f"{tg}kc1{h}")
            kb.copy("vector", kaug.ap(32, 64, 512, 2560),
                    kr_t[2 + ti].ap(r0, r0 + 32, 0, T), [kr_t[2 + ti].res],
                    [kaug.res], tag=f"{tg}kc2{h}")
            qaug = kb.ab.alloc(T, f"{tg}qa{h}")
            kb.copy("vector", qaug.ap(0, 32, 0, T),
                    qr_t[ti].ap(r0, r0 + 32, 0, T), [qr_t[ti].res],
                    [qaug.res], tag=f"{tg}qc1{h}")
            kb.copy("vector", qaug.ap(32, 64, 0, T),
                    qr_t[2 + ti].ap(r0, r0 + 32, 0, T), [qr_t[2 + ti].res],
                    [qaug.res], tag=f"{tg}qc2{h}")
            coll = kb.af.alloc(16, f"{tg}cl{h}")
            bkp = [kb.take_banks(2), kb.take_banks(2)]
            for qt in range(16):
                bks = bkp[qt % 2]
                qt_ap = qaug.ap(0, 64, qt * 128, (qt + 1) * 128)
                if qt >= 4:
                    kb.mm(kb.wide_ap(bks, 0, 128), qt_ap,
                          kaug.ap(0, 64, qt * 128, qt * 128 + 128),
                          True, False, [qaug.res, kaug.res],
                          [kb.pbank[bks[0]]], tag=f"{tg}sp{h}_{qt}a0")
                    kb.mm(kb.wide_ap(bks, 0, 128), idb, mle,
                          False, True, [idb_res, mle_res],
                          [kb.pbank[bks[0]]], tag=f"{tg}sm1{h}_{qt}")
                    kb.mm(kb.wide_ap(bks, 128, 512), qt_ap,
                          kaug.ap(0, 64, qt * 128 + 128, qt * 128 + 512),
                          True, True, [qaug.res, kaug.res],
                          [kb.pbank[bks[0]]], tag=f"{tg}sp{h}_{qt}a1")
                else:
                    kb.mm(kb.wide_ap(bks, 0, 512), qt_ap,
                          kaug.ap(0, 64, qt * 128, qt * 128 + 512),
                          True, True, [qaug.res, kaug.res],
                          [kb.pbank[bks[0]]], tag=f"{tg}sp{h}_{qt}a")
                kb.mm(kb.wide_ap(bks, 512, 640), qt_ap,
                      kaug.ap(0, 64, qt * 128 + 512, qt * 128 + 640),
                      True, False, [qaug.res, kaug.res],
                      [kb.pbank[bks[1]]], tag=f"{tg}sp{h}_{qt}b")
                kb.mm(kb.wide_ap(bks, 512, 640), idb, mgt,
                      False, True, [idb_res, mgt_res],
                      [kb.pbank[bks[1]]], tag=f"{tg}sm2{h}_{qt}")
                c0 = 128 * max(0, 4 - qt)
                kb.reduce("vector", coll.ap(0, 128, qt, qt + 1),
                          kb.wide_ap(bks, c0, 640), ALU.max,
                          [kb.pbank[bks[0]], kb.pbank[bks[1]]], [coll.res],
                          tag=f"{tg}rmx{h}_{qt}")
            bkf = kb.take_banks(1)[0]
            kb.transp(kb.bank_ap(bkf, 0, 128, 0, 16), coll.ap(0, 128, 0, 16),
                      [coll.res], [kb.pbank[bkf]], tag=f"{tg}tp{h}")
            ftmp = kb.af.alloc(128, f"{tg}ft{h}")
            kb.copy("vector", ftmp.ap(0, 16, 0, 128),
                    kb.bank_ap(bkf, 0, 128, 0, 16), [kb.pbank[bkf]],
                    [ftmp.res], tag=f"{tg}fc{h}")
            kb.dma("sync", fl_dap, ftmp.ap(0, 16, 0, 128), [ftmp.res],
                   [fl_res], tag=f"{tg}fd{h}")
            kb.dma("gpsimd", qaug.ap(64, 65, 0, T),
                   fl_dap.rearrange("a b -> (a b)"), [fl_res], [qaug.res],
                   tag=f"{tg}fl{h}")
            ftmp.free()
            coll.free()
            pt_bufs = [kb.ab.alloc(640, f"{tg}pt{j}") for j in range(7)]
            bsp = [kb.take_banks(2), kb.take_banks(2)]
            pvb2 = [kb.take_banks(1)[0], kb.take_banks(1)[0]]
            bcb = kb.take_banks(1)[0]
            rrs2 = [kb.af.alloc(128, f"{tg}rrs{h}{j}") for j in range(2)]
            pvs = kb.ab.alloc(128, f"{tg}pvs{h}")

            def do_pv(qt):
                pvb = pvb2[qt % 2]
                kts = list(range(max(0, qt - 4), qt + 1))
                n = len(kts)
                for j, kta in enumerate(kts):
                    src_pt = pt_bufs[(kta + 4) % 7]
                    qoff = 128 * (qt - kta)
                    kb.mm(kb.bank_ap(pvb, 0, 128, 0, 65),
                          vtok[h].ap(0, 128, kta * 65, kta * 65 + 65),
                          src_pt.ap(0, 128, qoff, qoff + 128),
                          j == 0, j == n - 1,
                          [vtok[h].res, src_pt.res], [kb.pbank[pvb]],
                          tag=f"{tg}pv{h}_{qt}_{kta}")
                self._pv_ln(pvb, rrs2[qt % 2], h, qt * 128, 128, tg)
                if qt >= 1:
                    self._pv_norm(pvb2[(qt - 1) % 2], bcb, rrs2[(qt - 1) % 2],
                                  pvs, att, h, (qt - 1) * 128, 128, tg)
                if qt == 15:
                    self._pv_norm(pvb2[15 % 2], bcb, rrs2[15 % 2], pvs, att,
                                  h, 15 * 128, 128, tg)

            for KT in range(4, 20):
                bse = bsp[KT % 2]
                qlo = 128 * (KT - 4)
                qhi = min(T, 128 * (KT + 1))
                W = qhi - qlo
                kt_ap = kaug.ap(0, 65, KT * 128, KT * 128 + 128)
                kb.mm(kb.wide_ap(bse, 0, 128), kt_ap,
                      qaug.ap(0, 65, qlo, qlo + 128),
                      True, False, [kaug.res, qaug.res],
                      [kb.pbank[bse[0]]], tag=f"{tg}st{h}_{KT}_0")
                kb.mm(kb.wide_ap(bse, 0, 128), idb, mlt,
                      False, True, [idb_res, mlt_res],
                      [kb.pbank[bse[0]]], tag=f"{tg}em1{h}_{KT}")
                if W > 128:
                    pw = min(W, 512)
                    kb.mm(kb.wide_ap(bse, 128, pw), kt_ap,
                          qaug.ap(0, 65, qlo + 128, qlo + pw),
                          True, True, [kaug.res, qaug.res],
                          [kb.pbank[bse[0]]], tag=f"{tg}st{h}_{KT}_1")
                if W == 640:
                    kb.mm(kb.wide_ap(bse, 512, 640), kt_ap,
                          qaug.ap(0, 65, qlo + 512, qlo + 640),
                          True, False, [kaug.res, qaug.res],
                          [kb.pbank[bse[1]]], tag=f"{tg}st{h}_{KT}_2")
                if W == 640:
                    kb.mm(kb.wide_ap(bse, 512, 640), idb, mge,
                          False, True, [idb_res, mge_res],
                          [kb.pbank[bse[1]]], tag=f"{tg}em2{h}_{KT}")
                pt = pt_bufs[KT % 7]
                kb.act(pt.ap(0, 128, 0, W), kb.wide_ap(bse, 0, W), AF.Exp,
                       [kb.pbank[bse[0]], kb.pbank[bse[1]]], [pt.res],
                       scale=8.0, tag=f"{tg}ex{h}_{KT}")
                if KT >= 5:
                    do_pv(KT - 5)
            do_pv(15)
            for b in rrs2:
                b.free()
            pvs.free()
            for b in pt_bufs:
                b.free()
            kaug.free()
            qaug.free()

    def _pv_ln(self, pvb, rrs, h, q0, qw, tg):
        kb = self.kb
        kb.act(rrs.ap(0, 1, 0, qw), kb.bank_ap(pvb, 0, qw, 64, 65), AF.Ln,
               [kb.pbank[pvb]], [rrs.res], tag=f"{tg}rl{h}_{q0}")
        kb.act(rrs.bap(0, 1, 2 * qw, 3 * qw), rrs.ap(0, 1, 0, qw), AF.Exp,
               [rrs.res], [rrs.res], scale=-1.0, tag=f"{tg}re{h}_{q0}")

    def _pv_norm(self, pvb, bcb, rrs, pvs, att, h, q0, qw, tg):
        kb = self.kb
        ones1b, o1b_res = kb.cap("ones1b")
        kb.mm(kb.bank_ap(bcb, 0, qw, 0, 64), ones1b[0:1, 0:64],
              rrs.bap(0, 1, 2 * qw, 3 * qw), True, True, [o1b_res, rrs.res],
              [kb.pbank[bcb]], tag=f"{tg}bc{h}_{q0}")
        kb.copy("vector", pvs.ap(0, 64, 0, qw),
                kb.bank_ap(pvb, 0, qw, 0, 64), [kb.pbank[pvb]], [pvs.res],
                tag=f"{tg}pe{h}_{q0}")
        r0 = 64 * (h % 2)
        kb.tt("vector", att[h // 2].ap(r0, r0 + 64, q0, q0 + qw),
              pvs.ap(0, 64, 0, qw), kb.bank_ap(bcb, 0, qw, 0, 64), ALU.mult,
              [pvs.res, kb.pbank[bcb]], [att[h // 2].res],
              tag=f"{tg}pn{h}_{q0}")


    def attn_b(self, x_bf, x3):
        kb = self.kb
        tg = "B"
        negm, rr = self.stats_rows(lambda kt: (x_bf[kt], False), tg + "st")
        r_b = self.bcast_row(rr, tg + "r")
        rr.free()
        negm_bf = kb.ab.alloc(T, tg + "nm")
        kb.copy("vector", negm_bf.ap(0, 1), negm.ap(0, 1), [negm.res],
                [negm_bf.res], tag=tg + "nmc")
        negm.free()

        ocol, ocol_res = kb.cap("ones_col")
        projs = {}
        for nm, mtot in (("b_dq", 256), ("b_dkv", 256), ("b_kr", 64)):
            dap, dres = kb.wbf[nm]
            slabs = []
            for kt in range(NKT):
                sb = kb.ab.alloc(mtot, f"{tg}{nm}{kt}")
                kb.dma(kb.wqueue(), sb.ap(0, 128, 0, mtot),
                       dap[kt * 128:(kt + 1) * 128, :], [dres], [sb.res],
                       tag=f"{tg}{nm}d{kt}")
                slabs.append(sb)
            wsum = kb.ab.alloc(mtot, f"{tg}{nm}ws")
            bk = kb.take_banks(1)[0]
            for kt in range(NKT):
                kb.mm(kb.bank_ap(bk, 0, mtot, 0, 1), ocol,
                      slabs[kt].ap(0, 128, 0, mtot), kt == 0, kt == NKT - 1,
                      [ocol_res, slabs[kt].res], [kb.pbank[bk]],
                      tag=f"{tg}{nm}ws{kt}")
            kb.copy("vector", wsum.ap(0, 1, 0, mtot),
                    kb.bank_ap(bk, 0, mtot, 0, 1), [kb.pbank[bk]],
                    [wsum.res], tag=f"{tg}{nm}wse")
            projs[nm] = (slabs, wsum)

        def run_proj(nm, mtot, evac):
            slabs, wsum = projs[nm]
            n_mt = (mtot + 127) // 128
            bks = kb.take_banks(2) + kb.take_banks(2)
            bi = 0
            for mi in range(n_mt):
                mw = min(128, mtot - mi * 128)
                for c in range(NCH):
                    bk = bks[bi % 4]
                    bi += 1
                    for kt in range(NKT):
                        kb.mm(kb.bank_ap(bk, 0, CW, 0, mw),
                              slabs[kt].ap(0, 128, mi * 128, mi * 128 + mw),
                              x_bf[kt].ap(0, 128, c * CW, (c + 1) * CW),
                              kt == 0, False,
                              [slabs[kt].res, x_bf[kt].res], [kb.pbank[bk]],
                              tag=f"{tg}{nm}m{mi}_{c}_{kt}")
                    kb.mm(kb.bank_ap(bk, 0, CW, 0, mw),
                          wsum.ap(0, 1, mi * 128, mi * 128 + mw),
                          negm_bf.ap(0, 1, c * CW, (c + 1) * CW), False, True,
                          [wsum.res, negm_bf.res], [kb.pbank[bk]],
                          tag=f"{tg}{nm}a{mi}_{c}")
                    evac(mi, c, kb.bank_ap(bk, 0, CW, 0, mw), kb.pbank[bk])

        cq = kb.ab.alloc(2 * T, tg + "cq")
        ckv = kb.ab.alloc(2 * T, tg + "ckv")
        krf = kb.af.alloc(T, tg + "krf")

        def ev_cq(mi, c, ps, pres):
            kb.tt("vector", cq.ap(0, 128, mi * T + c * CW, mi * T + (c + 1) * CW),
                  ps, r_b.ap(0, 128, c * CW, (c + 1) * CW), ALU.mult,
                  [pres, r_b.res], [cq.res], tag=f"{tg}cqe{mi}{c}")

        def ev_ckv(mi, c, ps, pres):
            kb.tt("vector", ckv.ap(0, 128, mi * T + c * CW, mi * T + (c + 1) * CW),
                  ps, r_b.ap(0, 128, c * CW, (c + 1) * CW), ALU.mult,
                  [pres, r_b.res], [ckv.res], tag=f"{tg}ckve{mi}{c}")

        def ev_kr(mi, c, ps, pres):
            kb.tt("vector", krf.ap(0, 64, c * CW, (c + 1) * CW), ps,
                  r_b.ap(0, 64, c * CW, (c + 1) * CW), ALU.mult,
                  [pres, r_b.res], [krf.res], tag=f"{tg}kre{c}")

        run_proj("b_dq", 256, ev_cq)
        run_proj("b_dkv", 256, ev_ckv)
        run_proj("b_kr", 64, ev_kr)
        for nm in ("b_dq", "b_dkv", "b_kr"):
            slabs, wsum = projs[nm]
            for sb in slabs:
                sb.free()
            wsum.free()
        negm_bf.free()
        r_b.free()
        for b in x_bf:
            b.free()

        cosb = kb.af.alloc(T, tg + "cos")
        sinb = kb.af.alloc(T, tg + "sin")
        bt_dap, bt_res = kb.dram["btab"]
        kb.dma("sync", cosb.ap(), bt_dap[0], [bt_res], [cosb.res],
               tag=tg + "tc")
        kb.dma("sync", sinb.ap(), bt_dap[1], [bt_res], [sinb.res],
               tag=tg + "ts")
        krb = kb.ab.alloc(T, tg + "krb")
        tmp = kb.af.alloc(T, tg + "krt")
        kb.tt("vector", tmp.ap(0, 16), krf.ap(32, 48), sinb.ap(32, 48),
              ALU.mult, [krf.res, sinb.res], [tmp.res], tag=tg + "kr1")
        kb.tt("vector", krb.ap(0, 16), krf.ap(0, 16), cosb.ap(0, 16),
              ALU.mult, [krf.res, cosb.res], [krb.res], tag=tg + "kr2")
        kb.tt("vector", krb.ap(0, 16), krb.ap(0, 16), tmp.ap(0, 16),
              ALU.subtract, [krb.res, tmp.res], [krb.res], tag=tg + "kr3")
        kb.tt("vector", tmp.ap(32, 48), krf.ap(0, 16), sinb.ap(0, 16),
              ALU.mult, [krf.res, sinb.res], [tmp.res], tag=tg + "kr4")
        kb.tt("vector", krb.ap(32, 48), krf.ap(32, 48), cosb.ap(32, 48),
              ALU.mult, [krf.res, cosb.res], [krb.res], tag=tg + "kr5")
        kb.tt("vector", krb.ap(32, 48), krb.ap(32, 48), tmp.ap(32, 48),
              ALU.add, [krb.res, tmp.res], [krb.res], tag=tg + "kr6")
        tmp.free()
        krf.free()

        sp2_ap, sp2_res = kb.dram["x_spill2"]
        for grp in range(2):
            att = [kb.ab.alloc(T, f"{tg}att{grp}{i}") for i in range(4)]
            pre_bwo = self.sm_prewarm("b_wo", 4, [128] * 8,
                                      dram_row0=grp * 512,
                                      tag=f"{tg}pwwo{grp}", chunks_len=4)
            self._attnb_group(grp, cq, ckv, krb, cosb, sinb, att, tg)
            xs = {}

            def evac_wo(mi, ci, ps_ap, pres, grp=grp):
                if ci == 0:
                    src_nm = "x_spill" if grp == 0 else "x_spill2"
                    xs[mi] = self.load_x_tile(src_nm, mi, f"{tg}xs{grp}{mi}")
                    x3[mi] = kb.af.alloc(T, f"{tg}x3_{grp}{mi}")
                kb.tt("vector", x3[mi].ap(0, 128, ci * CW, (ci + 1) * CW),
                      xs[mi].ap(0, 128, ci * CW, (ci + 1) * CW), ps_ap,
                      ALU.add, [xs[mi].res, pres], [x3[mi].res],
                      tag=f"{tg}wor{grp}{mi}_{ci}")
                if ci == NCH - 1:
                    xs[mi].free()
                    if grp == 0:
                        kb.dma("sync", sp2_ap[mi * 128:(mi + 1) * 128, :],
                               x3[mi].ap(), [x3[mi].res], [sp2_res],
                               tag=f"{tg}sp2_{mi}")
                        x3[mi].free()
                        x3[mi] = None

            self.stream_matmul("b_wo", 4, [128] * 8,
                               lambda kt, ci: (att[kt].ap(0, 128, ci * CW,
                                                          (ci + 1) * CW),
                                               [att[kt].res]),
                               evac_wo, dram_row0=grp * 512,
                               tag=f"{tg}wo{grp}", pre=pre_bwo)
            for b in att:
                b.free()
        cosb.free()
        sinb.free()
        krb.free()
        cq.free()
        ckv.free()

    def _attnb_group(self, grp, cq, ckv, krb, cosb, sinb, att, tg):
        kb = self.kb
        SCL = 1.0 / math.sqrt(96.0)
        CHI = math.exp(80.0 / math.sqrt(96.0))
        CLO = math.exp(-80.0 / math.sqrt(96.0))
        m01, m01_res = kb.cap("m01_ge")
        qasm = [kb.af.alloc(T // 2, f"{tg}qa{grp}{i}") for i in range(8)]
        kasm = [kb.af.alloc(T // 2, f"{tg}ka{grp}{i}") for i in range(8)]

        uq_dap, uq_res = kb.wbf["b_uq"]
        uk_dap, uk_res = kb.wbf["b_uk"]
        uv_dap, uv_res = kb.wbf["b_uv"]
        qr_dap, qr_res = kb.wbf["b_qr"]

        def up_proj(dap, dres, mcol0, n_mt, evac, tagp, src):
            slabs = []
            for kt in range(2):
                sb = kb.ab.alloc(n_mt * 128, f"{tagp}w{kt}")
                kb.dma(kb.wqueue(), sb.ap(0, 128, 0, n_mt * 128),
                       dap[kt * 128:(kt + 1) * 128,
                           mcol0:mcol0 + n_mt * 128],
                       [dres], [sb.res], tag=f"{tagp}d{kt}")
                slabs.append(sb)
            bks = kb.take_banks(2) + kb.take_banks(2)
            bi = 0
            for mi in range(n_mt):
                for c in range(NCH):
                    bk = bks[bi % 4]
                    bi += 1
                    for kt in range(2):
                        kb.mm(kb.bank_ap(bk),
                              slabs[kt].ap(0, 128, mi * 128, (mi + 1) * 128),
                              src.ap(0, 128, kt * T + c * CW,
                                     kt * T + (c + 1) * CW),
                              kt == 0, kt == 1,
                              [slabs[kt].res, src.res], [kb.pbank[bk]],
                              tag=f"{tagp}m{mi}{c}{kt}")
                    evac(mi, c, bk, kb.pbank[bk])
            for sb in slabs:
                sb.free()

        def ev_uq2(mi, c, bk, pres):
            for hh in range(2):
                a = qasm[2 * mi + hh]
                kb.copy("scalar", a.bap(0, 64, c * CW, (c + 1) * CW),
                        kb.bank_ap(bk, 0, CW, 64 * hh, 64 * hh + 64),
                        [pres], [a.res], tag=f"{tg}uqe{grp}{mi}{c}{hh}")

        def ev_uk2(mi, c, bk, pres):
            for hh in range(2):
                a = kasm[2 * mi + hh]
                kb.copy("scalar", a.bap(0, 64, c * CW, (c + 1) * CW),
                        kb.bank_ap(bk, 0, CW, 64 * hh, 64 * hh + 64),
                        [pres], [a.res], tag=f"{tg}uke{grp}{mi}{c}{hh}")

        up_proj(uq_dap, uq_res, grp * 512, 4, ev_uq2, f"{tg}uq{grp}", cq)
        up_proj(uk_dap, uk_res, grp * 512, 4, ev_uk2, f"{tg}uk{grp}", ckv)

        v_fm = [kb.ab.alloc(T, f"{tg}v{grp}{i}") for i in range(4)]

        def ev_uv(mi, c, bk, pres):
            kb.copy("scalar", v_fm[mi].ap(0, 128, c * CW, (c + 1) * CW),
                    kb.bank_ap(bk), [pres], [v_fm[mi].res],
                    tag=f"{tg}uve{grp}{mi}{c}")

        up_proj(uv_dap, uv_res, grp * 512, 4, ev_uv, f"{tg}uv{grp}", ckv)

        vtok = [kb.ab.alloc(16 * 65 + 16, f"{tg}vt{grp}{h}") for h in range(8)]
        for h in range(8):
            kb.memset("vector", vtok[h].ap(0, 128, 0, 16 * 65), 1.0,
                      [vtok[h].res], tag=f"{tg}vz{grp}{h}")
        bks = kb.take_banks(2)
        for i in range(4):
            for tc in range(16):
                bk = bks[tc % 2]
                kb.transp(kb.bank_bap(bk, 0, 128),
                          v_fm[i].ap(0, 128, tc * 128, (tc + 1) * 128),
                          [v_fm[i].res], [kb.pbank[bk]],
                          tag=f"{tg}vt{grp}{i}_{tc}", bf16=True)
                for hh in range(2):
                    h = 2 * i + hh
                    kb.copy("vector",
                            vtok[h].ap(0, 128, tc * 65, tc * 65 + 64),
                            kb.bank_bap(bk, hh * 64, hh * 64 + 64, 0, 128),
                            [kb.pbank[bk]], [vtok[h].res],
                            tag=f"{tg}ve{grp}{h}_{tc}")
        for b in v_fm:
            b.free()

        qr_top = kb.af.alloc(T, f"{tg}qrt{grp}")
        qr_bot = kb.af.alloc(T, f"{tg}qrb{grp}")

        def ev_qr(mi, c, bk, pres):
            dst = qr_top if mi == 0 else qr_bot
            kb.copy("scalar", dst.ap(0, 128, c * CW, (c + 1) * CW),
                    kb.bank_ap(bk), [pres], [dst.res],
                    tag=f"{tg}qre{grp}{mi}{c}")

        def up_proj_qr():
            slabs = []
            for kt in range(2):
                sb = kb.ab.alloc(256, f"{tg}qr{grp}w{kt}")
                for j, c0 in enumerate((grp * 128, 256 + grp * 128)):
                    kb.dma(kb.wqueue(), sb.ap(0, 128, j * 128, (j + 1) * 128),
                           qr_dap[kt * 128:(kt + 1) * 128, c0:c0 + 128],
                           [qr_res], [sb.res], tag=f"{tg}qrd{grp}{kt}{j}")
                slabs.append(sb)
            bks2 = kb.take_banks(2)
            for mi in range(2):
                for c in range(NCH):
                    bk = bks2[c % 2]
                    for kt in range(2):
                        kb.mm(kb.bank_ap(bk),
                              slabs[kt].ap(0, 128, mi * 128, (mi + 1) * 128),
                              cq.ap(0, 128, kt * T + c * CW,
                                    kt * T + (c + 1) * CW),
                              kt == 0, kt == 1, [slabs[kt].res, cq.res],
                              [kb.pbank[bk]], tag=f"{tg}qrm{grp}{mi}{c}{kt}")
                    ev_qr(mi, c, bk, kb.pbank[bk])
            for sb in slabs:
                sb.free()

        up_proj_qr()
        rt = kb.af.alloc(T, f"{tg}rt{grp}")
        rb_ = kb.af.alloc(T, f"{tg}rb{grp}")
        kb.tt("vector", rt.ap(), qr_top.ap(), cosb.ap(), ALU.mult,
              [qr_top.res, cosb.res], [rt.res], tag=f"{tg}ro1{grp}")
        kb.tt("vector", rb_.ap(), qr_top.ap(), sinb.ap(), ALU.mult,
              [qr_top.res, sinb.res], [rb_.res], tag=f"{tg}ro2{grp}")
        qr_top.free()
        t1 = kb.af.alloc(T, f"{tg}t1{grp}")
        kb.tt("vector", t1.ap(), qr_bot.ap(), sinb.ap(), ALU.mult,
              [qr_bot.res, sinb.res], [t1.res], tag=f"{tg}ro3{grp}")
        kb.tt("vector", rt.ap(), rt.ap(), t1.ap(), ALU.subtract,
              [rt.res, t1.res], [rt.res], tag=f"{tg}ro4{grp}")
        kb.tt("vector", t1.ap(), qr_bot.ap(), cosb.ap(), ALU.mult,
              [qr_bot.res, cosb.res], [t1.res], tag=f"{tg}ro5{grp}")
        kb.tt("vector", rb_.ap(), rb_.ap(), t1.ap(), ALU.add,
              [rb_.res, t1.res], [rb_.res], tag=f"{tg}ro6{grp}")
        t1.free()
        qr_bot.free()
        rt_bf = kb.ab.alloc(T, f"{tg}rtb{grp}")
        rb_bf = kb.ab.alloc(T, f"{tg}rbb{grp}")
        kb.copy("vector", rt_bf.ap(), rt.ap(), [rt.res], [rt_bf.res],
                tag=f"{tg}rtc{grp}")
        kb.copy("vector", rb_bf.ap(), rb_.ap(), [rb_.res], [rb_bf.res],
                tag=f"{tg}rbc{grp}")
        rt.free()
        rb_.free()
        for h in range(8):
            kb.dma("scalar", qasm[h].bap(64, 80, 0, T),
                   rt_bf.ap(h * 16, h * 16 + 16, 0, T), [rt_bf.res],
                   [qasm[h].res], tag=f"{tg}qat{grp}{h}")
            kb.dma("scalar", qasm[h].bap(80, 96, 0, T),
                   rb_bf.ap(h * 16, h * 16 + 16, 0, T), [rb_bf.res],
                   [qasm[h].res], tag=f"{tg}qab{grp}{h}")
            kb.dma("scalar", kasm[h].bap(64, 80, 0, T),
                   krb.ap(0, 16, 0, T), [krb.res], [kasm[h].res],
                   tag=f"{tg}kat{grp}{h}")
            kb.dma("scalar", kasm[h].bap(80, 96, 0, T),
                   krb.ap(32, 48, 0, T), [krb.res], [kasm[h].res],
                   tag=f"{tg}kab{grp}{h}")
        rt_bf.free()
        rb_bf.free()

        pbuf = [kb.ab.alloc(1024, f"{tg}p{grp}{j}") for j in range(3)]
        pvs = kb.ab.alloc(CW, f"{tg}pvs{grp}")
        rrs2 = [kb.af.alloc(CW, f"{tg}rrs{grp}{j}") for j in range(2)]
        ones1b_, o1b_res_ = kb.cap("ones1b")
        pending = []

        def flush_pending():
            while pending:
                fin = pending.pop(0)
                fin()

        pvb2 = [kb.take_banks(1)[0], kb.take_banks(1)[0]]
        sbk = kb.take_banks(2)
        bcb_ded = kb.take_banks(1)[0]
        for hh in range(8):
            h = 8 * grp + hh
            qa, ka = qasm[hh], kasm[hh]
            for qc in range(NCH):
                pvb = pvb2[qc % 2]
                n_kt = 4 * qc + 4

                def do_pv_b(kt, n_kt=n_kt, pvb=pvb):
                    pb = pbuf[kt % 3]
                    kb.mm(kb.bank_ap(pvb, 0, CW, 0, 65),
                          vtok[hh].ap(0, 128, kt * 65, kt * 65 + 65),
                          pb.ap(0, 128, 0, CW), kt == 0, kt == n_kt - 1,
                          [vtok[hh].res, pb.res], [kb.pbank[pvb]],
                          tag=f"{tg}pv{h}_{qc}_{kt}")

                for kt in range(n_kt):
                    bk = sbk[kt % 2]
                    kb.mm(kb.bank_ap(bk),
                          ka.bap(0, 96, kt * 128, (kt + 1) * 128),
                          qa.bap(0, 96, qc * CW, (qc + 1) * CW), True, True,
                          [ka.res, qa.res], [kb.pbank[bk]],
                          tag=f"{tg}st{h}_{qc}_{kt}")
                    if kt == 1:
                        flush_pending()
                    pb = pbuf[kt % 3]
                    kb.act(pb.ap(0, 128, 0, CW), kb.bank_ap(bk), AF.Exp,
                           [kb.pbank[bk]], [pb.res], scale=SCL,
                           tag=f"{tg}ex{h}_{qc}_{kt}")
                    kb.ts("gpsimd", pb.ap(0, 128, 0, CW),
                          pb.ap(0, 128, 0, CW), CHI, CLO, ALU.min, ALU.max,
                          [pb.res], [pb.res], tag=f"{tg}cl{h}_{qc}_{kt}")
                    dq0 = kt * 128 - qc * CW
                    if 0 <= dq0:
                        if dq0 > 0:
                            kb.memset("gpsimd", pb.ap(0, 128, 0, dq0), 0.0,
                                      [pb.res], tag=f"{tg}z{h}_{qc}_{kt}")
                        kb.tt("gpsimd", pb.ap(0, 128, dq0, dq0 + 128),
                              pb.ap(0, 128, dq0, dq0 + 128), m01, ALU.mult,
                              [pb.res, m01_res], [pb.res],
                              tag=f"{tg}dm{h}_{qc}_{kt}")
                    if kt >= 2:
                        do_pv_b(kt - 2)
                do_pv_b(n_kt - 2)
                do_pv_b(n_kt - 1)

                rrs = rrs2[qc % 2]
                kb.act(rrs.ap(0, 1, 0, CW),
                       kb.bank_ap(pvb, 0, CW, 64, 65), AF.Ln,
                       [kb.pbank[pvb]], [rrs.res], tag=f"{tg}rl{h}_{qc}")
                kb.act(rrs.bap(0, 1, 2 * CW, 3 * CW), rrs.ap(0, 1, 0, CW),
                       AF.Exp, [rrs.res], [rrs.res], scale=-1.0,
                       tag=f"{tg}re{h}_{qc}")

                def finish(pvb=pvb, qc=qc, hh=hh, h=h, rrs=rrs):
                    bcb = bcb_ded
                    kb.mm(kb.bank_ap(bcb, 0, CW, 0, 64), ones1b_[0:1, 0:64],
                          rrs.bap(0, 1, 2 * CW, 3 * CW), True, True,
                          [o1b_res_, rrs.res], [kb.pbank[bcb]],
                          tag=f"{tg}bc{h}_{qc}")
                    kb.copy("vector", pvs.ap(0, 64, 0, CW),
                            kb.bank_ap(pvb, 0, CW, 0, 64), [kb.pbank[pvb]],
                            [pvs.res], tag=f"{tg}pe{h}_{qc}")
                    r0 = 64 * (hh % 2)
                    kb.tt("vector", att[hh // 2].ap(r0, r0 + 64, qc * CW,
                                                    (qc + 1) * CW),
                          pvs.ap(0, 64, 0, CW), kb.bank_ap(bcb, 0, CW, 0, 64),
                          ALU.mult, [pvs.res, kb.pbank[bcb]],
                          [att[hh // 2].res], tag=f"{tg}pn{h}_{qc}")
                pending.append(finish)
        flush_pending()
        pvs.free()
        for b in rrs2:
            b.free()
        for b in pbuf:
            b.free()
        for b in qasm + kasm + vtok:
            b.free()



def build_full(weights_shapes_only, stage=99):
    consts = host_constants()
    kb = KB(weights_shapes_only, consts)
    nc = kb.nc
    ft = nc.dram_tensor("ftab", [4, 128, T], BF16, kind="Internal")
    kb.dram["ftab"] = (ft.ap(), Res("ftab"))
    bt = nc.dram_tensor("btab", [2, 128, T], F32, kind="Internal")
    kb.dram["btab"] = (bt.ap(), Res("btab"))
    m = Model(kb)
    kb.make_bf16_twins(["a_inproj", "a_gate", "a_outproj", "a_coupling_bd",
                        "a_w1", "a_w2", "f_wqkv", "f_wo", "f_w1", "f_w2",
                        "b_dq", "b_dkv", "b_kr", "b_uq", "b_uk", "b_uv",
                        "b_qr", "b_wo"], weights_shapes_only)

    xout_ap, xout_res = kb.dram["x_out"]
    spill_ap, spill_res = kb.dram["x_spill"]

    def store_out(bufs):
        for kt in range(NKT):
            kb.dma("sync" if kt % 2 else "scalar",
                   xout_ap[kt * 128:(kt + 1) * 128, :], bufs[kt].ap(),
                   [bufs[kt].res], [xout_res], tag=f"out{kt}")

    xls = [m.load_x_tile("x_in", kt, f"txa{kt}") for kt in range(NKT)]
    negm, rr = m.stats_rows(lambda kt: (xls[kt], False), "tln0", f32=True)
    negm_b = m.bcast_row(negm, "tl0m")
    r_b = m.bcast_row(rr, "tl0r")
    negm.free()
    rr.free()
    xa = []
    for kt in range(NKT):
        b = kb.af.alloc(T, f"xa{kt}")
        kb.tt("vector", b.ap(), xls[kt].ap(), negm_b.ap(), ALU.add,
              [xls[kt].res, negm_b.res], [b.res], tag=f"xa_a{kt}")
        kb.tt("vector", b.ap(), b.ap(), r_b.ap(), ALU.mult,
              [b.res, r_b.res], [b.res], tag=f"xa_m{kt}")
        xls[kt].free()
        xa.append(b)
    negm_b.free()
    r_b.free()
    if stage == 0:
        store_out(xa)
        return kb, m
    m.stage = stage
    xa_bf = [m.cast_tile(xa[kt], f"xabf{kt}") for kt in range(NKT)]
    xa = m.attn_a(xa, xa_bf)
    if m.early_out is not None:
        for i, (b, p0, p1) in enumerate(m.early_out[:NKT]):
            if b.arena is kb.ab:
                f = kb.af.alloc(b.w, f"eo{i}")
                kb.copy("vector", f.ap(p0, p1, 0, b.w), b.ap(p0, p1, 0, b.w),
                        [b.res], [f.res], tag=f"eo{i}")
                kb.dma("sync", xout_ap[i * 128:(i + 1) * 128, 0:min(f.w, T)],
                       f.ap(0, 128, 0, min(f.w, T)), [f.res], [xout_res],
                       tag=f"eout{i}")
                f.free()
            else:
                kb.dma("sync", xout_ap[i * 128:(i + 1) * 128, 0:min(b.w, T)],
                       b.ap(0, 128, 0, min(b.w, T)), [b.res], [xout_res],
                       tag=f"eout{i}")
        return kb, m

    x1_bf = []
    for w0 in range(0, NKT, 4):
        xw = [m.load_x_tile("x_in", kt, f"x1l{kt}")
              for kt in range(w0, w0 + 4)]
        for j, kt in enumerate(range(w0, w0 + 4)):
            xl = xw[j]
            kb.tt("vector", xa[kt].ap(), xl.ap(), xa[kt].ap(), ALU.add,
                  [xl.res, xa[kt].res], [xa[kt].res], tag=f"x1a{kt}")
            xl.free()
            kb.dma("sync", spill_ap[kt * 128:(kt + 1) * 128, :], xa[kt].ap(),
                   [xa[kt].res], [spill_res], tag=f"sp1_{kt}")
            x1_bf.append(m.cast_tile(xa[kt], f"x1b{kt}",
                                     eng="scalar" if kt % 2 else "vector"))
    if stage == 1:
        store_out(xa)
        return kb, m

    m.build_ffn_tables()
    m.build_attnb_tables()

    negm, rr = m.stats_rows(lambda kt: (x1_bf[kt], False), "tln1")
    negm_b = m.bcast_row(negm, "tl1m")
    r_b = m.bcast_row(rr, "tl1r")
    negm.free()
    rr.free()
    for b in x1_bf:
        b.free()
    y = []
    for kt in range(NKT):
        b = kb.af.alloc(T, f"y{kt}")
        kb.tt("vector", b.ap(), xa[kt].ap(), negm_b.ap(), ALU.add,
              [xa[kt].res, negm_b.res], [b.res], tag=f"y_a{kt}")
        kb.tt("vector", b.ap(), b.ap(), r_b.ap(), ALU.mult,
              [b.res, r_b.res], [b.res], tag=f"y_m{kt}")
        xa[kt].free()
        y.append(b)
    negm_b.free()
    r_b.free()
    m.ffn_layer(y, 0, 0)
    if m.early_out is not None:
        for i, (b, p0, p1) in enumerate(m.early_out[:NKT]):
            if b.arena is kb.ab:
                f = kb.af.alloc(b.w, f"feo{i}")
                kb.copy("vector", f.ap(p0, p1, 0, b.w), b.ap(p0, p1, 0, b.w),
                        [b.res], [f.res], tag=f"feo{i}")
                kb.dma("sync", xout_ap[i * 128:(i + 1) * 128, 0:min(f.w, T)],
                       f.ap(0, 128, 0, min(f.w, T)), [f.res], [xout_res],
                       tag=f"feout{i}")
                f.free()
            else:
                kb.dma("sync", xout_ap[i * 128:(i + 1) * 128, 0:min(b.w, T)],
                       b.ap(0, 128, 0, min(b.w, T)), [b.res], [xout_res],
                       tag=f"feout{i}")
        return kb, m
    m.ffn_layer(y, 1, 0)

    x2_bf = []
    for w0 in range(0, NKT, 4):
        xw = [m.load_x_tile("x_spill", kt, f"x2l{kt}")
              for kt in range(w0, w0 + 4)]
        for j, kt in enumerate(range(w0, w0 + 4)):
            xl = xw[j]
            kb.tt("vector", y[kt].ap(), xl.ap(), y[kt].ap(), ALU.add,
                  [xl.res, y[kt].res], [y[kt].res], tag=f"x2a{kt}")
            xl.free()
            kb.dma("sync" if kt % 2 else "scalar",
                   spill_ap[kt * 128:(kt + 1) * 128, :], y[kt].ap(),
                   [y[kt].res], [spill_res], tag=f"sp2_{kt}")
            x2_bf.append(m.cast_tile(y[kt], f"x2b{kt}",
                                         eng="scalar" if kt % 2 else "vector"))
    if stage == 2:
        store_out(y)
        return kb, m
    for b in y:
        b.free()

    x3 = [None] * NKT
    m.attn_b(x2_bf, x3)
    x3_bf = []
    for kt in range(NKT):
        kb.dma("sync" if kt % 2 else "scalar",
               spill_ap[kt * 128:(kt + 1) * 128, :], x3[kt].ap(),
               [x3[kt].res], [spill_res], tag=f"sp3_{kt}")
        x3_bf.append(m.cast_tile(x3[kt], f"x3b{kt}",
                                     eng="scalar" if kt % 2 else "vector"))
    if stage == 3:
        store_out(x3)
        return kb, m

    negm, rr = m.stats_rows(lambda kt: (x3_bf[kt], False), "tln3")
    negm_b = m.bcast_row(negm, "tl3m")
    r_b = m.bcast_row(rr, "tl3r")
    negm.free()
    rr.free()
    for b in x3_bf:
        b.free()
    y2 = []
    for kt in range(NKT):
        b = kb.af.alloc(T, f"y2_{kt}")
        kb.tt("vector", b.ap(), x3[kt].ap(), negm_b.ap(), ALU.add,
              [x3[kt].res, negm_b.res], [b.res], tag=f"y2a{kt}")
        kb.tt("vector", b.ap(), b.ap(), r_b.ap(), ALU.mult,
              [b.res, r_b.res], [b.res], tag=f"y2m{kt}")
        x3[kt].free()
        y2.append(b)
    negm_b.free()
    r_b.free()
    m.ffn_layer(y2, 0, 1)
    m.ffn_layer(y2, 1, 1)

    for w0 in range(0, NKT, 4):
        xw = [m.load_x_tile("x_spill", kt, f"fo{kt}")
              for kt in range(w0, w0 + 4)]
        for j, kt in enumerate(range(w0, w0 + 4)):
            xl = xw[j]
            kb.tt("vector", y2[kt].ap(), xl.ap(), y2[kt].ap(), ALU.add,
                  [xl.res, y2[kt].res], [y2[kt].res], tag=f"fo_a{kt}")
            xl.free()
    store_out(y2)
    return kb, m


_CACHE = {}


def _alloc_sems(kb):
    sems = {}
    st = kb.st
    for e in COMPUTE_ENGS:
        sems[("c", e)] = st.enter_context(kb.nc.semaphore("sc_" + e))
        if kb.s.dseq[e]:
            for i in range(min(kb.s.dseq[e], DMA_POOL)):
                sems[("q", e, i)] = st.enter_context(
                    kb.nc.semaphore(f"sq_{e}{i}"))
    return sems


def _get_program(stage):
    if stage not in _CACHE:
        dummy_shapes = None
        raise RuntimeError("use _build_with_weights")
    return _CACHE[stage]


def kernel(**inputs):
    stage = DEV_STAGE
    w = prepare_weights(inputs)
    key = ("prog", stage)
    if key not in _CACHE:
        kb, m = build_full(w, stage)
        sems = _alloc_sems(kb)
        with kb.nc.Block() as block:
            kb.s.emit(kb.nc, block, sems)
        _CACHE[key] = kb.nc
    nc = _CACHE[key]

    x = np.asarray(inputs["L0_attn_a_x"], dtype=np.float32)
    in_maps = []
    for c in range(NCORES):
        im = {"x_in": np.ascontiguousarray(x[c].T)}
        for name, arr in w.items():
            im["w_" + name] = arr
        in_maps.append(im)
    res = run_bass_kernel_spmd(nc, in_maps, core_ids=list(range(NCORES)))
    out = np.stack([np.asarray(r["x_out"]).T for r in res.results])
    return np.ascontiguousarray(out.astype(np.float32))


# version 24
# speedup vs baseline: 1.0421x; 1.0307x over previous
import contextlib
import math

import numpy as np
import ml_dtypes

import concourse.bass as bass
import concourse.mybir as mybir
from concourse.bass_utils import run_bass_kernel_spmd

F32 = mybir.dt.float32
BF16 = mybir.dt.bfloat16
AF = mybir.ActivationFunctionType
ALU = mybir.AluOpType
AX = mybir.AxisListType

NCORES = 8
T = 2048
D = 1024
NKT = 8
NCH = 4
CW = 512

NEG = -1.0e6
LN_EPS = 1e-5
SIM_SAFE = False


COMPUTE_ENGS = ("tensor", "vector", "scalar", "gpsimd", "sync")
DMA_POOL = 10


class Res:
    __slots__ = ("name", "w", "rs", "init_deps")

    def __init__(self, name="r"):
        self.name = name
        self.w = None
        self.rs = []
        self.init_deps = ()


class _Op:
    __slots__ = ("eng", "fn", "waits", "inc", "is_dma", "ev", "tag",
                 "dma_slot", "dma_epoch", "always_sem")

    def __init__(self, eng, fn, is_dma, tag):
        self.eng = eng
        self.fn = fn
        self.waits = []
        self.inc = False
        self.is_dma = is_dma
        self.ev = None
        self.tag = tag
        self.dma_slot = 0
        self.dma_epoch = 0
        self.always_sem = False


class Sched:
    def __init__(self):
        self.ops = {e: [] for e in COMPUTE_ENGS}
        self.dseq = {e: 0 for e in COMPUTE_ENGS}
        self.all_ops = []

    def op(self, eng, fn, reads=(), writes=(), sreads=(), tag=""):
        o = _Op(eng, fn, False, tag)
        idx = len(self.ops[eng])
        ev = ("c", eng, idx, o)
        o.ev = ev
        self._edges(o, ev, reads, writes, sreads)
        self.ops[eng].append(o)
        self.all_ops.append(o)
        return o

    def dma(self, eng, fn, reads=(), writes=(), tag=""):
        o = _Op(eng, fn, True, tag)
        idx0 = self.dseq[eng]
        self.dseq[eng] += 1
        o.dma_slot = idx0 % DMA_POOL
        o.dma_epoch = idx0 // DMA_POOL + 1
        ev = ("d", eng, idx0, o)
        o.ev = ev
        self._edges(o, ev, reads, writes, ())
        self.ops[eng].append(o)
        self.all_ops.append(o)
        return o

    @staticmethod
    def _need_wait(my_eng, ev, forced):
        kind, eng, idx, op = ev
        if (kind == "c" and eng == my_eng and not forced and not SIM_SAFE
                and not op.always_sem):
            return False
        return True

    def _edges(self, o, ev, reads, writes, sreads):
        for r in reads:
            if r.w is not None and self._need_wait(o.eng, r.w, False):
                o.waits.append(r.w)
            r.rs.append(ev)
        for r in sreads:
            if r.w is not None and self._need_wait(o.eng, r.w, True):
                o.waits.append(r.w)
            r.rs.append(ev)
        for r in writes:
            if r.w is not None and self._need_wait(o.eng, r.w, False):
                o.waits.append(r.w)
            for rev in r.rs:
                if rev is not ev and self._need_wait(o.eng, rev, False):
                    o.waits.append(rev)
            for rev in r.init_deps:
                if rev is not ev and self._need_wait(o.eng, rev, False):
                    o.waits.append(rev)
            r.init_deps = ()
            r.w = ev
            r.rs = []

    def emit(self, nc, block, sems, final_waits=True):
        for o in self.all_ops:
            for ev in o.waits:
                kind, eng, idx, op = ev
                if kind == "c":
                    op.inc = True
        inc_at = {}
        for e in COMPUTE_ENGS:
            c = 0
            ci = 0
            for o in self.ops[e]:
                if not o.is_dma:
                    if o.inc:
                        c += 1
                    inc_at[(e, ci)] = c
                    ci += 1
        cidx_of = {}
        for e in COMPUTE_ENGS:
            ci = 0
            for i, o in enumerate(self.ops[e]):
                if not o.is_dma:
                    cidx_of[id(o)] = ci
                    ci += 1

        def run_stream(eng_obj, e):
            watermark = {}

            def do_wait(key, sem, val):
                if watermark.get(key, -1) < val:
                    eng_obj.wait_ge(sem, val)
                    watermark[key] = val

            for o in self.ops[e]:
                for ev in o.waits:
                    kind, peng, pidx, pop = ev
                    if kind == "c":
                        sem = sems[("c", peng)]
                        val = inc_at[(peng, cidx_of[id(pop)])]
                        do_wait(("c", peng), sem, val)
                    else:
                        sem = sems[("q", peng, pop.dma_slot)]
                        do_wait(("q", peng, pop.dma_slot), sem,
                                16 * pop.dma_epoch)
                if o.is_dma:
                    if o.dma_epoch > 1:
                        do_wait(("q", e, o.dma_slot),
                                sems[("q", e, o.dma_slot)],
                                16 * (o.dma_epoch - 1))
                    ins = o.fn(eng_obj)
                    ins.then_inc(sems[("q", e, o.dma_slot)], 16)
                else:
                    ins = o.fn(eng_obj)
                    if o.inc:
                        ins.then_inc(sems[("c", e)], 1)

        def run_sync(s):
            run_stream(s, "sync")
            if final_waits:
                for e in COMPUTE_ENGS:
                    n = self.dseq[e]
                    for slot in range(min(n, DMA_POOL)):
                        last_epoch = (n - 1 - slot) // DMA_POOL + 1
                        s.wait_ge(sems[("q", e, slot)], 16 * last_epoch)

        block.tensor(lambda t: run_stream(t, "tensor"))
        block.vector(lambda v: run_stream(v, "vector"))
        block.scalar(lambda s: run_stream(s, "scalar"))
        block.gpsimd(lambda g: run_stream(g, "gpsimd"))
        block.sync(run_sync)



class Buf:
    __slots__ = ("arena", "c0", "w", "res")

    def __init__(self, arena, c0, w, res):
        self.arena = arena
        self.c0 = c0
        self.w = w
        self.res = res

    def ap(self, p0=0, p1=128, c0=0, c1=None):
        if c1 is None:
            c1 = self.w
        assert 0 <= c0 < c1 <= self.w
        return self.arena.t[p0:p1, self.c0 + c0:self.c0 + c1]

    def bap(self, p0=0, p1=128, cb0=0, cb1=None):
        if cb1 is None:
            cb1 = 2 * self.w
        assert cb0 % 2 == 0 and cb1 % 2 == 0
        return self.arena.t[p0:p1,
                            self.c0 + cb0 // 2:self.c0 + cb1 // 2].bitcast(BF16)

    def free(self):
        self.arena.free(self)


class Arena:
    GRAN = 16

    def __init__(self, tensor, cols, name):
        self.t = tensor
        self.cols = cols
        self.name = name
        self.free_list = [(0, cols)]
        self.live = {}
        self.hwm = 0
        self.ghosts = {}

    def alloc(self, w, tag=""):
        w = (w + 15) & ~15
        best = None
        for i, (c0, sz) in enumerate(self.free_list):
            if sz >= w and (best is None or sz < self.free_list[best][1]):
                best = i
        if best is None:
            raise RuntimeError(f"arena {self.name}: no space for {w} ({tag}); "
                               f"free={self.free_list}")
        c0, sz = self.free_list[best]
        self.free_list[best] = (c0 + w, sz - w)
        if self.free_list[best][1] == 0:
            del self.free_list[best]
        r = Res(f"{self.name}:{tag}@{c0}")
        deps = []
        for g in range(c0 // self.GRAN, (c0 + w) // self.GRAN):
            evs = self.ghosts.pop(g, None)
            if evs:
                deps.extend(evs)
        if deps:
            seen = set()
            uniq = []
            for e in deps:
                if id(e) not in seen:
                    seen.add(id(e))
                    uniq.append(e)
            r.init_deps = tuple(uniq)
        self.live[c0] = (w, r)
        self.hwm = max(self.hwm, max((a + b for a, b in
                                      [(k, v[0]) for k, v in self.live.items()]),
                                     default=0))
        return Buf(self, c0, w, r)

    def free(self, buf):
        w, r = self.live.pop(buf.c0)
        evs = tuple(([r.w] if r.w is not None else []) + r.rs +
                    list(r.init_deps))
        for g in range(buf.c0 // self.GRAN, (buf.c0 + w) // self.GRAN):
            self.ghosts[g] = evs
        self.free_list.append((buf.c0, w))
        self.free_list.sort()
        out = []
        for c0, sz in self.free_list:
            if out and out[-1][0] + out[-1][1] == c0:
                out[-1] = (out[-1][0], out[-1][1] + sz)
            else:
                out.append((c0, sz))
        self.free_list = out



def _perm_qk_cols():
    idx = np.zeros(512, np.int64)
    for half in range(2):
        for h in range(8):
            for i in range(32):
                idx[half * 256 + h * 32 + i] = h * 64 + half * 32 + i
    return idx


def _perm_w1_cols():
    idx = []
    for blk in range(21):
        idx.extend(range(128 * blk, 128 * blk + 128))
        idx.extend(range(2730 + 128 * blk, 2730 + 128 * blk + 128))
    idx.extend(range(2688, 2730))
    idx.extend(range(2730 + 2688, 5460))
    return np.array(idx, np.int64)


def _perm_rope_b(n_pairs, n_heads):
    idx = np.zeros(2 * n_pairs * n_heads, np.int64)
    for h in range(n_heads):
        for p in range(n_pairs):
            idx[h * n_pairs + p] = h * 2 * n_pairs + 2 * p
            idx[n_pairs * n_heads + h * n_pairs + p] = h * 2 * n_pairs + 2 * p + 1
    return idx


def prepare_weights(inp):
    C = lambda a: np.ascontiguousarray(np.asarray(a, dtype=np.float32))
    w = {}
    w["a_inproj"] = C(np.asarray(inp["L0_attn_a_in_proj_w"]).T)
    w["a_gate"] = C(np.asarray(inp["L0_attn_a_gate_w"]).T)
    w["a_outproj"] = C(np.asarray(inp["L0_attn_a_out_proj_w"]).T)
    w["a_w1"] = C(np.asarray(inp["L0_attn_a_mlp_w1"]).T)
    w["a_w2"] = C(np.asarray(inp["L0_attn_a_mlp_w2"]).T)
    cp = np.asarray(inp["L0_attn_a_coupling"])
    bd = np.zeros((64, 64), np.float32)
    for h in range(4):
        bd[16 * h:16 * h + 16, 16 * h:16 * h + 16] = cp[h].T
    w["a_coupling_bd"] = bd
    for k in ("log_decay", "frequency"):
        w["a_" + k] = C(np.asarray(inp["L0_attn_a_" + k]).reshape(1, -1))
    for k in ("gate_b", "n1_g", "n1_b", "n2_g", "n2_b", "mlp_b2"):
        w["a_" + k] = C(np.asarray(inp["L0_attn_a_" + k]).reshape(-1, 128).T)
    w["a_mlp_b1"] = C(np.asarray(inp["L0_attn_a_mlp_b1"]).reshape(-1, 128).T)
    w["b_dq"] = C(inp["L0_attn_b_W_DQ"])
    w["b_uq"] = C(inp["L0_attn_b_W_UQ"])
    w["b_qr"] = C(np.asarray(inp["L0_attn_b_W_QR"])[:, _perm_rope_b(16, 16)])
    w["b_dkv"] = C(inp["L0_attn_b_W_DKV"])
    w["b_uk"] = C(inp["L0_attn_b_W_UK"])
    w["b_uv"] = C(inp["L0_attn_b_W_UV"])
    kr = np.asarray(inp["L0_attn_b_W_KR"])
    krp = np.zeros((1024, 64), np.float32)
    krp[:, 0:16] = kr[:, 0::2]
    krp[:, 32:48] = kr[:, 1::2]
    w["b_kr"] = C(krp)
    w["b_wo"] = C(inp["L0_attn_b_W_O"])
    pqk = _perm_qk_cols()
    wqkv = np.asarray(inp["L0_ffn_wqkv"], dtype=np.float32)
    wq = wqkv[:, :, 0:512][:, :, pqk]
    wk = wqkv[:, :, 512:1024][:, :, pqk]
    wv = wqkv[:, :, 1024:1536]
    w["f_wqkv"] = C(np.concatenate([wq, wk, wv], axis=2))
    w["f_wo"] = C(inp["L0_ffn_wo"])
    p1 = _perm_w1_cols()
    w["f_w1"] = C(np.asarray(inp["L0_ffn_w1"])[:, :, p1])
    b1p = np.asarray(inp["L0_ffn_b1"])[:, p1]
    b1_pad = np.zeros((2, 44 * 128), np.float32)
    b1_pad[:, :42 * 128] = b1p[:, :42 * 128]
    b1_pad[:, 42 * 128:42 * 128 + 42] = b1p[:, 5376:5418]
    b1_pad[:, 43 * 128:43 * 128 + 42] = b1p[:, 5418:5460]
    w["f_b1"] = C(np.stack([b1_pad[l].reshape(-1, 128).T for l in range(2)]))
    w["f_w2"] = C(inp["L0_ffn_w2"])
    w["f_b2"] = C(np.stack([np.asarray(inp["L0_ffn_b2"])[l].reshape(-1, 128).T
                            for l in range(2)]))
    w["f_lng"] = C(np.stack([np.asarray(inp["L0_ffn_ln_g"])[l].reshape(-1, 128).T
                             for l in range(2)]))
    w["f_lnb"] = C(np.stack([np.asarray(inp["L0_ffn_ln_b"])[l].reshape(-1, 128).T
                             for l in range(2)]))
    qs = np.asarray(inp["L0_ffn_q_scale"], dtype=np.float32)
    ks = np.asarray(inp["L0_ffn_k_scale"], dtype=np.float32)

    def expand(s):
        out = np.zeros((2, 512), np.float32)
        for l in range(2):
            for half in range(2):
                for h in range(8):
                    for i in range(32):
                        out[l, half * 256 + h * 32 + i] = s[l, half * 32 + i]
        return out
    w["f_qs"] = C(np.stack([expand(qs)[l].reshape(-1, 128).T for l in range(2)]))
    w["f_ks"] = C(np.stack([expand(ks)[l].reshape(-1, 128).T for l in range(2)]))
    return w


def host_constants():
    c = {}
    c["ident"] = np.eye(128, dtype=np.float32)
    c["identb"] = np.eye(128).astype(ml_dtypes.bfloat16)
    onescol = np.zeros((128, 2), np.float32)
    onescol[:, 0] = 1.0 / D
    onescol[:, 1] = -1.0 / D
    c["stat_ones"] = onescol.astype(ml_dtypes.bfloat16)
    c["stat_onesf"] = onescol.astype(np.float32)
    c["ones2"] = np.ones((2, 128), dtype=ml_dtypes.bfloat16)
    c["ones1f"] = np.ones((1, 128), dtype=np.float32)
    c["ones1b"] = np.ones((1, 128), dtype=ml_dtypes.bfloat16)
    k = np.arange(128)[:, None]
    q = np.arange(128)[None, :]
    c["m_lt"] = np.where(q < k, np.float32(NEG), np.float32(0.0)).astype(np.float32)
    c["m_ge"] = np.where(q >= k, np.float32(NEG), np.float32(0.0)).astype(np.float32)
    c["m_le"] = np.where(q <= k, np.float32(NEG), np.float32(0.0)).astype(np.float32)
    c["m_gt"] = np.where(q > k, np.float32(NEG), np.float32(0.0)).astype(np.float32)
    c["m01_ge"] = (q >= k).astype(np.float32).astype(ml_dtypes.bfloat16)
    for nm in ("m_lt", "m_ge", "m_le", "m_gt"):
        c[nm + "b"] = c[nm].astype(ml_dtypes.bfloat16)
    c["ones_col"] = np.ones((128, 1), dtype=ml_dtypes.bfloat16)
    oh = np.zeros((4, 128, 8), np.float32)
    for kt in range(4):
        for r in range(128):
            rr = kt * 128 + r
            half = rr // 256
            h = (rr - half * 256) // 32
            oh[kt, r, h] = 1.0
    c["l2_onehot"] = oh.astype(ml_dtypes.bfloat16)
    ohT = np.transpose(oh, (0, 2, 1))
    c["l2_onehot_T"] = np.ascontiguousarray(ohT).astype(ml_dtypes.bfloat16)
    c["eps_row"] = np.full((1, 1), LN_EPS, np.float32)
    c["mod32_col"] = (np.arange(128, dtype=np.float32) % 32).reshape(128, 1)
    c["mod16_col"] = (np.arange(128, dtype=np.float32) % 16).reshape(128, 1)
    c["small"] = np.full((1, 1), 1e-12, np.float32)
    return c



class KB:
    def __init__(self, weights, consts):
        nc = bass.Bass("TRN2", target_bir_lowering=False, debug=False,
                       detect_race_conditions=SIM_SAFE)
        self.nc = nc
        self.s = Sched()
        self.dram = {}
        self.cst_dram = {}
        x = nc.dram_tensor("x_in", [D, T], F32, kind="ExternalInput")
        self.dram["x_in"] = (x.ap(), Res("x_in"))
        y = nc.dram_tensor("x_out", [D, T], F32, kind="ExternalOutput")
        self.dram["x_out"] = (y.ap(), Res("x_out"))
        for name, arr in weights.items():
            t = nc.dram_tensor("w_" + name, list(arr.shape), F32,
                               kind="ExternalInput")
            self.dram[name] = (t.ap(), Res(name))
        sp = nc.dram_tensor("x_spill", [D, T], F32, kind="Internal")
        self.dram["x_spill"] = (sp.ap(), Res("x_spill"))
        sp2 = nc.dram_tensor("x_spill2", [D, T], F32, kind="Internal")
        self.dram["x_spill2"] = (sp2.ap(), Res("x_spill2"))
        fl = nc.dram_tensor("flip_scratch", [16, 128], F32, kind="Internal")
        self.dram["flip"] = (fl.ap(), Res("flip"))
        for name, arr in consts.items():
            t = nc.inline_tensor(np.ascontiguousarray(arr), "c_" + name)
            self.cst_dram[name] = (t.ap(), Res("c_" + name))

        self.st = contextlib.ExitStack()
        st = self.st
        af_t = st.enter_context(nc.sbuf_tensor("arena_f32", [128, 30208], F32))
        ab_t = st.enter_context(nc.sbuf_tensor("arena_bf16", [128, 43008], BF16))
        self.af = Arena(af_t, 30208, "f32")
        self.ab = Arena(ab_t, 43008, "bf16")
        self.psA = st.enter_context(nc.psum_tensor("psA", [128, 2048], F32))
        self.psB = st.enter_context(nc.psum_tensor("psB", [128, 2048], F32))
        self.pbank = [Res(f"pb{i}") for i in range(8)]
        self._bank_rr = 0
        self.cst = {}
        self._load_consts(consts)
        self.tap_bufs = {}
        self.wbf = {}
        self._wq_rr = 0

    def wqueue(self):
        self._wq_rr += 1
        return "sync" if self._wq_rr % 2 else "scalar"

    def make_bf16_twins(self, names, weights):
        nc = self.nc
        for n in names:
            arr = weights[n]
            t = nc.dram_tensor("wbf_" + n, list(arr.shape), BF16,
                               kind="Internal")
            r = Res("wbf_" + n)
            dap, dres = self.dram[n]
            self.s.dma("gpsimd",
                       lambda e, o=t.ap(), i=dap: e.dma_start(out=o, in_=i),
                       reads=[dres], writes=[r], tag="wcast_" + n)
            self.wbf[n] = (t.ap(), r)

    def bank_ap(self, i, c0=0, c1=512, p0=0, p1=128):
        t = self.psA if i < 4 else self.psB
        base = (i % 4) * 512
        return t[p0:p1, base + c0:base + c1]

    def bank_bap(self, i, cb0=0, cb1=1024, p0=0, p1=128):
        t = self.psA if i < 4 else self.psB
        base = (i % 4) * 512
        return t[p0:p1, base + cb0 // 2:base + cb1 // 2].bitcast(BF16)

    def take_banks(self, n):
        for _ in range(8):
            s0 = self._bank_rr % 8
            if (s0 % 4) + n <= 4:
                self._bank_rr = (s0 + n) % 8
                return list(range(s0, s0 + n))
            self._bank_rr = ((s0 // 4) * 4 + 4) % 8
        raise RuntimeError("bank alloc")

    def wide_ap(self, banks, c0, c1, p0=0, p1=128):
        t = self.psA if banks[0] < 4 else self.psB
        base = (banks[0] % 4) * 512
        return t[p0:p1, base + c0:base + c1]

    def _load_consts(self, consts):
        nc, st = self.nc, self.st
        specs = {
            "ident": ([128, 128], F32),
            "identb": ([128, 128], BF16),
            "stat_ones": ([128, 2], BF16),
            "stat_onesf": ([128, 2], F32),
            "ones2": ([2, 128], BF16),
            "ones1f": ([1, 128], F32),
            "ones1b": ([1, 128], BF16),
            "m_ltb": ([128, 128], BF16),
            "m_geb": ([128, 128], BF16),
            "m_leb": ([128, 128], BF16),
            "m_gtb": ([128, 128], BF16),
            "m01_ge": ([128, 128], BF16),
            "ones_col": ([128, 1], BF16),
            "l2_onehot": ([4, 128, 8], BF16),
            "l2_onehot_T": ([4, 8, 128], BF16),
            "eps_row": ([1, 1], F32),
            "mod32_col": ([128, 1], F32),
            "mod16_col": ([128, 1], F32),
            "small": ([1, 1], F32),
        }
        for name, (shape, dt) in specs.items():
            if len(shape) == 3:
                tens = st.enter_context(
                    nc.sbuf_tensor("k_" + name, [shape[1], shape[0] * shape[2]], dt))
                r = Res("k_" + name)
                dap, dres = self.cst_dram[name]
                for i in range(shape[0]):
                    self.dma("sync", tens[:, i * shape[2]:(i + 1) * shape[2]],
                             dap[i], [dres], [r], tag=f"cst_{name}{i}")
                self.cst[name] = (tens, r, shape[2])
            else:
                tens = st.enter_context(nc.sbuf_tensor("k_" + name, shape, dt))
                r = Res("k_" + name)
                dap, dres = self.cst_dram[name]
                self.dma("sync", tens[:, :], dap, [dres], [r], tag="cst_" + name)
                self.cst[name] = (tens, r, None)

    def cap(self, name, i=None):
        tens, r, w = self.cst[name]
        if i is None:
            return tens[:, :], r
        return tens[:, i * w:(i + 1) * w], r

    def dma(self, eng, out_ap, in_ap, reads, writes, tag=""):
        return self.s.dma(eng, lambda e, o=out_ap, i=in_ap: e.dma_start(out=o, in_=i),
                          reads=reads, writes=writes, tag=tag)

    def mm(self, out_ap, lhsT_ap, rhs_ap, start, stop, reads, writes, tag=""):
        return self.s.op(
            "tensor",
            lambda t, o=out_ap, l=lhsT_ap, r=rhs_ap, a=start, b=stop:
                t.matmul(o, l, r, start=a, stop=b),
            reads=reads, writes=writes, tag=tag)

    def transp(self, out_ap, in_ap, reads, writes, tag="", bf16=False):
        ident, ires = self.cap("identb" if bf16 else "ident")
        nc = self.nc

        def fn(t, o=out_ap, i=in_ap, d=ident):
            if bf16:
                with nc.allow_low_precision("bf16 transpose, no accumulation"):
                    return t.transpose(o, i, d)
            return t.transpose(o, i, d)
        return self.s.op("tensor", fn, reads=list(reads) + [ires],
                         writes=writes, tag=tag)

    def act(self, out_ap, in_ap, func, reads, writes, bias=0.0, scale=1.0,
            sreads=(), accum_out=None, tag=""):
        def fn(s, o=out_ap, i=in_ap, f=func, b=bias, sc=scale, ao=accum_out):
            if ao is not None:
                return s.activation(o, i, f, bias=b, scale=sc, accum_out=ao)
            return s.activation(o, i, f, bias=b, scale=sc)
        return self.s.op("scalar", fn, reads=reads, writes=writes,
                         sreads=sreads, tag=tag)

    def tt(self, eng, out_ap, in0, in1, op, reads, writes, tag=""):
        return self.s.op(eng,
                         lambda e, o=out_ap, a=in0, b=in1, p=op:
                             e.tensor_tensor(o, a, b, p),
                         reads=reads, writes=writes, tag=tag)

    def ts(self, eng, out_ap, in0, s1, s2, op0, op1, reads, writes,
           sreads=(), tag=""):
        return self.s.op(eng,
                         lambda e, o=out_ap, a=in0, x=s1, y=s2, p0=op0, p1=op1:
                             e.tensor_scalar(o, a, x, y, p0, p1),
                         reads=reads, writes=writes, sreads=sreads, tag=tag)

    def stt(self, eng, out_ap, in0, scalar, in1, op0, op1, reads, writes,
            sreads=(), tag=""):
        return self.s.op(eng,
                         lambda e, o=out_ap, a=in0, sc=scalar, b=in1, p0=op0, p1=op1:
                             e.scalar_tensor_tensor(o, a, sc, b, p0, p1),
                         reads=reads, writes=writes, sreads=sreads, tag=tag)

    def scan(self, out_ap, d0, d1, init, reads, writes, tag=""):
        return self.s.op("vector",
                         lambda e, o=out_ap, a=d0, b=d1, i=init:
                             e.tensor_tensor_scan(o, a, b, i, ALU.mult, ALU.add),
                         reads=reads, writes=writes, tag=tag)

    def copy(self, eng, out_ap, in_ap, reads, writes, tag=""):
        if eng == "scalar":
            return self.s.op(eng, lambda e, o=out_ap, i=in_ap: e.copy(o, i),
                             reads=reads, writes=writes, tag=tag)
        return self.s.op(eng, lambda e, o=out_ap, i=in_ap: e.tensor_copy(o, i),
                         reads=reads, writes=writes, tag=tag)

    def memset(self, eng, ap, val, writes, tag=""):
        o = self.s.op(eng, lambda e, a=ap, v=val: e.memset(a, v),
                      writes=writes, tag=tag)
        o.always_sem = True
        return o

    def reduce(self, eng, out_ap, in_ap, op, reads, writes, negate=False, tag=""):
        return self.s.op(eng,
                         lambda e, o=out_ap, i=in_ap, p=op, n=negate:
                             e.tensor_reduce(o, i, AX.X, p, negate=n),
                         reads=reads, writes=writes, tag=tag)

    def recip(self, out_ap, in_ap, reads, writes, tag=""):
        return self.s.op("vector",
                         lambda e, o=out_ap, i=in_ap: e.reciprocal(o, i),
                         reads=reads, writes=writes, tag=tag)

    def iota(self, out_ap, pattern, base, cm, writes, tag=""):
        return self.s.op("gpsimd",
                         lambda e, o=out_ap, p=pattern, b=base, c=cm:
                             e.iota(o, p, b, c, allow_small_or_imprecise_dtypes=True),
                         writes=writes, tag=tag)

    def tap(self, name, src_bufs, width=T, p1=128):
        nc = self.nc
        n = len(src_bufs)
        t = nc.dram_tensor("tap_" + name, [n * 128, width], F32,
                           kind="ExternalOutput")
        r = Res("tap_" + name)
        for i, b in enumerate(src_bufs):
            tmp = self.af.alloc(width, "tapt")
            self.copy("vector", tmp.ap(0, p1, 0, width), b.ap(0, p1, 0, width),
                      [b.res], [tmp.res], tag="tapc")
            self.dma("sync", t.ap()[i * 128:i * 128 + p1, :],
                     tmp.ap(0, p1, 0, width), [tmp.res], [r], tag="tapd")
            tmp.free()



class Model:
    def __init__(self, kb: KB):
        self.kb = kb
        self.stage = 99
        self.early_out = None
        self.weights_np = None


    def load_param_cols(self, wname, n, tag, rows_total=None):
        kb = self.kb
        dap, dres = kb.dram[wname]
        total = rows_total if rows_total is not None else n * 128
        buf = kb.ab
        b = kb.af.alloc(n, tag)
        src = dap[0:1, 0:total].rearrange("o (j p) -> (o p) j", p=128)
        kb.dma("sync", b.ap(0, 128, 0, n), src, [dres], [b.res], tag=tag)
        return b

    def stats_rows(self, tile_fn, tag, f32=False):
        kb = self.kb
        so_ap, so_res = kb.cap("stat_onesf" if f32 else "stat_ones")
        banks = kb.take_banks(4)
        ar = kb.af if f32 else kb.ab
        x2 = [ar.alloc(T, f"x2_{tag}") for _ in range(2)]
        for kt in range(NKT):
            xb, owned = tile_fn(kt)
            x2b = x2[kt % 2]
            kb.tt("vector", x2b.ap(), xb.ap(), xb.ap(),
                  ALU.mult, [xb.res], [x2b.res], tag=f"sq_{tag}{kt}")
            for c in range(NCH):
                bk = banks[c]
                first = kt == 0
                last = kt == NKT - 1
                kb.mm(kb.bank_ap(bk, 0, CW, 0, 1), so_ap[:, 1:2],
                      xb.ap(0, 128, c * CW, (c + 1) * CW),
                      first, last,
                      [so_res, xb.res], [kb.pbank[bk]],
                      tag=f"st_m_{tag}")
                kb.s.op("tensor",
                        lambda t, o=kb.bank_ap(bk, 0, CW, 32, 33),
                        l=so_ap[:, 0:1],
                        r=x2b.ap(0, 128, c * CW, (c + 1) * CW), fst=first,
                        lst=last:
                            t.matmul(o, l, r, start=fst, stop=lst,
                                     tile_position=(0, 32)),
                        reads=[so_res, x2b.res], writes=[kb.pbank[bk]],
                        tag=f"st_v_{tag}")
            if owned:
                xb.free()
        negm = kb.af.alloc(T, f"negm_{tag}")
        ex2 = kb.af.alloc(T, f"ex2_{tag}")
        for c in range(NCH):
            bk = banks[c]
            kb.copy("scalar", negm.ap(0, 1, c * CW, (c + 1) * CW),
                    kb.bank_ap(bk, 0, CW, 0, 1), [kb.pbank[bk]], [negm.res],
                    tag=f"ev_m_{tag}")
            kb.copy("scalar", ex2.ap(0, 1, c * CW, (c + 1) * CW),
                    kb.bank_ap(bk, 0, CW, 32, 33), [kb.pbank[bk]], [ex2.res],
                    tag=f"ev_v_{tag}")
        for b in x2:
            b.free()
        msq = kb.af.alloc(T, f"msq_{tag}")
        kb.tt("vector", msq.ap(0, 1), negm.ap(0, 1), negm.ap(0, 1), ALU.mult,
              [negm.res], [msq.res], tag=f"msq_{tag}")
        var = msq
        kb.tt("vector", var.ap(0, 1), ex2.ap(0, 1), msq.ap(0, 1), ALU.subtract,
              [ex2.res, msq.res], [var.res], tag=f"var_{tag}")
        ex2.free()
        sd = kb.af.alloc(T, f"sd_{tag}")
        eps_ap, eps_res = kb.cap("eps_row")
        kb.act(sd.ap(0, 1), var.ap(0, 1), AF.Ln, [var.res], [sd.res],
               bias=eps_ap[0:1, 0:1], scale=1.0, sreads=[eps_res],
               tag=f"lnv_{tag}")
        var.free()
        rr = kb.af.alloc(T, f"r_{tag}")
        kb.act(rr.ap(0, 1), sd.ap(0, 1), AF.Exp, [sd.res], [rr.res],
               scale=-0.5, tag=f"rex_{tag}")
        sd.free()
        return negm, rr

    def bcast_row(self, row, tag, out_dtype_f32=True):
        kb = self.kb
        hilo = kb.ab.alloc(T * 2, f"hilo_{tag}")
        kb.copy("vector", hilo.ap(0, 1, 0, T), row.ap(0, 1), [row.res],
                [hilo.res], tag=f"hi_{tag}")
        kb.tt("vector", hilo.ap(0, 1, T, 2 * T), row.ap(0, 1),
              hilo.ap(0, 1, 0, T), ALU.subtract, [row.res, hilo.res],
              [hilo.res], tag=f"lo_{tag}")
        ones1b, o1res = kb.cap("ones1b")
        out = kb.af.alloc(T, f"bc_{tag}")
        banks = kb.take_banks(2)
        for c in range(NCH):
            bk = banks[c % 2]
            kb.mm(kb.bank_ap(bk), ones1b, hilo.ap(0, 1, c * CW, (c + 1) * CW),
                  True, False, [o1res, hilo.res], [kb.pbank[bk]],
                  tag=f"bcm_{tag}")
            kb.mm(kb.bank_ap(bk), ones1b,
                  hilo.ap(0, 1, T + c * CW, T + (c + 1) * CW),
                  False, True, [o1res, hilo.res], [kb.pbank[bk]],
                  tag=f"bcm2_{tag}")
            kb.copy("scalar", out.ap(0, 128, c * CW, (c + 1) * CW),
                    kb.bank_ap(bk), [kb.pbank[bk]], [out.res],
                    tag=f"bce_{tag}")
        hilo.free()
        return out

    def materialize_ln(self, src_f32_tile_fn, stats_tile_fn, g_b=None,
                       tag="ln", out_f32=False):
        kb = self.kb
        negm, rr = self.stats_rows(stats_tile_fn, tag)
        negm_b = self.bcast_row(negm, tag + "m")
        r_b = self.bcast_row(rr, tag + "r")
        negm.free()
        rr.free()
        outs = []
        outs32 = []
        tmp = kb.af.alloc(T, tag + "_t")
        for kt in range(NKT):
            sap, sres = src_f32_tile_fn(kt)
            kb.tt("vector", tmp.ap(), sap, negm_b.ap(), ALU.add,
                  sres + [negm_b.res], [tmp.res], tag=f"{tag}_a{kt}")
            if g_b is None:
                ob = kb.ab.alloc(T, tag + f"_o{kt}")
                kb.tt("vector", ob.ap(), tmp.ap(), r_b.ap(), ALU.mult,
                      [tmp.res, r_b.res], [ob.res], tag=f"{tag}_m{kt}")
                if out_f32:
                    o32 = kb.af.alloc(T, tag + f"_f{kt}")
                    kb.tt("scalar" if False else "vector", o32.ap(), tmp.ap(),
                          r_b.ap(), ALU.mult, [tmp.res, r_b.res], [o32.res],
                          tag=f"{tag}_f{kt}")
                    outs32.append(o32)
                outs.append(ob)
            else:
                gb_buf, g_col, b_col = g_b
                kb.tt("vector", tmp.ap(), tmp.ap(), r_b.ap(), ALU.mult,
                      [tmp.res, r_b.res], [tmp.res], tag=f"{tag}_m{kt}")
                ob = kb.ab.alloc(T, tag + f"_o{kt}")
                kb.ts("vector", ob.ap(), tmp.ap(),
                      gb_buf.ap(0, 128, g_col + kt, g_col + kt + 1),
                      gb_buf.ap(0, 128, b_col + kt, b_col + kt + 1),
                      ALU.mult, ALU.add, [tmp.res, gb_buf.res], [ob.res],
                      sreads=[gb_buf.res], tag=f"{tag}_gb{kt}")
                outs.append(ob)
        tmp.free()
        negm_b.free()
        r_b.free()
        if out_f32:
            return outs, outs32
        return outs

    def sm_prewarm(self, wname, n_kt, m_widths, dram_row0=0, dram_col0=0,
                   layer_idx=None, tag="pw", mb_cols=1024, k_group=None,
                   k_rows=None, chunks_len=4, nk_pre=None):
        kb = self.kb
        if wname in kb.wbf:
            dap, dres = kb.wbf[wname]
        else:
            dap, dres = kb.dram[wname]
        if layer_idx is not None:
            dap = dap[layer_idx]
        if k_rows is None:
            k_rows = [(kt * 128, 128) for kt in range(n_kt)]
        moff = [0]
        for w in m_widths:
            moff.append(moff[-1] + w)
        kg = k_group if k_group is not None else n_kt
        kgroups = [list(range(i, min(i + kg, n_kt)))
                   for i in range(0, n_kt, kg)]
        max_mt = len(m_widths)
        if len(kgroups) > 1:
            max_mt = max(1, 4 // chunks_len)
        cur, cw = [], 0
        for mi, w in enumerate(m_widths):
            if cur and (cw + w > mb_cols or len(cur) >= max_mt):
                break
            cur.append(mi)
            cw += w
        c0 = moff[cur[0]]
        c1 = moff[cur[-1] + 1]
        slabs = {}
        for kt in kgroups[0][:nk_pre] if nk_pre is not None else kgroups[0]:
            r0, nr = k_rows[kt]
            sb = kb.ab.alloc(c1 - c0, f"{tag}_w0k{kt}")
            kb.dma("gpsimd", sb.ap(0, nr, 0, c1 - c0),
                   dap[dram_row0 + r0:dram_row0 + r0 + nr,
                       dram_col0 + c0:dram_col0 + c1],
                   [dres], [sb.res], tag=f"{tag}_pwd{kt}")
            slabs[kt] = sb
        return (slabs, c0)

    def stream_matmul(self, wname, n_kt, m_widths, rhs_fn, evac_fn,
                      dram_row0=0, dram_col0=0, layer_idx=None, tag="sm",
                      aug=None, mb_cols=1024, k_group=None, k_rows=None,
                      chunks=None, pre=None, xblk=True):
        self._mk_nxt = None
        """Generic streamed matmul: out[m, t] = sum_k W[k, m] rhs[k, t].

        wname: dram weight [K_total, M_total] (or [L, K, M] w/ layer_idx).
        n_kt: number of k tiles; k_rows optionally gives (row0, nrows) per
        tile (partial tiles allowed).
        m_widths: output tile widths (<=128 each).
        rhs_fn(kt, ci) -> (ap, [res]) bf16 [rows, chunk_w].
        evac_fn(mi, ci, psum_ap, bank_res).
        chunks: list of (t0, w<=512) token chunks; default 4x512.
        k_group: if set, k tiles stream in groups of k_group with psum held
        open across groups (m-block is clamped so mtiles*chunks <= 4).
        aug: (lhs_fn(mi, mw) -> (ap, res), rhs_fn(ci) -> (ap, [res]))
        appended as a final K=1 accumulate (LN mean-fold).
        """
        kb = self.kb
        if wname in kb.wbf:
            dap, dres = kb.wbf[wname]
        else:
            dap, dres = kb.dram[wname]
        if layer_idx is not None:
            dap = dap[layer_idx]
        if chunks is None:
            chunks = [(c * CW, CW) for c in range(NCH)]
        if k_rows is None:
            k_rows = [(kt * 128, 128) for kt in range(n_kt)]
        moff = [0]
        for w in m_widths:
            moff.append(moff[-1] + w)
        kg = k_group if k_group is not None else n_kt
        kgroups = [list(range(i, min(i + kg, n_kt))) for i in range(0, n_kt, kg)]
        multi_kg = len(kgroups) > 1
        max_mt = len(m_widths)
        if multi_kg:
            max_mt = max(1, 4 // len(chunks))
        blocks = []
        cur, cw = [], 0
        for mi, w in enumerate(m_widths):
            if cur and (cw + w > mb_cols or len(cur) >= max_mt):
                blocks.append(cur)
                cur, cw = [], 0
            cur.append(mi)
            cw += w
        if cur:
            blocks.append(cur)

        def load_slabs(bi, kts, have=None):
            mis = blocks[bi]
            c0 = moff[mis[0]]
            c1 = moff[mis[-1] + 1]
            slabs = dict(have) if have else {}
            for kt in kts:
                if kt in slabs:
                    continue
                r0, nr = k_rows[kt]
                sb = kb.ab.alloc(c1 - c0, f"{tag}_w{bi}k{kt}")
                kb.dma(kb.wqueue(), sb.ap(0, nr, 0, c1 - c0),
                       dap[dram_row0 + r0:dram_row0 + r0 + nr,
                           dram_col0 + c0:dram_col0 + c1],
                       [dres], [sb.res], tag=f"{tag}_wd{bi}_{kt}")
                slabs[kt] = sb
            return slabs, c0

        rrbanks = kb.take_banks(2) + kb.take_banks(2)
        bank_i = 0
        n_aug = 1 if aug is not None else 0
        for bi in range(len(blocks)):
            mis = blocks[bi]
            if multi_kg:
                bmap = {}
                for j, mi in enumerate(mis):
                    for ci in range(len(chunks)):
                        bmap[(mi, ci)] = rrbanks[(j * len(chunks) + ci) % 4]
                n_mm_total = sum(len(g) for g in kgroups) + n_aug
                i_base = 0
                if bi == 0 and pre is not None:
                    nxt = load_slabs(0, kgroups[0], have=pre[0])
                elif self._mk_nxt is not None:
                    nxt = self._mk_nxt
                    self._mk_nxt = None
                else:
                    nxt = load_slabs(bi, kgroups[0])
                for gi, kts in enumerate(kgroups):
                    slabs, base = nxt
                    if gi + 1 < len(kgroups):
                        nxt = load_slabs(bi, kgroups[gi + 1])
                    elif bi + 1 < len(blocks) and xblk:
                        self._mk_nxt = load_slabs(bi + 1, kgroups[0])
                    for mi in mis:
                        mw = m_widths[mi]
                        for ci, (t0, tw) in enumerate(chunks):
                            bk = bmap[(mi, ci)]
                            pres = kb.pbank[bk]
                            for jj, kt in enumerate(kts):
                                r0, nr = k_rows[kt]
                                rap, rres = rhs_fn(kt, ci)
                                sb = slabs[kt]
                                i_mm = i_base + jj
                                kb.mm(kb.bank_ap(bk, 0, tw, 0, mw),
                                      sb.ap(0, nr, moff[mi] - base,
                                            moff[mi] - base + mw),
                                      rap, i_mm == 0,
                                      i_mm == n_mm_total - 1,
                                      [sb.res] + rres, [pres],
                                      tag=f"{tag}_mm{mi}_{ci}_{kt}")
                            if gi == len(kgroups) - 1:
                                if aug is not None:
                                    lap, lres = aug[0](mi, mw)
                                    rap2, rres2 = aug[1](ci)
                                    kb.mm(kb.bank_ap(bk, 0, tw, 0, mw),
                                          lap, rap2, False, True,
                                          [lres] + rres2, [pres],
                                          tag=f"{tag}_aug{mi}_{ci}")
                                evac_fn(mi, ci, kb.bank_ap(bk, 0, tw, 0, mw),
                                        pres)
                    i_base += len(kts)
                    for sb in slabs.values():
                        sb.free()
            else:
                if bi == 0:
                    if pre is not None:
                        self._sm_slabs = load_slabs(0, kgroups[0],
                                                    have=pre[0])[0]
                        self._sm_base = pre[1]
                    else:
                        self._sm_slabs, self._sm_base = load_slabs(
                            0, kgroups[0])
                slabs, base = self._sm_slabs, self._sm_base
                if bi + 1 < len(blocks):
                    self._sm_slabs, self._sm_base = load_slabs(bi + 1,
                                                               kgroups[0])
                n_mm = n_kt + n_aug
                for mi in mis:
                    mw = m_widths[mi]
                    for ci, (t0, tw) in enumerate(chunks):
                        bk = rrbanks[bank_i % 4]
                        bank_i += 1
                        pres = kb.pbank[bk]
                        for kt in kgroups[0]:
                            r0, nr = k_rows[kt]
                            rap, rres = rhs_fn(kt, ci)
                            sb = slabs[kt]
                            kb.mm(kb.bank_ap(bk, 0, tw, 0, mw),
                                  sb.ap(0, nr, moff[mi] - base,
                                        moff[mi] - base + mw),
                                  rap, kt == kgroups[0][0],
                                  kt == kgroups[0][-1] and aug is None,
                                  [sb.res] + rres, [pres],
                                  tag=f"{tag}_mm{mi}_{ci}_{kt}")
                        if aug is not None:
                            lap, lres = aug[0](mi, mw)
                            rap2, rres2 = aug[1](ci)
                            kb.mm(kb.bank_ap(bk, 0, tw, 0, mw), lap, rap2,
                                  False, True, [lres] + rres2, [pres],
                                  tag=f"{tag}_aug{mi}_{ci}")
                        evac_fn(mi, ci, kb.bank_ap(bk, 0, tw, 0, mw), pres)
                for sb in slabs.values():
                    sb.free()


    def load_x_tile(self, src, kt, tag, queue=None):
        kb = self.kb
        dap, dres = kb.dram[src]
        b = kb.af.alloc(T, tag)
        kb.dma(queue or kb.wqueue(), b.ap(),
               dap[kt * 128:(kt + 1) * 128, :], [dres],
               [b.res], tag=tag)
        return b

    def cast_tile(self, src_buf, tag, eng="vector"):
        kb = self.kb
        b = kb.ab.alloc(T, tag)
        kb.copy(eng, b.ap(), src_buf.ap(), [src_buf.res], [b.res], tag=tag)
        return b


def _chunks(total, step):
    out = []
    c = 0
    while c < total:
        out.append((c, min(step, total - c)))
        c += step
    return out



DEV_STAGE = 99


class Model(Model):

    def emit_floor(self, buf, tmp, p1=128, w=None, tag="fl"):
        kb = self.kb
        if w is None:
            w = buf.w
        kb.copy("vector", tmp.ap(0, p1, 0, w).bitcast(mybir.dt.int32),
                buf.ap(0, p1, 0, w), [buf.res], [tmp.res], tag=tag + "_i")
        kb.copy("vector", tmp.ap(0, p1, 0, w),
                tmp.ap(0, p1, 0, w).bitcast(mybir.dt.int32),
                [tmp.res], [tmp.res], tag=tag + "_f")
        kb.tt("vector", buf.ap(0, p1, 0, w), tmp.ap(0, p1, 0, w),
              buf.ap(0, p1, 0, w), ALU.is_gt, [tmp.res, buf.res], [buf.res],
              tag=tag + "_g")
        kb.tt("vector", buf.ap(0, p1, 0, w), tmp.ap(0, p1, 0, w),
              buf.ap(0, p1, 0, w), ALU.subtract, [tmp.res, buf.res],
              [buf.res], tag=tag + "_s")

    def sincos_from_angle(self, ang, tag, want_cos=True):
        kb = self.kb
        inv2pi = 1.0 / (2.0 * math.pi)
        c_hi = 6.28125
        c_lo = 2.0 * math.pi - c_hi
        q = kb.af.alloc(T, tag + "_q")
        kb.ts("vector", q.ap(), ang.ap(), inv2pi, 0.5, ALU.mult, ALU.add,
              [ang.res], [q.res], tag=tag + "_q")
        qi = kb.af.alloc(T, tag + "_qi")
        self.emit_floor(q, qi, tag=tag + "_fl")
        qi.free()
        red = kb.af.alloc(T, tag + "_red")
        kb.stt("vector", red.ap(), q.ap(), -c_hi, ang.ap(), ALU.mult, ALU.add,
               [q.res, ang.res], [red.res], tag=tag + "_r1")
        kb.stt("vector", red.ap(), q.ap(), -c_lo, red.ap(), ALU.mult, ALU.add,
               [q.res, red.res], [red.res], tag=tag + "_r2")
        q.free()
        kb.ts("vector", red.ap(), red.ap(), math.pi, -math.pi, ALU.min,
              ALU.max, [red.res], [red.res], tag=tag + "_cl")
        sin_b = kb.af.alloc(T, tag + "_sin")
        kb.act(sin_b.ap(), red.ap(), AF.Sin, [red.res], [sin_b.res],
               tag=tag + "_sin")
        cos_b = None
        if want_cos:
            half = kb.af.alloc(T, tag + "_h2")
            kb.act(half.ap(), red.ap(), AF.Sin, [red.res], [half.res],
                   scale=0.5, tag=tag + "_sh")
            cos_b = kb.af.alloc(T, tag + "_cos")
            kb.tt("vector", cos_b.ap(), half.ap(), half.ap(), ALU.mult,
                  [half.res], [cos_b.res], tag=tag + "_sq")
            kb.ts("vector", cos_b.ap(), cos_b.ap(), -2.0, 1.0, ALU.mult,
                  ALU.add, [cos_b.res], [cos_b.res], tag=tag + "_c")
            half.free()
        red.free()
        return sin_b, cos_b

    def iota_row_tile(self, tag):
        kb = self.kb
        it = kb.af.alloc(T, tag)
        kb.iota(it.ap(), [[1, T]], 0, 0, [it.res], tag=tag)
        return it


    def load_gb_cols(self, gname, bname, tag):
        kb = self.kb
        b = kb.af.alloc(16, tag)
        for j, nm in enumerate((gname, bname)):
            dap, dres = kb.dram[nm]
            n = dap.tensor.shape[1]
            kb.dma("sync", kb.af.t[0:128, b.c0 + 8 * j:b.c0 + 8 * j + n],
                   dap, [dres], [b.res], tag=tag + nm)
        return b

    def attn_a(self, xa, xa_bf):
        kb = self.kb
        n1gb = self.load_gb_cols("a_n1_g", "a_n1_b", "n1gb")
        xn_bf = self.materialize_ln(
            lambda kt: (xa[kt].ap(), [xa[kt].res]),
            lambda kt: (xa_bf[kt], False),
            g_b=(n1gb, 0, 8), tag="n1")
        for b in xa_bf:
            b.free()
        n1gb.free()
        if self.stage == 101:
            self.early_out = [(b, 0, 128) for b in xn_bf]
            return xa
        sp_ap, sp_res = kb.dram["x_spill"]
        for kt in range(NKT):
            kb.dma("sync", sp_ap[kt * 128:(kt + 1) * 128, :], xa[kt].ap(),
                   [xa[kt].res], [sp_res], tag=f"aspl{kt}")
            xa[kt].free()
        xa = None

        beta = kb.af.alloc(T, "beta")
        pre_inp = self.sm_prewarm("a_inproj", NKT, [128], tag="pwain",
                                  chunks_len=4)

        def evac_beta(mi, c, ps_ap, pres):
            kb.copy("scalar", beta.ap(0, 128, c * CW, (c + 1) * CW), ps_ap,
                    [pres], [beta.res], tag="ev_beta")

        self.stream_matmul("a_inproj", NKT, [128],
                           lambda kt, c: (xn_bf[kt].ap(0, 128, c * CW, (c + 1) * CW),
                                          [xn_bf[kt].res]),
                           evac_beta, tag="inproj", pre=pre_inp)
        if self.stage == 102:
            self.early_out = [(beta, 0, 128)]
            return xa

        ld_col = kb.af.alloc(16, "ldcol")
        fq_col = kb.af.alloc(16, "fqcol")
        for nm, buf in (("a_log_decay", ld_col), ("a_frequency", fq_col)):
            dap, dres = kb.dram[nm]
            src = dap[0:1, 0:64].rearrange("a b -> b a")
            kb.dma("sync", kb.af.t[0:64, buf.c0:buf.c0 + 1], src, [dres],
                   [buf.res], tag="ld_" + nm)
            kb.dma("sync", kb.af.t[64:128, buf.c0:buf.c0 + 1], src, [dres],
                   [buf.res], tag="ld2_" + nm)
        mag_col = kb.af.alloc(16, "magcol")
        kb.act(mag_col.ap(0, 128, 0, 1), ld_col.ap(0, 128, 0, 1), AF.Sigmoid,
               [ld_col.res], [mag_col.res], tag="mag")
        ld_col.free()
        it = self.iota_row_tile("ssm_iota")
        ang = kb.af.alloc(T, "ssm_ang")
        kb.ts("vector", ang.ap(), it.ap(), fq_col.ap(0, 128, 0, 1), None,
              ALU.mult, ALU.bypass, [it.res, fq_col.res], [ang.res],
              sreads=[fq_col.res], tag="ssm_ang")
        fq_col.free()
        it.free()
        ST, CT = self.sincos_from_angle(ang, "ssm")
        ang.free()
        MT = kb.af.alloc(T, "ssm_mt")
        kb.memset("vector", MT.ap(), 1.0, [MT.res], tag="mt0")
        kb.ts("vector", MT.ap(), MT.ap(), mag_col.ap(0, 128, 0, 1), None,
              ALU.mult, ALU.bypass, [MT.res, mag_col.res], [MT.res],
              sreads=[mag_col.res], tag="mt1")
        mag_col.free()
        if self.stage in (1031, 1032, 1033):
            b = {1031: ST, 1032: CT, 1033: MT}[self.stage]
            self.early_out = [(b, 0, 128)]
            return xa

        G = kb.af.alloc(T, "ssm_g")
        t1 = kb.af.alloc(T, "ssm_t1")
        t2 = kb.af.alloc(T, "ssm_t2")
        kb.tt("vector", t1.ap(0, 64), beta.ap(64, 128), ST.ap(64, 128),
              ALU.mult, [beta.res, ST.res], [t1.res], tag="g1")
        kb.tt("vector", t2.ap(0, 64), beta.ap(0, 64), CT.ap(0, 64),
              ALU.mult, [beta.res, CT.res], [t2.res], tag="g2")
        kb.tt("vector", G.ap(0, 64), t1.ap(0, 64), t2.ap(0, 64), ALU.add,
              [t1.res, t2.res], [G.res], tag="g3")
        kb.tt("vector", t1.ap(64, 128), beta.ap(0, 64), ST.ap(0, 64),
              ALU.mult, [beta.res, ST.res], [t1.res], tag="g4")
        kb.tt("vector", t2.ap(64, 128), beta.ap(64, 128), CT.ap(64, 128),
              ALU.mult, [beta.res, CT.res], [t2.res], tag="g5")
        kb.tt("vector", G.ap(64, 128), t2.ap(64, 128), t1.ap(64, 128),
              ALU.subtract, [t1.res, t2.res], [G.res], tag="g6")
        beta.free()
        if self.stage == 1034:
            self.early_out = [(G, 0, 128)]
            return xa

        DD = kb.af.alloc(T, "ssm_d")
        kb.scan(DD.ap(), MT.ap(), G.ap(), 0.0, [MT.res, G.res], [DD.res],
                tag="scan")
        G.free()
        MT.free()

        C = kb.ab.alloc(T, "ssm_c")
        kb.tt("vector", t1.ap(0, 64), DD.ap(64, 128), ST.ap(64, 128),
              ALU.mult, [DD.res, ST.res], [t1.res], tag="u1")
        kb.tt("vector", t2.ap(0, 64), DD.ap(0, 64), CT.ap(0, 64),
              ALU.mult, [DD.res, CT.res], [t2.res], tag="u2")
        kb.tt("vector", C.ap(0, 64), t2.ap(0, 64), t1.ap(0, 64),
              ALU.subtract, [t1.res, t2.res], [C.res], tag="u3")
        kb.tt("vector", t1.ap(64, 128), DD.ap(0, 64), ST.ap(0, 64),
              ALU.mult, [DD.res, ST.res], [t1.res], tag="u4")
        kb.tt("vector", t2.ap(64, 128), DD.ap(64, 128), CT.ap(64, 128),
              ALU.mult, [DD.res, CT.res], [t2.res], tag="u5")
        kb.tt("vector", C.ap(64, 128), t1.ap(64, 128), t2.ap(64, 128),
              ALU.add, [t1.res, t2.res], [C.res], tag="u6")
        DD.free()
        ST.free()
        CT.free()
        t1.free()
        t2.free()
        if self.stage == 103:
            self.early_out = [(C, 0, 128)]
            return xa

        cpl = kb.ab.alloc(64, "cpl")
        dap, dres = kb.wbf["a_coupling_bd"]
        kb.dma("sync", cpl.ap(0, 64, 0, 64), dap, [dres], [cpl.res],
               tag="cpl_ld")
        kb.dma("sync", cpl.ap(64, 128, 0, 64), dap, [dres], [cpl.res],
               tag="cpl_ld2")
        EIG = kb.ab.alloc(T, "eig")
        bks = kb.take_banks(2)
        for c in range(NCH):
            for half in range(2):
                bk = bks[half]
                kb.mm(kb.bank_ap(bk, 0, CW, 0, 64),
                      cpl.ap(64 * half, 64 * half + 64, 0, 64),
                      C.ap(64 * half, 64 * half + 64, c * CW, (c + 1) * CW),
                      True, True, [cpl.res, C.res], [kb.pbank[bk]],
                      tag=f"cplmm{c}_{half}")
                kb.copy("scalar",
                        EIG.ap(64 * half, 64 * half + 64, c * CW, (c + 1) * CW),
                        kb.bank_ap(bk, 0, CW, 0, 64), [kb.pbank[bk]],
                        [EIG.res], tag=f"cplev{c}_{half}")
        C.free()
        cpl.free()
        if self.stage == 104:
            self.early_out = [(EIG, 0, 128)]
            return xa

        gateb = self.load_gb_cols("a_gate_b", "a_gate_b", "gateb")
        gdap, gdres = kb.wbf["a_gate"]
        odap, odres = kb.wbf["a_outproj"]
        sig = kb.af.alloc(CW, "sig")
        htmp = kb.af.alloc(CW, "htmp")
        xa2 = [None] * NKT
        for mb in range(4):
            gslabs = []
            for kt in range(NKT):
                sb = kb.ab.alloc(256, f"gw{mb}_{kt}")
                kb.dma(kb.wqueue(), sb.ap(0, 128, 0, 256),
                       gdap[kt * 128:(kt + 1) * 128, mb * 256:(mb + 1) * 256],
                       [gdres], [sb.res], tag=f"gwd{mb}_{kt}")
                gslabs.append(sb)
            oslab = kb.ab.alloc(256, f"ow{mb}")
            kb.dma(kb.wqueue(), oslab.ap(0, 128, 0, 256),
                   odap[0:128, mb * 256:(mb + 1) * 256], [odres],
                   [oslab.res], tag=f"owd{mb}")
            for mi2 in range(2):
                mi = 2 * mb + mi2
                xs = self.load_x_tile("x_spill", mi, f"gxs{mi}")
                xa2[mi] = kb.af.alloc(T, f"xa2_{mi}")
                for c in range(NCH):
                    bkg = kb.take_banks(1)[0]
                    bko = kb.take_banks(1)[0]
                    for kt in range(NKT):
                        kb.mm(kb.bank_ap(bkg),
                              gslabs[kt].ap(0, 128, mi2 * 128, mi2 * 128 + 128),
                              xn_bf[kt].ap(0, 128, c * CW, (c + 1) * CW),
                              kt == 0, kt == NKT - 1,
                              [gslabs[kt].res, xn_bf[kt].res],
                              [kb.pbank[bkg]], tag=f"gmm{mi}_{c}_{kt}")
                    kb.mm(kb.bank_ap(bko),
                          oslab.ap(0, 128, mi2 * 128, mi2 * 128 + 128),
                          EIG.ap(0, 128, c * CW, (c + 1) * CW),
                          True, True, [oslab.res, EIG.res], [kb.pbank[bko]],
                          tag=f"omm{mi}_{c}")
                    kb.act(sig.ap(), kb.bank_ap(bkg), AF.Sigmoid,
                           [kb.pbank[bkg], gateb.res], [sig.res],
                           bias=gateb.ap(0, 128, mi, mi + 1),
                           sreads=[gateb.res], tag=f"sig{mi}_{c}")
                    kb.tt("vector", htmp.ap(), sig.ap(), kb.bank_ap(bko),
                          ALU.mult, [sig.res, kb.pbank[bko]], [htmp.res],
                          tag=f"hm{mi}_{c}")
                    kb.tt("vector", xa2[mi].ap(0, 128, c * CW, (c + 1) * CW),
                          xs.ap(0, 128, c * CW, (c + 1) * CW), htmp.ap(),
                          ALU.add, [xs.res, htmp.res], [xa2[mi].res],
                          tag=f"hr{mi}_{c}")
                xs.free()
            for sb in gslabs:
                sb.free()
            oslab.free()
        sig.free()
        htmp.free()
        gateb.free()
        EIG.free()
        for b in xn_bf:
            b.free()

        xa = xa2
        if self.stage == 105:
            self.early_out = [(b, 0, 128) for b in xa2]
            return xa
        n2gb = self.load_gb_cols("a_n2_g", "a_n2_b", "n2gb")
        xa_bf2 = [self.cast_tile(xa[kt], f"xabf2_{kt}") for kt in range(NKT)]
        x2n_bf = self.materialize_ln(
            lambda kt: (xa[kt].ap(), [xa[kt].res]),
            lambda kt: (xa_bf2[kt], False),
            g_b=(n2gb, 0, 8), tag="n2")
        for b in xa_bf2:
            b.free()
        n2gb.free()

        b1c = kb.af.alloc(32, "b1c")
        dap, dres = kb.dram["a_mlp_b1"]
        kb.dma("sync", kb.af.t[0:128, b1c.c0:b1c.c0 + 32], dap, [dres],
               [b1c.res], tag="b1c")
        b2c = self.load_gb_cols("a_mlp_b2", "a_mlp_b2", "b2c")

        TCW = 512
        pre_aw1 = self.sm_prewarm("a_w1", NKT, [128] * 32, tag="pwaw1",
                                  mb_cols=512, chunks_len=1, nk_pre=4)
        for tc0 in range(0, T, TCW):
            hid = [kb.ab.alloc(TCW, f"ahid{m}") for m in range(32)]
            chunks = [(tc0, TCW)]

            def rhs_w1(kt, ci, t0=tc0):
                return (x2n_bf[kt].ap(0, 128, t0, t0 + TCW),
                        [x2n_bf[kt].res])

            def evac_w1(mi, ci, ps_ap, pres):
                kb.act(hid[mi].ap(), ps_ap, AF.Silu,
                       [pres, b1c.res], [hid[mi].res],
                       bias=b1c.ap(0, 128, mi, mi + 1), sreads=[b1c.res],
                       tag=f"silu{mi}")

            pre_aw2 = self.sm_prewarm("a_w2", 32, [128] * 8,
                                      tag=f"pwaw2_{tc0}", mb_cols=1024,
                                      k_group=8, chunks_len=1, nk_pre=2)
            self.stream_matmul("a_w1", NKT, [128] * 32, rhs_w1, evac_w1,
                               chunks=chunks, tag="aw1", mb_cols=512,
                               pre=pre_aw1)
            pre_aw1 = None

            def rhs_w2(kt, ci, t0=tc0):
                return (hid[kt].ap(), [hid[kt].res])

            def evac_w2(mi, ci, ps_ap, pres, t0=tc0):
                kb.stt("vector", xa[mi].ap(0, 128, t0, t0 + TCW), ps_ap,
                       b2c.ap(0, 128, mi, mi + 1),
                       xa[mi].ap(0, 128, t0, t0 + TCW), ALU.add, ALU.add,
                       [pres, xa[mi].res, b2c.res], [xa[mi].res],
                       sreads=[b2c.res], tag=f"w2r{mi}")

            self.stream_matmul("a_w2", 32, [128] * 8, rhs_w2, evac_w2,
                               chunks=chunks, tag="aw2", mb_cols=1024,
                               k_group=8, pre=pre_aw2, xblk=False)
            for b in hid:
                b.free()
        for b in x2n_bf:
            b.free()
        b1c.free()
        b2c.free()
        return xa


    def build_ffn_tables(self):
        kb = self.kb
        it = self.iota_row_tile("ft_iota")
        icol_ap, icol_res = self.kb.cap("mod32_col")
        invf = kb.af.alloc(16, "ft_if")
        kb.act(invf.ap(0, 128, 0, 1), icol_ap, AF.Exp,
               [icol_res], [invf.res],
               scale=-2.0 * math.log(10000.0) / 64.0, tag="ft_inv")
        lnb = kb.af.alloc(16, "ft_lb")
        lnb2 = kb.af.alloc(16, "ft_lb2")
        kb.ts("vector", lnb.ap(0, 128, 0, 1), icol_ap,
              2.0 / (1.4 * 64.0), (0.4 * 64.0) / (1.4 * 64.0), ALU.mult,
              ALU.add, [icol_res], [lnb.res], tag="ft_b")
        kb.act(lnb2.ap(0, 128, 0, 1), lnb.ap(0, 128, 0, 1), AF.Ln,
               [lnb.res], [lnb2.res], tag="ft_lnb")
        lnb.free()
        lnb = lnb2
        ang = kb.af.alloc(T, "ft_ang")
        kb.ts("vector", ang.ap(), it.ap(), invf.ap(0, 128, 0, 1), None,
              ALU.mult, ALU.bypass, [it.res, invf.res], [ang.res],
              sreads=[invf.res], tag="ft_ang")
        invf.free()
        sinb, cosb = self.sincos_from_angle(ang, "ftsc")
        ang.free()
        pw = kb.af.alloc(T, "ft_pw")
        kb.ts("vector", pw.ap(), it.ap(), 1.0 / 256.0, -(T // 2) / 256.0,
              ALU.mult, ALU.add, [it.res], [pw.res], tag="ft_pw")
        it.free()
        sc = kb.af.alloc(T, "ft_sc")
        kb.ts("vector", sc.ap(), pw.ap(), lnb.ap(0, 128, 0, 1), None,
              ALU.mult, ALU.bypass, [pw.res, lnb.res], [sc.res],
              sreads=[lnb.res], tag="ft_scm")
        pw.free()
        lnb.free()
        kb.act(sc.ap(), sc.ap(), AF.Exp, [sc.res], [sc.res], tag="ft_sce")
        dap, dres = kb.dram["ftab"]
        tmp = kb.ab.alloc(T, "ft_t")
        for j, (trig, trig_res) in enumerate(((cosb.ap(), cosb.res),
                                              (sinb.ap(), sinb.res))):
            kb.tt("vector", tmp.ap(), trig, sc.ap(), ALU.mult,
                  [trig_res, sc.res], [tmp.res], tag=f"ft_q{j}")
            kb.dma("sync", dap[j], tmp.ap(), [tmp.res], [dres],
                   tag=f"ft_sq{j}")
        rsc = kb.af.alloc(T, "ft_rs")
        kb.act(rsc.ap(), sc.ap(), AF.Ln, [sc.res], [rsc.res], tag="ft_lsc")
        kb.act(rsc.ap(), rsc.ap(), AF.Exp, [rsc.res], [rsc.res], scale=-1.0,
               sreads=[rsc.res], tag="ft_esc")
        sc.free()
        for j, (trig, trig_res) in enumerate(((cosb.ap(), cosb.res),
                                              (sinb.ap(), sinb.res))):
            kb.tt("vector", tmp.ap(), trig, rsc.ap(), ALU.mult,
                  [trig_res, rsc.res], [tmp.res], tag=f"ft_k{j}")
            kb.dma("sync", dap[2 + j], tmp.ap(), [tmp.res], [dres],
                   tag=f"ft_sk{j}")
        rsc.free()
        tmp.free()
        sinb.free()
        cosb.free()

    def build_attnb_tables(self):
        kb = self.kb
        it = self.iota_row_tile("bt_iota")
        pcol_ap, pcol_res = self.kb.cap("mod16_col")
        invf = kb.af.alloc(16, "bt_if")
        kb.act(invf.ap(0, 128, 0, 1), pcol_ap, AF.Exp,
               [pcol_res], [invf.res],
               scale=-2.0 * math.log(10000.0) / 32.0, tag="bt_inv")
        ang = kb.af.alloc(T, "bt_ang")
        kb.ts("vector", ang.ap(), it.ap(), invf.ap(0, 128, 0, 1), None,
              ALU.mult, ALU.bypass, [it.res, invf.res], [ang.res],
              sreads=[invf.res], tag="bt_ang")
        invf.free()
        it.free()
        sinb, cosb = self.sincos_from_angle(ang, "btsc")
        ang.free()
        dap, dres = kb.dram["btab"]
        kb.dma("sync", dap[0], cosb.ap(), [cosb.res], [dres], tag="bt_sc")
        kb.dma("sync", dap[1], sinb.ap(), [sinb.res], [dres], tag="bt_ss")
        sinb.free()
        cosb.free()


    def ffn_layer(self, y, l, inst):
        kb = self.kb
        dbg = (inst == 0 and l == 0)
        tg = f"F{inst}{l}"
        pre_qkv = [self.sm_prewarm("f_wqkv", NKT, [128] * 12, layer_idx=l,
                                   tag=f"{tg}pwq0", mb_cols=512,
                                   chunks_len=2), None]
        qkv = [kb.ab.alloc(T, f"{tg}qkv{m}") for m in range(12)]
        HT = T // 2
        for th in range(2):
            t0 = th * HT
            h_bf = []
            for kt in range(NKT):
                b = kb.ab.alloc(HT, f"{tg}h{th}_{kt}")
                eng = "scalar" if kt % 2 else "vector"
                if kt < 4:
                    kb.copy(eng, b.ap(), y[kt].ap(0, 128, t0, t0 + HT),
                            [y[kt].res], [b.res], tag=f"{tg}hc{th}{kt}")
                elif t0 == 0:
                    kb.memset("vector", b.ap(0, 128, 0, 1), 0.0, [b.res],
                              tag=f"{tg}hz{kt}")
                    kb.copy(eng, b.ap(0, 128, 1, HT),
                            y[kt].ap(0, 128, 0, HT - 1), [y[kt].res],
                            [b.res], tag=f"{tg}hs{th}{kt}")
                else:
                    kb.copy(eng, b.ap(0, 128, 0, HT),
                            y[kt].ap(0, 128, t0 - 1, t0 + HT - 1),
                            [y[kt].res], [b.res], tag=f"{tg}hs{th}{kt}")
                h_bf.append(b)
            chunks_h = [(t0, CW), (t0 + CW, CW)]

            def evac_qkv(mi, ci, ps_ap, pres, chunks_h=chunks_h):
                cc = chunks_h[ci][0]
                kb.copy("scalar", qkv[mi].ap(0, 128, cc, cc + CW),
                        ps_ap, [pres], [qkv[mi].res],
                        tag=f"{tg}qkve{th}{mi}_{ci}")

            self.stream_matmul("f_wqkv", NKT, [128] * 12,
                               lambda kt, ci, h_bf=h_bf: (
                                   h_bf[kt].ap(0, 128, ci * CW,
                                               (ci + 1) * CW),
                                   [h_bf[kt].res]),
                               evac_qkv, layer_idx=l, chunks=chunks_h,
                               tag=f"{tg}qkv{th}", mb_cols=512,
                               pre=pre_qkv[th])
            if th == 0:
                pre_qkv[1] = self.sm_prewarm(
                    "f_wqkv", NKT, [128] * 12, layer_idx=l,
                    tag=f"{tg}pwq1", mb_cols=512, chunks_len=2)
            for b in h_bf:
                b.free()

        if dbg and self.stage == 201:
            self.early_out = [(qkv[i], 0, 128) for i in range(8)]
            return
        qs_col = kb.af.alloc(16, f"{tg}qsc")
        ks_col = kb.af.alloc(16, f"{tg}ksc")
        for nm, buf in (("f_qs", qs_col), ("f_ks", ks_col)):
            dap, dres = kb.dram[nm]
            kb.dma("sync", kb.af.t[0:128, buf.c0:buf.c0 + 4], dap[l],
                   [dres], [buf.res], tag=f"{tg}ld{nm}")
        tab_dap, tab_dres = kb.dram["ftab"]
        qt_out = []
        for side in range(2):
            tiles = qkv[side * 4:side * 4 + 4]
            scol = qs_col if side == 0 else ks_col
            oh_ap, oh_res, _ = kb.cst["l2_onehot"]
            banks = kb.take_banks(4)
            q2 = [kb.ab.alloc(T, f"{tg}q2_{side}{i}") for i in range(2)]
            for i in range(4):
                q2b = q2[i % 2]
                kb.tt("vector", q2b.ap(), tiles[i].ap(), tiles[i].ap(),
                      ALU.mult, [tiles[i].res], [q2b.res],
                      tag=f"{tg}sq{side}{i}")
                for c in range(NCH):
                    bk = banks[c]
                    kb.mm(kb.bank_ap(bk, 0, CW, 0, 8),
                          kb.cst["l2_onehot"][0][:, i * 8:(i + 1) * 8],
                          q2b.ap(0, 128, c * CW, (c + 1) * CW),
                          i == 0, i == 3, [oh_res, q2b.res],
                          [kb.pbank[bk]], tag=f"{tg}l2m{side}{i}{c}")
            for b in q2:
                b.free()
            nrm = kb.af.alloc(T, f"{tg}nrm{side}")
            for c in range(NCH):
                bk = banks[c]
                kb.ts("vector", nrm.ap(0, 8, c * CW, (c + 1) * CW),
                      kb.bank_ap(bk, 0, CW, 0, 8), 1e-24, None,
                      ALU.max, ALU.bypass, [kb.pbank[bk]], [nrm.res],
                      tag=f"{tg}l2c{side}{c}")
            rqf = kb.af.alloc(T, f"{tg}rqf{side}")
            kb.act(rqf.ap(0, 8), nrm.ap(0, 8), AF.Ln, [nrm.res], [rqf.res],
                   tag=f"{tg}l2l{side}")
            rq = kb.ab.alloc(T, f"{tg}rq{side}")
            kb.act(rq.ap(0, 8), rqf.ap(0, 8), AF.Exp, [rqf.res], [rq.res],
                   scale=-0.5, tag=f"{tg}l2e{side}")
            nrm.free()
            rqf.free()
            cs = kb.ab.alloc(T, f"{tg}cs{side}")
            ss = kb.ab.alloc(T, f"{tg}ss{side}")
            kb.dma("sync", cs.ap(), tab_dap[2 * side + 0], [tab_dres],
                   [cs.res], tag=f"{tg}tc{side}")
            kb.dma("sync", ss.ap(), tab_dap[2 * side + 1], [tab_dres],
                   [ss.res], tag=f"{tg}ts{side}")
            if dbg and self.stage == 2021 and side == 0:
                self.early_out = [(cs, 0, 128), (ss, 0, 128)]
                return
            a_t = [kb.ab.alloc(T, f"{tg}a{side}{i}") for i in range(4)]
            ohT_ap, ohT_res, _ = kb.cst["l2_onehot_T"]
            bks2 = kb.take_banks(2)
            for i in range(4):
                for c in range(NCH):
                    bk = bks2[c % 2]
                    kb.mm(kb.bank_ap(bk), kb.cst["l2_onehot_T"][0][:, i * 128:(i + 1) * 128],
                          rq.ap(0, 8, c * CW, (c + 1) * CW), True, True,
                          [ohT_res, rq.res], [kb.pbank[bk]],
                          tag=f"{tg}rqb{side}{i}{c}")
                    kb.stt("vector", a_t[i].ap(0, 128, c * CW, (c + 1) * CW),
                           tiles[i].ap(0, 128, c * CW, (c + 1) * CW),
                           scol.ap(0, 128, i, i + 1), kb.bank_ap(bk),
                           ALU.mult, ALU.mult,
                           [tiles[i].res, scol.res, kb.pbank[bk]],
                           [a_t[i].res], sreads=[scol.res],
                           tag=f"{tg}am{side}{i}{c}")
                tiles[i].free()
            rq.free()
            tsw = kb.ab.alloc(T, f"{tg}tsw{side}")
            for i in range(4):
                prt = (i + 2) % 4
                ob = kb.ab.alloc(T, f"{tg}rp{side}{i}")
                kb.tt("vector", tsw.ap(), a_t[prt].ap(), ss.ap(), ALU.mult,
                      [a_t[prt].res, ss.res], [tsw.res],
                      tag=f"{tg}rs{side}{i}")
                kb.tt("vector", ob.ap(), a_t[i].ap(), cs.ap(), ALU.mult,
                      [a_t[i].res, cs.res], [ob.res], tag=f"{tg}rc{side}{i}")
                kb.tt("vector", ob.ap(), ob.ap(), tsw.ap(),
                      ALU.subtract if i < 2 else ALU.add,
                      [ob.res, tsw.res], [ob.res], tag=f"{tg}ra{side}{i}")
                qt_out.append(ob)
            tsw.free()
            for b in a_t:
                b.free()
            cs.free()
            ss.free()
        qs_col.free()
        ks_col.free()
        if self.early_out is not None:
            return
        qr_t = qt_out[0:4]
        kr_t = qt_out[4:8]
        if dbg and self.stage == 202:
            self.early_out = [(b, 0, 128) for b in qt_out]
            return

        v_t = qkv[8:12]
        vtok = [kb.ab.alloc(16 * 65 + 16, f"{tg}vt{h}") for h in range(8)]
        for h in range(8):
            kb.memset("vector", vtok[h].ap(0, 128, 0, 16 * 65), 1.0,
                      [vtok[h].res], tag=f"{tg}vz{h}")
        bks = kb.take_banks(2)
        for i in range(4):
            for tc in range(16):
                bk = bks[tc % 2]
                kb.transp(kb.bank_bap(bk, 0, 128),
                          v_t[i].ap(0, 128, tc * 128, (tc + 1) * 128),
                          [v_t[i].res], [kb.pbank[bk]], tag=f"{tg}vt{i}_{tc}",
                          bf16=True)
                for hh in range(2):
                    h = 2 * i + hh
                    kb.copy("vector",
                            vtok[h].ap(0, 128, tc * 65, tc * 65 + 64),
                            kb.bank_bap(bk, hh * 64, hh * 64 + 64, 0, 128),
                            [kb.pbank[bk]], [vtok[h].res],
                            tag=f"{tg}ve{h}_{tc}")
        for b in v_t:
            b.free()

        att = [kb.ab.alloc(T, f"{tg}att{i}") for i in range(4)]
        self.local_attention(qr_t, kr_t, vtok, att, tg)
        for b in qr_t + kr_t:
            b.free()
        for b in vtok:
            b.free()
        if dbg and self.stage == 203:
            self.early_out = [(b, 0, 128) for b in att]
            return

        m_widths_w1 = [128] * 42 + [42, 42]
        pre_w1 = self.sm_prewarm("f_w1", NKT, m_widths_w1, layer_idx=l,
                                 tag=f"{tg}pw1a", mb_cols=512, chunks_len=1)

        def evac_wo(mi, ci, ps_ap, pres):
            kb.tt("vector", y[mi].ap(0, 128, ci * CW, (ci + 1) * CW),
                  y[mi].ap(0, 128, ci * CW, (ci + 1) * CW), ps_ap, ALU.add,
                  [y[mi].res, pres], [y[mi].res], tag=f"{tg}wor{mi}_{ci}")

        self.stream_matmul("f_wo", 4, [128] * 8,
                           lambda kt, ci: (att[kt].ap(0, 128, ci * CW,
                                                      (ci + 1) * CW),
                                           [att[kt].res]),
                           evac_wo, layer_idx=l, tag=f"{tg}wo")
        for b in att:
            b.free()
        if dbg and self.stage == 204:
            self.early_out = [(b, 0, 128) for b in y]
            return

        h2_bf = []
        for kt in range(NKT):
            b = kb.ab.alloc(T, f"{tg}h2{kt}")
            eng = "scalar" if kt % 2 else "vector"
            if kt < 4:
                kb.copy(eng, b.ap(), y[kt].ap(), [y[kt].res], [b.res],
                        tag=f"{tg}h2c{kt}")
            else:
                kb.memset("vector", b.ap(0, 128, 0, 1), 0.0, [b.res],
                          tag=f"{tg}h2z{kt}")
                kb.copy(eng, b.ap(0, 128, 1, T),
                        y[kt].ap(0, 128, 0, T - 1), [y[kt].res], [b.res],
                        tag=f"{tg}h2s{kt}")
            h2_bf.append(b)
        lngb = kb.af.alloc(16, f"{tg}lngb")
        for j, nm in enumerate(("f_lng", "f_lnb")):
            dap, dres = kb.dram[nm]
            kb.dma("sync", kb.af.t[0:128, lngb.c0 + 8 * j:lngb.c0 + 8 * j + 8],
                   dap[l], [dres], [lngb.res], tag=f"{tg}lngb{j}")
        hn_bf = self.materialize_ln(
            lambda kt: (h2_bf[kt].ap(), [h2_bf[kt].res]),
            lambda kt: (h2_bf[kt], False),
            g_b=(lngb, 0, 8), tag=f"{tg}ln")
        for b in h2_bf:
            b.free()
        lngb.free()

        b1c = kb.af.alloc(48, f"{tg}b1c")
        dap, dres = kb.dram["f_b1"]
        kb.dma("sync", kb.af.t[0:128, b1c.c0:b1c.c0 + 44], dap[l], [dres],
               [b1c.res], tag=f"{tg}b1c")
        b2c = kb.af.alloc(16, f"{tg}b2c")
        dap, dres = kb.dram["f_b2"]
        kb.dma("sync", kb.af.t[0:128, b2c.c0:b2c.c0 + 8], dap[l], [dres],
               [b2c.res], tag=f"{tg}b2c")

        k_rows_w2 = [(i * 128, 128) for i in range(21)] + [(2688, 42)]
        TCW2 = 512
        for tc0 in range(0, T, TCW2):
            chunks = [(tc0, CW)]
            hid = [kb.ab.alloc(TCW2, f"{tg}hid{p}") for p in range(22)]
            atmp = [kb.ab.alloc(TCW2, f"{tg}at{p % 2}") for p in range(2)]

            def evac_w1(mi, ci, ps_ap, pres, t0=tc0, hid=hid, atmp=atmp):
                pair = mi // 2
                pw = 128 if pair < 21 else 42
                cc0 = ci * CW
                if mi % 2 == 0:
                    kb.act(atmp[pair % 2].ap(0, pw, cc0, cc0 + CW), ps_ap,
                           AF.Identity, [pres, b1c.res],
                           [atmp[pair % 2].res],
                           bias=b1c.ap(0, pw, mi, mi + 1), sreads=[b1c.res],
                           tag=f"{tg}w1a{mi}_{ci}")
                else:
                    gg = kb.ab.alloc(CW, f"{tg}gg")
                    kb.act(gg.ap(0, pw), ps_ap, AF.Gelu, [pres, b1c.res],
                           [gg.res], bias=b1c.ap(0, pw, mi, mi + 1),
                           sreads=[b1c.res], tag=f"{tg}w1g{mi}_{ci}")
                    kb.tt("vector", hid[pair].ap(0, pw, cc0, cc0 + CW),
                          atmp[pair % 2].ap(0, pw, cc0, cc0 + CW),
                          gg.ap(0, pw), ALU.mult,
                          [atmp[pair % 2].res, gg.res], [hid[pair].res],
                          tag=f"{tg}w1h{mi}_{ci}")
                    gg.free()

            pre_w2 = self.sm_prewarm("f_w2", 22, [128] * 8, layer_idx=l,
                                     tag=f"{tg}pw2_{tc0}", mb_cols=256,
                                     k_group=8, k_rows=k_rows_w2,
                                     chunks_len=1)
            self.stream_matmul("f_w1", NKT, m_widths_w1,
                               lambda kt, ci: (hn_bf[kt].ap(0, 128,
                                                            chunks[ci][0],
                                                            chunks[ci][0] + CW),
                                               [hn_bf[kt].res]),
                               evac_w1, layer_idx=l, chunks=chunks,
                               tag=f"{tg}w1", mb_cols=512, pre=pre_w1)
            if tc0 + TCW2 < T:
                pre_w1 = self.sm_prewarm("f_w1", NKT, m_widths_w1,
                                         layer_idx=l, tag=f"{tg}pw1_{tc0}",
                                         mb_cols=512, chunks_len=1)

            def evac_w2(mi, ci, ps_ap, pres, hid=hid):
                t0 = chunks[ci][0]
                kb.stt("vector", y[mi].ap(0, 128, t0, t0 + CW), ps_ap,
                       b2c.ap(0, 128, mi, mi + 1),
                       y[mi].ap(0, 128, t0, t0 + CW), ALU.add, ALU.add,
                       [pres, y[mi].res, b2c.res], [y[mi].res],
                       sreads=[b2c.res], tag=f"{tg}w2r{mi}_{ci}")

            self.stream_matmul("f_w2", 22, [128] * 8,
                               lambda kt, ci: (hid[kt].ap(0, k_rows_w2[kt][1],
                                                          ci * CW,
                                                          (ci + 1) * CW),
                                               [hid[kt].res]),
                               evac_w2, layer_idx=l, chunks=chunks,
                               k_rows=k_rows_w2, k_group=8, tag=f"{tg}w2",
                               mb_cols=256, pre=pre_w2)
            for b in hid:
                b.free()
            for b in atmp:
                b.free()
        for b in hn_bf:
            b.free()
        b1c.free()
        b2c.free()

    def local_attention(self, qr_t, kr_t, vtok, att, tg):
        kb = self.kb
        mle, mle_res = kb.cap("m_leb")
        mgt, mgt_res = kb.cap("m_gtb")
        mlt, mlt_res = kb.cap("m_ltb")
        mge, mge_res = kb.cap("m_geb")
        idb, idb_res = kb.cap("identb")
        fl_dap, fl_res = kb.dram["flip"]
        for h in range(8):
            ti = h // 4
            r0 = 32 * (h % 4)
            kaug = kb.ab.alloc(2560, f"{tg}ka{h}")
            kb.memset("vector", kaug.ap(0, 64, 0, 512), 0.0, [kaug.res],
                      tag=f"{tg}kz{h}")
            kb.memset("vector", kaug.ap(64, 65, 0, 2560), -1.0, [kaug.res],
                      tag=f"{tg}kn{h}")
            kb.copy("vector", kaug.ap(0, 32, 512, 2560),
                    kr_t[ti].ap(r0, r0 + 32, 0, T), [kr_t[ti].res],
                    [kaug.res], tag=f"{tg}kc1{h}")
            kb.copy("vector", kaug.ap(32, 64, 512, 2560),
                    kr_t[2 + ti].ap(r0, r0 + 32, 0, T), [kr_t[2 + ti].res],
                    [kaug.res], tag=f"{tg}kc2{h}")
            qaug = kb.ab.alloc(T, f"{tg}qa{h}")
            kb.copy("vector", qaug.ap(0, 32, 0, T),
                    qr_t[ti].ap(r0, r0 + 32, 0, T), [qr_t[ti].res],
                    [qaug.res], tag=f"{tg}qc1{h}")
            kb.copy("vector", qaug.ap(32, 64, 0, T),
                    qr_t[2 + ti].ap(r0, r0 + 32, 0, T), [qr_t[2 + ti].res],
                    [qaug.res], tag=f"{tg}qc2{h}")
            coll = kb.af.alloc(16, f"{tg}cl{h}")
            bkp = [kb.take_banks(2), kb.take_banks(2)]
            for qt in range(16):
                bks = bkp[qt % 2]
                qt_ap = qaug.ap(0, 64, qt * 128, (qt + 1) * 128)
                if qt >= 4:
                    kb.mm(kb.wide_ap(bks, 0, 128), qt_ap,
                          kaug.ap(0, 64, qt * 128, qt * 128 + 128),
                          True, False, [qaug.res, kaug.res],
                          [kb.pbank[bks[0]]], tag=f"{tg}sp{h}_{qt}a0")
                    kb.mm(kb.wide_ap(bks, 0, 128), idb, mle,
                          False, True, [idb_res, mle_res],
                          [kb.pbank[bks[0]]], tag=f"{tg}sm1{h}_{qt}")
                    kb.mm(kb.wide_ap(bks, 128, 512), qt_ap,
                          kaug.ap(0, 64, qt * 128 + 128, qt * 128 + 512),
                          True, True, [qaug.res, kaug.res],
                          [kb.pbank[bks[0]]], tag=f"{tg}sp{h}_{qt}a1")
                else:
                    kb.mm(kb.wide_ap(bks, 0, 512), qt_ap,
                          kaug.ap(0, 64, qt * 128, qt * 128 + 512),
                          True, True, [qaug.res, kaug.res],
                          [kb.pbank[bks[0]]], tag=f"{tg}sp{h}_{qt}a")
                kb.mm(kb.wide_ap(bks, 512, 640), qt_ap,
                      kaug.ap(0, 64, qt * 128 + 512, qt * 128 + 640),
                      True, False, [qaug.res, kaug.res],
                      [kb.pbank[bks[1]]], tag=f"{tg}sp{h}_{qt}b")
                kb.mm(kb.wide_ap(bks, 512, 640), idb, mgt,
                      False, True, [idb_res, mgt_res],
                      [kb.pbank[bks[1]]], tag=f"{tg}sm2{h}_{qt}")
                c0 = 128 * max(0, 4 - qt)
                kb.reduce("vector", coll.ap(0, 128, qt, qt + 1),
                          kb.wide_ap(bks, c0, 640), ALU.max,
                          [kb.pbank[bks[0]], kb.pbank[bks[1]]], [coll.res],
                          tag=f"{tg}rmx{h}_{qt}")
            bkf = kb.take_banks(1)[0]
            kb.transp(kb.bank_ap(bkf, 0, 128, 0, 16), coll.ap(0, 128, 0, 16),
                      [coll.res], [kb.pbank[bkf]], tag=f"{tg}tp{h}")
            ftmp = kb.af.alloc(128, f"{tg}ft{h}")
            kb.copy("vector", ftmp.ap(0, 16, 0, 128),
                    kb.bank_ap(bkf, 0, 128, 0, 16), [kb.pbank[bkf]],
                    [ftmp.res], tag=f"{tg}fc{h}")
            kb.dma("sync", fl_dap, ftmp.ap(0, 16, 0, 128), [ftmp.res],
                   [fl_res], tag=f"{tg}fd{h}")
            kb.dma("gpsimd", qaug.ap(64, 65, 0, T),
                   fl_dap.rearrange("a b -> (a b)"), [fl_res], [qaug.res],
                   tag=f"{tg}fl{h}")
            ftmp.free()
            coll.free()
            pt_bufs = [kb.ab.alloc(640, f"{tg}pt{j}") for j in range(7)]
            bsp = [kb.take_banks(2), kb.take_banks(2)]
            pvb2 = [kb.take_banks(1)[0], kb.take_banks(1)[0]]
            bcb = kb.take_banks(1)[0]
            rrs2 = [kb.af.alloc(128, f"{tg}rrs{h}{j}") for j in range(2)]
            pvs = kb.ab.alloc(128, f"{tg}pvs{h}")

            def do_pv(qt):
                pvb = pvb2[qt % 2]
                kts = list(range(max(0, qt - 4), qt + 1))
                n = len(kts)
                for j, kta in enumerate(kts):
                    src_pt = pt_bufs[(kta + 4) % 7]
                    qoff = 128 * (qt - kta)
                    kb.mm(kb.bank_ap(pvb, 0, 128, 0, 65),
                          vtok[h].ap(0, 128, kta * 65, kta * 65 + 65),
                          src_pt.ap(0, 128, qoff, qoff + 128),
                          j == 0, j == n - 1,
                          [vtok[h].res, src_pt.res], [kb.pbank[pvb]],
                          tag=f"{tg}pv{h}_{qt}_{kta}")
                self._pv_ln(pvb, rrs2[qt % 2], h, qt * 128, 128, tg)
                if qt >= 1:
                    self._pv_norm(pvb2[(qt - 1) % 2], bcb, rrs2[(qt - 1) % 2],
                                  pvs, att, h, (qt - 1) * 128, 128, tg)
                if qt == 15:
                    self._pv_norm(pvb2[15 % 2], bcb, rrs2[15 % 2], pvs, att,
                                  h, 15 * 128, 128, tg)

            for KT in range(4, 20):
                bse = bsp[KT % 2]
                qlo = 128 * (KT - 4)
                qhi = min(T, 128 * (KT + 1))
                W = qhi - qlo
                kt_ap = kaug.ap(0, 65, KT * 128, KT * 128 + 128)
                kb.mm(kb.wide_ap(bse, 0, 128), kt_ap,
                      qaug.ap(0, 65, qlo, qlo + 128),
                      True, False, [kaug.res, qaug.res],
                      [kb.pbank[bse[0]]], tag=f"{tg}st{h}_{KT}_0")
                kb.mm(kb.wide_ap(bse, 0, 128), idb, mlt,
                      False, True, [idb_res, mlt_res],
                      [kb.pbank[bse[0]]], tag=f"{tg}em1{h}_{KT}")
                if W > 128:
                    pw = min(W, 512)
                    kb.mm(kb.wide_ap(bse, 128, pw), kt_ap,
                          qaug.ap(0, 65, qlo + 128, qlo + pw),
                          True, True, [kaug.res, qaug.res],
                          [kb.pbank[bse[0]]], tag=f"{tg}st{h}_{KT}_1")
                if W == 640:
                    kb.mm(kb.wide_ap(bse, 512, 640), kt_ap,
                          qaug.ap(0, 65, qlo + 512, qlo + 640),
                          True, False, [kaug.res, qaug.res],
                          [kb.pbank[bse[1]]], tag=f"{tg}st{h}_{KT}_2")
                if W == 640:
                    kb.mm(kb.wide_ap(bse, 512, 640), idb, mge,
                          False, True, [idb_res, mge_res],
                          [kb.pbank[bse[1]]], tag=f"{tg}em2{h}_{KT}")
                pt = pt_bufs[KT % 7]
                kb.act(pt.ap(0, 128, 0, W), kb.wide_ap(bse, 0, W), AF.Exp,
                       [kb.pbank[bse[0]], kb.pbank[bse[1]]], [pt.res],
                       scale=8.0, tag=f"{tg}ex{h}_{KT}")
                if KT >= 5:
                    do_pv(KT - 5)
            do_pv(15)
            for b in rrs2:
                b.free()
            pvs.free()
            for b in pt_bufs:
                b.free()
            kaug.free()
            qaug.free()

    def _pv_ln(self, pvb, rrs, h, q0, qw, tg):
        kb = self.kb
        kb.act(rrs.ap(0, 1, 0, qw), kb.bank_ap(pvb, 0, qw, 64, 65), AF.Ln,
               [kb.pbank[pvb]], [rrs.res], tag=f"{tg}rl{h}_{q0}")
        kb.act(rrs.bap(0, 1, 2 * qw, 3 * qw), rrs.ap(0, 1, 0, qw), AF.Exp,
               [rrs.res], [rrs.res], scale=-1.0, tag=f"{tg}re{h}_{q0}")

    def _pv_norm(self, pvb, bcb, rrs, pvs, att, h, q0, qw, tg):
        kb = self.kb
        ones1b, o1b_res = kb.cap("ones1b")
        kb.mm(kb.bank_ap(bcb, 0, qw, 0, 64), ones1b[0:1, 0:64],
              rrs.bap(0, 1, 2 * qw, 3 * qw), True, True, [o1b_res, rrs.res],
              [kb.pbank[bcb]], tag=f"{tg}bc{h}_{q0}")
        kb.copy("vector", pvs.ap(0, 64, 0, qw),
                kb.bank_ap(pvb, 0, qw, 0, 64), [kb.pbank[pvb]], [pvs.res],
                tag=f"{tg}pe{h}_{q0}")
        r0 = 64 * (h % 2)
        kb.tt("vector", att[h // 2].ap(r0, r0 + 64, q0, q0 + qw),
              pvs.ap(0, 64, 0, qw), kb.bank_ap(bcb, 0, qw, 0, 64), ALU.mult,
              [pvs.res, kb.pbank[bcb]], [att[h // 2].res],
              tag=f"{tg}pn{h}_{q0}")


    def attn_b(self, x_bf, x3):
        kb = self.kb
        tg = "B"
        negm, rr = self.stats_rows(lambda kt: (x_bf[kt], False), tg + "st")
        r_b = self.bcast_row(rr, tg + "r")
        rr.free()
        negm_bf = kb.ab.alloc(T, tg + "nm")
        kb.copy("vector", negm_bf.ap(0, 1), negm.ap(0, 1), [negm.res],
                [negm_bf.res], tag=tg + "nmc")
        negm.free()

        ocol, ocol_res = kb.cap("ones_col")
        projs = {}
        for nm, mtot in (("b_dq", 256), ("b_dkv", 256), ("b_kr", 64)):
            dap, dres = kb.wbf[nm]
            slabs = []
            for kt in range(NKT):
                sb = kb.ab.alloc(mtot, f"{tg}{nm}{kt}")
                kb.dma(kb.wqueue(), sb.ap(0, 128, 0, mtot),
                       dap[kt * 128:(kt + 1) * 128, :], [dres], [sb.res],
                       tag=f"{tg}{nm}d{kt}")
                slabs.append(sb)
            wsum = kb.ab.alloc(mtot, f"{tg}{nm}ws")
            bk = kb.take_banks(1)[0]
            for kt in range(NKT):
                kb.mm(kb.bank_ap(bk, 0, mtot, 0, 1), ocol,
                      slabs[kt].ap(0, 128, 0, mtot), kt == 0, kt == NKT - 1,
                      [ocol_res, slabs[kt].res], [kb.pbank[bk]],
                      tag=f"{tg}{nm}ws{kt}")
            kb.copy("vector", wsum.ap(0, 1, 0, mtot),
                    kb.bank_ap(bk, 0, mtot, 0, 1), [kb.pbank[bk]],
                    [wsum.res], tag=f"{tg}{nm}wse")
            projs[nm] = (slabs, wsum)

        def run_proj(nm, mtot, evac):
            slabs, wsum = projs[nm]
            n_mt = (mtot + 127) // 128
            bks = kb.take_banks(2) + kb.take_banks(2)
            bi = 0
            for mi in range(n_mt):
                mw = min(128, mtot - mi * 128)
                for c in range(NCH):
                    bk = bks[bi % 4]
                    bi += 1
                    for kt in range(NKT):
                        kb.mm(kb.bank_ap(bk, 0, CW, 0, mw),
                              slabs[kt].ap(0, 128, mi * 128, mi * 128 + mw),
                              x_bf[kt].ap(0, 128, c * CW, (c + 1) * CW),
                              kt == 0, False,
                              [slabs[kt].res, x_bf[kt].res], [kb.pbank[bk]],
                              tag=f"{tg}{nm}m{mi}_{c}_{kt}")
                    kb.mm(kb.bank_ap(bk, 0, CW, 0, mw),
                          wsum.ap(0, 1, mi * 128, mi * 128 + mw),
                          negm_bf.ap(0, 1, c * CW, (c + 1) * CW), False, True,
                          [wsum.res, negm_bf.res], [kb.pbank[bk]],
                          tag=f"{tg}{nm}a{mi}_{c}")
                    evac(mi, c, kb.bank_ap(bk, 0, CW, 0, mw), kb.pbank[bk])

        cq = kb.ab.alloc(2 * T, tg + "cq")
        ckv = kb.ab.alloc(2 * T, tg + "ckv")
        krf = kb.af.alloc(T, tg + "krf")

        def ev_cq(mi, c, ps, pres):
            kb.tt("vector", cq.ap(0, 128, mi * T + c * CW, mi * T + (c + 1) * CW),
                  ps, r_b.ap(0, 128, c * CW, (c + 1) * CW), ALU.mult,
                  [pres, r_b.res], [cq.res], tag=f"{tg}cqe{mi}{c}")

        def ev_ckv(mi, c, ps, pres):
            kb.tt("vector", ckv.ap(0, 128, mi * T + c * CW, mi * T + (c + 1) * CW),
                  ps, r_b.ap(0, 128, c * CW, (c + 1) * CW), ALU.mult,
                  [pres, r_b.res], [ckv.res], tag=f"{tg}ckve{mi}{c}")

        def ev_kr(mi, c, ps, pres):
            kb.tt("vector", krf.ap(0, 64, c * CW, (c + 1) * CW), ps,
                  r_b.ap(0, 64, c * CW, (c + 1) * CW), ALU.mult,
                  [pres, r_b.res], [krf.res], tag=f"{tg}kre{c}")

        run_proj("b_dq", 256, ev_cq)
        run_proj("b_dkv", 256, ev_ckv)
        run_proj("b_kr", 64, ev_kr)
        for nm in ("b_dq", "b_dkv", "b_kr"):
            slabs, wsum = projs[nm]
            for sb in slabs:
                sb.free()
            wsum.free()
        negm_bf.free()
        r_b.free()
        for b in x_bf:
            b.free()

        cosb = kb.af.alloc(T, tg + "cos")
        sinb = kb.af.alloc(T, tg + "sin")
        bt_dap, bt_res = kb.dram["btab"]
        kb.dma("sync", cosb.ap(), bt_dap[0], [bt_res], [cosb.res],
               tag=tg + "tc")
        kb.dma("sync", sinb.ap(), bt_dap[1], [bt_res], [sinb.res],
               tag=tg + "ts")
        krb = kb.ab.alloc(T, tg + "krb")
        tmp = kb.af.alloc(T, tg + "krt")
        kb.tt("vector", tmp.ap(0, 16), krf.ap(32, 48), sinb.ap(32, 48),
              ALU.mult, [krf.res, sinb.res], [tmp.res], tag=tg + "kr1")
        kb.tt("vector", krb.ap(0, 16), krf.ap(0, 16), cosb.ap(0, 16),
              ALU.mult, [krf.res, cosb.res], [krb.res], tag=tg + "kr2")
        kb.tt("vector", krb.ap(0, 16), krb.ap(0, 16), tmp.ap(0, 16),
              ALU.subtract, [krb.res, tmp.res], [krb.res], tag=tg + "kr3")
        kb.tt("vector", tmp.ap(32, 48), krf.ap(0, 16), sinb.ap(0, 16),
              ALU.mult, [krf.res, sinb.res], [tmp.res], tag=tg + "kr4")
        kb.tt("vector", krb.ap(32, 48), krf.ap(32, 48), cosb.ap(32, 48),
              ALU.mult, [krf.res, cosb.res], [krb.res], tag=tg + "kr5")
        kb.tt("vector", krb.ap(32, 48), krb.ap(32, 48), tmp.ap(32, 48),
              ALU.add, [krb.res, tmp.res], [krb.res], tag=tg + "kr6")
        tmp.free()
        krf.free()

        sp2_ap, sp2_res = kb.dram["x_spill2"]
        for grp in range(2):
            att = [kb.ab.alloc(T, f"{tg}att{grp}{i}") for i in range(4)]
            pre_bwo = self.sm_prewarm("b_wo", 4, [128] * 8,
                                      dram_row0=grp * 512,
                                      tag=f"{tg}pwwo{grp}", chunks_len=4)
            self._attnb_group(grp, cq, ckv, krb, cosb, sinb, att, tg)
            xs = {}

            def evac_wo(mi, ci, ps_ap, pres, grp=grp):
                if ci == 0:
                    src_nm = "x_spill" if grp == 0 else "x_spill2"
                    xs[mi] = self.load_x_tile(src_nm, mi, f"{tg}xs{grp}{mi}")
                    x3[mi] = kb.af.alloc(T, f"{tg}x3_{grp}{mi}")
                kb.tt("vector", x3[mi].ap(0, 128, ci * CW, (ci + 1) * CW),
                      xs[mi].ap(0, 128, ci * CW, (ci + 1) * CW), ps_ap,
                      ALU.add, [xs[mi].res, pres], [x3[mi].res],
                      tag=f"{tg}wor{grp}{mi}_{ci}")
                if ci == NCH - 1:
                    xs[mi].free()
                    if grp == 0:
                        kb.dma("sync", sp2_ap[mi * 128:(mi + 1) * 128, :],
                               x3[mi].ap(), [x3[mi].res], [sp2_res],
                               tag=f"{tg}sp2_{mi}")
                        x3[mi].free()
                        x3[mi] = None

            self.stream_matmul("b_wo", 4, [128] * 8,
                               lambda kt, ci: (att[kt].ap(0, 128, ci * CW,
                                                          (ci + 1) * CW),
                                               [att[kt].res]),
                               evac_wo, dram_row0=grp * 512,
                               tag=f"{tg}wo{grp}", pre=pre_bwo)
            for b in att:
                b.free()
        cosb.free()
        sinb.free()
        krb.free()
        cq.free()
        ckv.free()

    def _attnb_group(self, grp, cq, ckv, krb, cosb, sinb, att, tg):
        kb = self.kb
        SCL = 1.0 / math.sqrt(96.0)
        CHI = math.exp(80.0 / math.sqrt(96.0))
        CLO = math.exp(-80.0 / math.sqrt(96.0))
        m01, m01_res = kb.cap("m01_ge")
        qasm = [kb.af.alloc(T // 2, f"{tg}qa{grp}{i}") for i in range(8)]
        kasm = [kb.af.alloc(T // 2, f"{tg}ka{grp}{i}") for i in range(8)]

        uq_dap, uq_res = kb.wbf["b_uq"]
        uk_dap, uk_res = kb.wbf["b_uk"]
        uv_dap, uv_res = kb.wbf["b_uv"]
        qr_dap, qr_res = kb.wbf["b_qr"]

        def up_proj(dap, dres, mcol0, n_mt, evac, tagp, src):
            slabs = []
            for kt in range(2):
                sb = kb.ab.alloc(n_mt * 128, f"{tagp}w{kt}")
                kb.dma(kb.wqueue(), sb.ap(0, 128, 0, n_mt * 128),
                       dap[kt * 128:(kt + 1) * 128,
                           mcol0:mcol0 + n_mt * 128],
                       [dres], [sb.res], tag=f"{tagp}d{kt}")
                slabs.append(sb)
            bks = kb.take_banks(2) + kb.take_banks(2)
            bi = 0
            for mi in range(n_mt):
                for c in range(NCH):
                    bk = bks[bi % 4]
                    bi += 1
                    for kt in range(2):
                        kb.mm(kb.bank_ap(bk),
                              slabs[kt].ap(0, 128, mi * 128, (mi + 1) * 128),
                              src.ap(0, 128, kt * T + c * CW,
                                     kt * T + (c + 1) * CW),
                              kt == 0, kt == 1,
                              [slabs[kt].res, src.res], [kb.pbank[bk]],
                              tag=f"{tagp}m{mi}{c}{kt}")
                    evac(mi, c, bk, kb.pbank[bk])
            for sb in slabs:
                sb.free()

        def ev_uq2(mi, c, bk, pres):
            for hh in range(2):
                a = qasm[2 * mi + hh]
                kb.copy("scalar", a.bap(0, 64, c * CW, (c + 1) * CW),
                        kb.bank_ap(bk, 0, CW, 64 * hh, 64 * hh + 64),
                        [pres], [a.res], tag=f"{tg}uqe{grp}{mi}{c}{hh}")

        def ev_uk2(mi, c, bk, pres):
            for hh in range(2):
                a = kasm[2 * mi + hh]
                kb.copy("scalar", a.bap(0, 64, c * CW, (c + 1) * CW),
                        kb.bank_ap(bk, 0, CW, 64 * hh, 64 * hh + 64),
                        [pres], [a.res], tag=f"{tg}uke{grp}{mi}{c}{hh}")

        up_proj(uq_dap, uq_res, grp * 512, 4, ev_uq2, f"{tg}uq{grp}", cq)
        up_proj(uk_dap, uk_res, grp * 512, 4, ev_uk2, f"{tg}uk{grp}", ckv)

        v_fm = [kb.ab.alloc(T, f"{tg}v{grp}{i}") for i in range(4)]

        def ev_uv(mi, c, bk, pres):
            kb.copy("scalar", v_fm[mi].ap(0, 128, c * CW, (c + 1) * CW),
                    kb.bank_ap(bk), [pres], [v_fm[mi].res],
                    tag=f"{tg}uve{grp}{mi}{c}")

        up_proj(uv_dap, uv_res, grp * 512, 4, ev_uv, f"{tg}uv{grp}", ckv)

        vtok = [kb.ab.alloc(16 * 65 + 16, f"{tg}vt{grp}{h}") for h in range(8)]
        for h in range(8):
            kb.memset("vector", vtok[h].ap(0, 128, 0, 16 * 65), 1.0,
                      [vtok[h].res], tag=f"{tg}vz{grp}{h}")
        bks = kb.take_banks(2)
        for i in range(4):
            for tc in range(16):
                bk = bks[tc % 2]
                kb.transp(kb.bank_bap(bk, 0, 128),
                          v_fm[i].ap(0, 128, tc * 128, (tc + 1) * 128),
                          [v_fm[i].res], [kb.pbank[bk]],
                          tag=f"{tg}vt{grp}{i}_{tc}", bf16=True)
                for hh in range(2):
                    h = 2 * i + hh
                    kb.copy("vector",
                            vtok[h].ap(0, 128, tc * 65, tc * 65 + 64),
                            kb.bank_bap(bk, hh * 64, hh * 64 + 64, 0, 128),
                            [kb.pbank[bk]], [vtok[h].res],
                            tag=f"{tg}ve{grp}{h}_{tc}")
        for b in v_fm:
            b.free()

        qr_top = kb.af.alloc(T, f"{tg}qrt{grp}")
        qr_bot = kb.af.alloc(T, f"{tg}qrb{grp}")

        def ev_qr(mi, c, bk, pres):
            dst = qr_top if mi == 0 else qr_bot
            kb.copy("scalar", dst.ap(0, 128, c * CW, (c + 1) * CW),
                    kb.bank_ap(bk), [pres], [dst.res],
                    tag=f"{tg}qre{grp}{mi}{c}")

        def up_proj_qr():
            slabs = []
            for kt in range(2):
                sb = kb.ab.alloc(256, f"{tg}qr{grp}w{kt}")
                for j, c0 in enumerate((grp * 128, 256 + grp * 128)):
                    kb.dma(kb.wqueue(), sb.ap(0, 128, j * 128, (j + 1) * 128),
                           qr_dap[kt * 128:(kt + 1) * 128, c0:c0 + 128],
                           [qr_res], [sb.res], tag=f"{tg}qrd{grp}{kt}{j}")
                slabs.append(sb)
            bks2 = kb.take_banks(2)
            for mi in range(2):
                for c in range(NCH):
                    bk = bks2[c % 2]
                    for kt in range(2):
                        kb.mm(kb.bank_ap(bk),
                              slabs[kt].ap(0, 128, mi * 128, (mi + 1) * 128),
                              cq.ap(0, 128, kt * T + c * CW,
                                    kt * T + (c + 1) * CW),
                              kt == 0, kt == 1, [slabs[kt].res, cq.res],
                              [kb.pbank[bk]], tag=f"{tg}qrm{grp}{mi}{c}{kt}")
                    ev_qr(mi, c, bk, kb.pbank[bk])
            for sb in slabs:
                sb.free()

        up_proj_qr()
        rt = kb.af.alloc(T, f"{tg}rt{grp}")
        rb_ = kb.af.alloc(T, f"{tg}rb{grp}")
        kb.tt("vector", rt.ap(), qr_top.ap(), cosb.ap(), ALU.mult,
              [qr_top.res, cosb.res], [rt.res], tag=f"{tg}ro1{grp}")
        kb.tt("vector", rb_.ap(), qr_top.ap(), sinb.ap(), ALU.mult,
              [qr_top.res, sinb.res], [rb_.res], tag=f"{tg}ro2{grp}")
        qr_top.free()
        t1 = kb.af.alloc(T, f"{tg}t1{grp}")
        kb.tt("vector", t1.ap(), qr_bot.ap(), sinb.ap(), ALU.mult,
              [qr_bot.res, sinb.res], [t1.res], tag=f"{tg}ro3{grp}")
        kb.tt("vector", rt.ap(), rt.ap(), t1.ap(), ALU.subtract,
              [rt.res, t1.res], [rt.res], tag=f"{tg}ro4{grp}")
        kb.tt("vector", t1.ap(), qr_bot.ap(), cosb.ap(), ALU.mult,
              [qr_bot.res, cosb.res], [t1.res], tag=f"{tg}ro5{grp}")
        kb.tt("vector", rb_.ap(), rb_.ap(), t1.ap(), ALU.add,
              [rb_.res, t1.res], [rb_.res], tag=f"{tg}ro6{grp}")
        t1.free()
        qr_bot.free()
        rt_bf = kb.ab.alloc(T, f"{tg}rtb{grp}")
        rb_bf = kb.ab.alloc(T, f"{tg}rbb{grp}")
        kb.copy("vector", rt_bf.ap(), rt.ap(), [rt.res], [rt_bf.res],
                tag=f"{tg}rtc{grp}")
        kb.copy("vector", rb_bf.ap(), rb_.ap(), [rb_.res], [rb_bf.res],
                tag=f"{tg}rbc{grp}")
        rt.free()
        rb_.free()
        for h in range(8):
            kb.dma("scalar", qasm[h].bap(64, 80, 0, T),
                   rt_bf.ap(h * 16, h * 16 + 16, 0, T), [rt_bf.res],
                   [qasm[h].res], tag=f"{tg}qat{grp}{h}")
            kb.dma("scalar", qasm[h].bap(80, 96, 0, T),
                   rb_bf.ap(h * 16, h * 16 + 16, 0, T), [rb_bf.res],
                   [qasm[h].res], tag=f"{tg}qab{grp}{h}")
            kb.dma("scalar", kasm[h].bap(64, 80, 0, T),
                   krb.ap(0, 16, 0, T), [krb.res], [kasm[h].res],
                   tag=f"{tg}kat{grp}{h}")
            kb.dma("scalar", kasm[h].bap(80, 96, 0, T),
                   krb.ap(32, 48, 0, T), [krb.res], [kasm[h].res],
                   tag=f"{tg}kab{grp}{h}")
        rt_bf.free()
        rb_bf.free()

        pbuf = [kb.ab.alloc(1024, f"{tg}p{grp}{j}") for j in range(3)]
        pvs = kb.ab.alloc(CW, f"{tg}pvs{grp}")
        rrs2 = [kb.af.alloc(CW, f"{tg}rrs{grp}{j}") for j in range(2)]
        ones1b_, o1b_res_ = kb.cap("ones1b")
        pending = []

        def flush_pending():
            while pending:
                fin = pending.pop(0)
                fin()

        pvb2 = [kb.take_banks(1)[0], kb.take_banks(1)[0]]
        sbk = kb.take_banks(2)
        bcb_ded = kb.take_banks(1)[0]
        for hh in range(8):
            h = 8 * grp + hh
            qa, ka = qasm[hh], kasm[hh]
            for qc in range(NCH):
                pvb = pvb2[qc % 2]
                n_kt = 4 * qc + 4

                def do_pv_b(kt, n_kt=n_kt, pvb=pvb):
                    pb = pbuf[kt % 3]
                    kb.mm(kb.bank_ap(pvb, 0, CW, 0, 65),
                          vtok[hh].ap(0, 128, kt * 65, kt * 65 + 65),
                          pb.ap(0, 128, 0, CW), kt == 0, kt == n_kt - 1,
                          [vtok[hh].res, pb.res], [kb.pbank[pvb]],
                          tag=f"{tg}pv{h}_{qc}_{kt}")

                for kt in range(n_kt):
                    bk = sbk[kt % 2]
                    kb.mm(kb.bank_ap(bk),
                          ka.bap(0, 96, kt * 128, (kt + 1) * 128),
                          qa.bap(0, 96, qc * CW, (qc + 1) * CW), True, True,
                          [ka.res, qa.res], [kb.pbank[bk]],
                          tag=f"{tg}st{h}_{qc}_{kt}")
                    if kt == 1:
                        flush_pending()
                    pb = pbuf[kt % 3]
                    kb.act(pb.ap(0, 128, 0, CW), kb.bank_ap(bk), AF.Exp,
                           [kb.pbank[bk]], [pb.res], scale=SCL,
                           tag=f"{tg}ex{h}_{qc}_{kt}")
                    kb.ts("gpsimd", pb.ap(0, 128, 0, CW),
                          pb.ap(0, 128, 0, CW), CHI, CLO, ALU.min, ALU.max,
                          [pb.res], [pb.res], tag=f"{tg}cl{h}_{qc}_{kt}")
                    dq0 = kt * 128 - qc * CW
                    if 0 <= dq0:
                        if dq0 > 0:
                            kb.memset("gpsimd", pb.ap(0, 128, 0, dq0), 0.0,
                                      [pb.res], tag=f"{tg}z{h}_{qc}_{kt}")
                        kb.tt("gpsimd", pb.ap(0, 128, dq0, dq0 + 128),
                              pb.ap(0, 128, dq0, dq0 + 128), m01, ALU.mult,
                              [pb.res, m01_res], [pb.res],
                              tag=f"{tg}dm{h}_{qc}_{kt}")
                    if kt >= 2:
                        do_pv_b(kt - 2)
                do_pv_b(n_kt - 2)
                do_pv_b(n_kt - 1)

                rrs = rrs2[qc % 2]
                kb.act(rrs.ap(0, 1, 0, CW),
                       kb.bank_ap(pvb, 0, CW, 64, 65), AF.Ln,
                       [kb.pbank[pvb]], [rrs.res], tag=f"{tg}rl{h}_{qc}")
                kb.act(rrs.bap(0, 1, 2 * CW, 3 * CW), rrs.ap(0, 1, 0, CW),
                       AF.Exp, [rrs.res], [rrs.res], scale=-1.0,
                       tag=f"{tg}re{h}_{qc}")

                def finish(pvb=pvb, qc=qc, hh=hh, h=h, rrs=rrs):
                    bcb = bcb_ded
                    kb.mm(kb.bank_ap(bcb, 0, CW, 0, 64), ones1b_[0:1, 0:64],
                          rrs.bap(0, 1, 2 * CW, 3 * CW), True, True,
                          [o1b_res_, rrs.res], [kb.pbank[bcb]],
                          tag=f"{tg}bc{h}_{qc}")
                    kb.copy("vector", pvs.ap(0, 64, 0, CW),
                            kb.bank_ap(pvb, 0, CW, 0, 64), [kb.pbank[pvb]],
                            [pvs.res], tag=f"{tg}pe{h}_{qc}")
                    r0 = 64 * (hh % 2)
                    kb.tt("vector", att[hh // 2].ap(r0, r0 + 64, qc * CW,
                                                    (qc + 1) * CW),
                          pvs.ap(0, 64, 0, CW), kb.bank_ap(bcb, 0, CW, 0, 64),
                          ALU.mult, [pvs.res, kb.pbank[bcb]],
                          [att[hh // 2].res], tag=f"{tg}pn{h}_{qc}")
                pending.append(finish)
        flush_pending()
        pvs.free()
        for b in rrs2:
            b.free()
        for b in pbuf:
            b.free()
        for b in qasm + kasm + vtok:
            b.free()



def build_full(weights_shapes_only, stage=99):
    consts = host_constants()
    kb = KB(weights_shapes_only, consts)
    nc = kb.nc
    ft = nc.dram_tensor("ftab", [4, 128, T], BF16, kind="Internal")
    kb.dram["ftab"] = (ft.ap(), Res("ftab"))
    bt = nc.dram_tensor("btab", [2, 128, T], F32, kind="Internal")
    kb.dram["btab"] = (bt.ap(), Res("btab"))
    m = Model(kb)
    kb.make_bf16_twins(["a_inproj", "a_gate", "a_outproj", "a_coupling_bd",
                        "a_w1", "a_w2", "f_wqkv", "f_wo", "f_w1", "f_w2",
                        "b_dq", "b_dkv", "b_kr", "b_uq", "b_uk", "b_uv",
                        "b_qr", "b_wo"], weights_shapes_only)

    xout_ap, xout_res = kb.dram["x_out"]
    spill_ap, spill_res = kb.dram["x_spill"]

    def store_out(bufs):
        for kt in range(NKT):
            kb.dma("sync", xout_ap[kt * 128:(kt + 1) * 128, :], bufs[kt].ap(),
                   [bufs[kt].res], [xout_res], tag=f"out{kt}")

    xls = [m.load_x_tile("x_in", kt, f"txa{kt}") for kt in range(NKT)]
    negm, rr = m.stats_rows(lambda kt: (xls[kt], False), "tln0", f32=True)
    negm_b = m.bcast_row(negm, "tl0m")
    r_b = m.bcast_row(rr, "tl0r")
    negm.free()
    rr.free()
    xa = []
    for kt in range(NKT):
        b = kb.af.alloc(T, f"xa{kt}")
        kb.tt("vector", b.ap(), xls[kt].ap(), negm_b.ap(), ALU.add,
              [xls[kt].res, negm_b.res], [b.res], tag=f"xa_a{kt}")
        kb.tt("vector", b.ap(), b.ap(), r_b.ap(), ALU.mult,
              [b.res, r_b.res], [b.res], tag=f"xa_m{kt}")
        xls[kt].free()
        xa.append(b)
    negm_b.free()
    r_b.free()
    if stage == 0:
        store_out(xa)
        return kb, m
    m.stage = stage
    xa_bf = [m.cast_tile(xa[kt], f"xabf{kt}") for kt in range(NKT)]
    xa = m.attn_a(xa, xa_bf)
    if m.early_out is not None:
        for i, (b, p0, p1) in enumerate(m.early_out[:NKT]):
            if b.arena is kb.ab:
                f = kb.af.alloc(b.w, f"eo{i}")
                kb.copy("vector", f.ap(p0, p1, 0, b.w), b.ap(p0, p1, 0, b.w),
                        [b.res], [f.res], tag=f"eo{i}")
                kb.dma("sync", xout_ap[i * 128:(i + 1) * 128, 0:min(f.w, T)],
                       f.ap(0, 128, 0, min(f.w, T)), [f.res], [xout_res],
                       tag=f"eout{i}")
                f.free()
            else:
                kb.dma("sync", xout_ap[i * 128:(i + 1) * 128, 0:min(b.w, T)],
                       b.ap(0, 128, 0, min(b.w, T)), [b.res], [xout_res],
                       tag=f"eout{i}")
        return kb, m

    x1_bf = []
    for w0 in range(0, NKT, 4):
        xw = [m.load_x_tile("x_in", kt, f"x1l{kt}")
              for kt in range(w0, w0 + 4)]
        for j, kt in enumerate(range(w0, w0 + 4)):
            xl = xw[j]
            kb.tt("vector", xa[kt].ap(), xl.ap(), xa[kt].ap(), ALU.add,
                  [xl.res, xa[kt].res], [xa[kt].res], tag=f"x1a{kt}")
            xl.free()
            kb.dma("sync", spill_ap[kt * 128:(kt + 1) * 128, :], xa[kt].ap(),
                   [xa[kt].res], [spill_res], tag=f"sp1_{kt}")
            x1_bf.append(m.cast_tile(xa[kt], f"x1b{kt}",
                                     eng="scalar" if kt % 2 else "vector"))
    if stage == 1:
        store_out(xa)
        return kb, m

    m.build_ffn_tables()
    m.build_attnb_tables()

    negm, rr = m.stats_rows(lambda kt: (x1_bf[kt], False), "tln1")
    negm_b = m.bcast_row(negm, "tl1m")
    r_b = m.bcast_row(rr, "tl1r")
    negm.free()
    rr.free()
    for b in x1_bf:
        b.free()
    y = []
    for kt in range(NKT):
        b = kb.af.alloc(T, f"y{kt}")
        kb.tt("vector", b.ap(), xa[kt].ap(), negm_b.ap(), ALU.add,
              [xa[kt].res, negm_b.res], [b.res], tag=f"y_a{kt}")
        kb.tt("vector", b.ap(), b.ap(), r_b.ap(), ALU.mult,
              [b.res, r_b.res], [b.res], tag=f"y_m{kt}")
        xa[kt].free()
        y.append(b)
    negm_b.free()
    r_b.free()
    m.ffn_layer(y, 0, 0)
    if m.early_out is not None:
        for i, (b, p0, p1) in enumerate(m.early_out[:NKT]):
            if b.arena is kb.ab:
                f = kb.af.alloc(b.w, f"feo{i}")
                kb.copy("vector", f.ap(p0, p1, 0, b.w), b.ap(p0, p1, 0, b.w),
                        [b.res], [f.res], tag=f"feo{i}")
                kb.dma("sync", xout_ap[i * 128:(i + 1) * 128, 0:min(f.w, T)],
                       f.ap(0, 128, 0, min(f.w, T)), [f.res], [xout_res],
                       tag=f"feout{i}")
                f.free()
            else:
                kb.dma("sync", xout_ap[i * 128:(i + 1) * 128, 0:min(b.w, T)],
                       b.ap(0, 128, 0, min(b.w, T)), [b.res], [xout_res],
                       tag=f"feout{i}")
        return kb, m
    m.ffn_layer(y, 1, 0)

    x2_bf = []
    for w0 in range(0, NKT, 4):
        xw = [m.load_x_tile("x_spill", kt, f"x2l{kt}")
              for kt in range(w0, w0 + 4)]
        for j, kt in enumerate(range(w0, w0 + 4)):
            xl = xw[j]
            kb.tt("vector", y[kt].ap(), xl.ap(), y[kt].ap(), ALU.add,
                  [xl.res, y[kt].res], [y[kt].res], tag=f"x2a{kt}")
            xl.free()
            kb.dma("sync", spill_ap[kt * 128:(kt + 1) * 128, :], y[kt].ap(),
                   [y[kt].res], [spill_res], tag=f"sp2_{kt}")
            x2_bf.append(m.cast_tile(y[kt], f"x2b{kt}",
                                         eng="scalar" if kt % 2 else "vector"))
    if stage == 2:
        store_out(y)
        return kb, m
    for b in y:
        b.free()

    x3 = [None] * NKT
    m.attn_b(x2_bf, x3)
    x3_bf = []
    for kt in range(NKT):
        kb.dma("sync", spill_ap[kt * 128:(kt + 1) * 128, :], x3[kt].ap(),
               [x3[kt].res], [spill_res], tag=f"sp3_{kt}")
        x3_bf.append(m.cast_tile(x3[kt], f"x3b{kt}",
                                     eng="scalar" if kt % 2 else "vector"))
    if stage == 3:
        store_out(x3)
        return kb, m

    negm, rr = m.stats_rows(lambda kt: (x3_bf[kt], False), "tln3")
    negm_b = m.bcast_row(negm, "tl3m")
    r_b = m.bcast_row(rr, "tl3r")
    negm.free()
    rr.free()
    for b in x3_bf:
        b.free()
    y2 = []
    for kt in range(NKT):
        b = kb.af.alloc(T, f"y2_{kt}")
        kb.tt("vector", b.ap(), x3[kt].ap(), negm_b.ap(), ALU.add,
              [x3[kt].res, negm_b.res], [b.res], tag=f"y2a{kt}")
        kb.tt("vector", b.ap(), b.ap(), r_b.ap(), ALU.mult,
              [b.res, r_b.res], [b.res], tag=f"y2m{kt}")
        x3[kt].free()
        y2.append(b)
    negm_b.free()
    r_b.free()
    m.ffn_layer(y2, 0, 1)
    m.ffn_layer(y2, 1, 1)

    for w0 in range(0, NKT, 4):
        xw = [m.load_x_tile("x_spill", kt, f"fo{kt}")
              for kt in range(w0, w0 + 4)]
        for j, kt in enumerate(range(w0, w0 + 4)):
            xl = xw[j]
            kb.tt("vector", y2[kt].ap(), xl.ap(), y2[kt].ap(), ALU.add,
                  [xl.res, y2[kt].res], [y2[kt].res], tag=f"fo_a{kt}")
            xl.free()
    store_out(y2)
    return kb, m


_CACHE = {}


def _alloc_sems(kb):
    sems = {}
    st = kb.st
    for e in COMPUTE_ENGS:
        sems[("c", e)] = st.enter_context(kb.nc.semaphore("sc_" + e))
        if kb.s.dseq[e]:
            for i in range(min(kb.s.dseq[e], DMA_POOL)):
                sems[("q", e, i)] = st.enter_context(
                    kb.nc.semaphore(f"sq_{e}{i}"))
    return sems


def _get_program(stage):
    if stage not in _CACHE:
        dummy_shapes = None
        raise RuntimeError("use _build_with_weights")
    return _CACHE[stage]


def kernel(**inputs):
    stage = DEV_STAGE
    w = prepare_weights(inputs)
    key = ("prog", stage)
    if key not in _CACHE:
        kb, m = build_full(w, stage)
        sems = _alloc_sems(kb)
        with kb.nc.Block() as block:
            kb.s.emit(kb.nc, block, sems)
        _CACHE[key] = kb.nc
    nc = _CACHE[key]

    x = np.asarray(inputs["L0_attn_a_x"], dtype=np.float32)
    in_maps = []
    for c in range(NCORES):
        im = {"x_in": np.ascontiguousarray(x[c].T)}
        for name, arr in w.items():
            im["w_" + name] = arr
        in_maps.append(im)
    res = run_bass_kernel_spmd(nc, in_maps, core_ids=list(range(NCORES)))
    out = np.stack([np.asarray(r["x_out"]).T for r in res.results])
    return np.ascontiguousarray(out.astype(np.float32))
